# version 01
# baseline (speedup 1.0000x reference)
import numpy as np
import ml_dtypes
import concourse.bass as bass
import concourse.mybir as mybir
from concourse.ap import AP
from concourse.bass_utils import run_bass_kernel_spmd
from contextlib import ExitStack

F32 = mybir.dt.float32
BF16 = mybir.dt.bfloat16
AF = mybir.ActivationFunctionType
ALU = mybir.AluOpType
AX = mybir.AxisListType.X

NCORES = 8
T = 2048
D = 4096
NT = T // 128
EPS = 1e-5
BF = ml_dtypes.bfloat16


class Sem:
    def __init__(self, h):
        self.h = h
        self.n = 0

    def tok(self):
        return (self.h, self.n)


class Bld:
    def __init__(self):
        self.nc = bass.Bass("TRN2", target_bir_lowering=False, debug=False)
        self.es = ExitStack()
        self._ns = 0

    def sem(self, name=None):
        self._ns += 1
        return Sem(self.es.enter_context(self.nc.semaphore(name or f"sem{self._ns}")))

    def sb(self, name, shape, dt):
        return self.es.enter_context(self.nc.sbuf_tensor(name, shape, dt))

    def scope(self):
        return ExitStack()

    def sbs(self, sc, name, shape, dt):
        return sc.enter_context(self.nc.sbuf_tensor(name, shape, dt))

    def pss(self, sc, name, shape, dt):
        return sc.enter_context(self.nc.psum_tensor(name, shape, dt))

    def ps(self, name, shape, dt):
        return self.es.enter_context(self.nc.psum_tensor(name, shape, dt))

    def dram(self, name, shape, dt, kind="Internal"):
        return self.nc.dram_tensor(name, shape, dt, kind=kind).ap()

    def inp(self, name, shape, dt=F32):
        return self.dram(name, shape, dt, kind="ExternalInput")

    def outp(self, name, shape, dt=F32):
        return self.dram(name, shape, dt, kind="ExternalOutput")


class Prog:
    ENGS = ("tensor", "vector", "scalar", "gpsimd", "sync")

    def __init__(self, bld):
        self.b = bld
        self.q = {e: [] for e in self.ENGS}
        self.serial = False
        self.serial_all = False
        self.serial_phases = ""
        self.all_sems = None

    def add(self, eng, fn, waits=(), inc=None, k=1):
        waits = tuple(w for w in waits if w is not None)
        if self.serial and self.all_sems:
            waits = waits + tuple(s.tok() for s in self.all_sems.values() if s.n > 0)
        tok = None
        semh = None
        if inc is not None:
            inc.n += k
            tok = (inc.h, inc.n)
            semh = inc.h
        self.q[eng].append((waits, fn, semh, k))
        return tok

    def finish(self):
        nc = self.b.nc
        with nc.Block() as block:
            for e in self.ENGS:
                ops = self.q[e]
                if not ops:
                    continue

                def mk(ops):
                    def run(eng):
                        for waits, fn, semh, k in ops:
                            for h, v in waits:
                                eng.wait_ge(h, v)
                            inst = fn(eng)
                            if semh is not None:
                                inst.then_inc(semh, k)
                    return run
                getattr(block, e)(mk(ops))


def emit_transpose_tile(p, sems, xrow, ident, x3T_ap_fn, banks, bank_free, q,
                        wait_x, nchunks=32, dt_src=F32):
    toks = []
    ngr = nchunks // 4
    for g in range(ngr):
        bank = banks[g % len(banks)]

        def tr(t, bank=bank, g=g, xrow=xrow, ident=ident):
            for j in range(4):
                c = g * 4 + j
                inst = t.transpose(bank[:, j * 128:(j + 1) * 128],
                                   xrow[:, c * 128:(c + 1) * 128], ident[:, :])
            return inst
        tt = p.add("tensor", tr, waits=(wait_x, bank_free.get(id(bank))), inc=sems["T"])

        def ev(v, bank=bank, g=g, q=q):
            return v.tensor_copy(x3T_ap_fn(g * 4, q),
                                 AP(bank, 0, [[512, 128], [128, 4], [1, 128]]))
        vt = p.add("vector", ev, waits=(tt,), inc=sems["V"])
        bank_free[id(bank)] = vt
        toks.append(vt)
    return toks


def emit_ln(p, sems, acc, sq, mcol, col0, tile_waits, out_dma=None, out_sem=None,
            extra=None):
    c = col0

    def red(v, acc=acc, c=c):
        return v.reduce_sum(mcol[:, c:c + 1], acc[:], axis=AX)
    t1 = p.add("vector", red, waits=tile_waits, inc=sems["V"])

    def mean(v, c=c):
        return v.tensor_scalar(mcol[:, c + 1:c + 2], mcol[:, c:c + 1], 1.0 / D, None,
                               ALU.mult)
    t1b = p.add("vector", mean, waits=(t1,), inc=sems["V"])

    def sub(v, acc=acc, c=c):
        return v.tensor_scalar(acc[:], acc[:], mcol[:, c + 1:c + 2], None, ALU.subtract)
    t2 = p.add("vector", sub, waits=(t1b,), inc=sems["V"])

    def sqr(s, acc=acc, c=c):
        return s.activation(sq[:], acc[:], AF.Square, accum_out=mcol[:, c + 2:c + 3])
    t3 = p.add("scalar", sqr, waits=(t2,), inc=sems["A"])

    def veps(v, c=c):
        return v.tensor_scalar(mcol[:, c + 3:c + 4], mcol[:, c + 2:c + 3],
                               1.0 / D, EPS, ALU.mult, ALU.add)
    t3b = p.add("vector", veps, waits=(t3,), inc=sems["V"])

    def rstd(s, c=c):
        s.activation(mcol[:, c + 3:c + 4], mcol[:, c + 3:c + 4], AF.Ln)
        return s.activation(mcol[:, c + 3:c + 4], mcol[:, c + 3:c + 4], AF.Exp,
                            scale=-0.5)
    t4 = p.add("scalar", rstd, waits=(t3b,), inc=sems["A"])

    def nrm(v, acc=acc, c=c):
        return v.tensor_scalar(acc[:], acc[:], mcol[:, c + 3:c + 4], None, ALU.mult)
    t5 = p.add("vector", nrm, waits=(t4,), inc=sems["V"])
    return t5


def emit_l1ffn(p, b, sems, x_in, V_in, UT_in, S_in, ident_sb, ident_tok, out_ap,
               start_tok=None, dbg_ap=None):
    nc = b.nc
    xvT = b.sb("f1_xvT", [128, 8 * T], BF16)
    Scol = b.sb("f1_Scol", [128, 8], F32)
    pb = [b.ps(f"f1_p{i}", [128, 512], F32) for i in range(6)]
    scA = b.scope()
    xrow = [b.sbs(scA, f"f1_xrow{i}", [128, D], F32) for i in range(2)]
    x3T = b.sbs(scA, "f1_x3T", [128, 32 * 512], BF16)
    Vb = b.sbs(scA, "f1_Vb", [128, 32 * 1024], BF16)
    dW, dX, dO = sems["dW"], sems["dX"], sems["dO"]
    sT, sV = sems["T"], sems["V"]
    def ld_s(g):
        return g.dma_start(out=Scol[:], in_=S_in)
    tok_S = p.add("gpsimd", ld_s, waits=(start_tok,), inc=dW, k=16)
    v_toks = []
    for i in range(4):
        def ld_v(g, i=i):
            return g.dma_start(
                out=Vb.ap().rearrange("p (k r) -> p k r", k=32)[:, i * 8:(i + 1) * 8, :],
                in_=V_in.rearrange("(k p) r -> p k r", p=128)[:, i * 8:(i + 1) * 8, :])
        v_toks.append(p.add("gpsimd", ld_v, inc=dW, k=16))
    tok_V = v_toks[-1]

    bank_free = {}
    xrow_free = {}
    xv_done = []
    for st in range(4):
        ev_toks_st = []
        for q in range(4):
            bi = (st * 4 + q) % 2

            def ldx(s, st=st, q=q, bi=bi):
                return s.dma_start(out=xrow[bi][:],
                                   in_=x_in[(st * 512 + q * 128):(st * 512 + q * 128 + 128), :])
            xtok = p.add("sync", ldx, waits=(xrow_free.get(bi), start_tok), inc=dX, k=16)

            toks = emit_transpose_tile(
                p, sems, xrow[bi], ident_sb,
                lambda c0, q, st=st: AP(x3T, c0 * 512 + q * 128,
                                        [[32 * 512, 128], [512, 4], [1, 128]]),
                pb[0:4], bank_free, q, wait_x=xtok,
            )
            xrow_free[bi] = sems["T"].tok()
            ev_toks_st.extend(toks)
        x3T_ready = ev_toks_st[-1]
        for rc in range(8):
            bank = pb[4 + rc % 2]

            def mmxv(t, bank=bank, rc=rc):
                for k in range(32):
                    inst = t.matmul(bank[:],
                                    Vb[:, k * 1024 + rc * 128: k * 1024 + rc * 128 + 128],
                                    x3T[:, k * 512:(k + 1) * 512],
                                    start=(k == 0), stop=(k == 31))
                return inst
            mt = p.add("tensor", mmxv,
                       waits=(x3T_ready, tok_V, bank_free.get(id(bank))), inc=sT)

            def evxv(v, bank=bank, rc=rc, st=st):
                return v.tensor_scalar(xvT[:, rc * T + st * 512: rc * T + st * 512 + 512],
                                       bank.ap(), Scol[:, rc:rc + 1], None, ALU.mult,
                                       accum_out=None)
            vt = p.add("vector", evxv, waits=(mt, tok_S), inc=sV)
            bank_free[id(bank)] = vt
            xv_done.append(vt)
    xv_all = xv_done[-1]
    if dbg_ap is not None:
        def dmp_sto(s):
            return s.dma_start(out=dbg_ap, in_=xvT[:])
        p.add("sync", dmp_sto, waits=(xv_all,), inc=sems["dO"], k=16)
    last_mm_A = sems["T"].tok()
    scA.close()

    scB = b.scope()
    UTb = b.sbs(scB, "f1_UTb", [128, 8 * D], BF16)
    rx = [b.sbs(scB, f"f1_rx{i}", [128, D], F32) for i in range(2)]
    acc = [b.sbs(scB, f"f1_acc{i}", [128, D], F32) for i in range(2)]
    sq = b.sbs(scB, "f1_sq", [128, D], BF16)
    mcol = b.sbs(scB, "f1_mcol", [128, 8], F32)

    ut_toks = []
    for i in range(4):
        def ld_u(g, i=i):
            return g.dma_start(
                out=UTb.ap().rearrange("p (rc o) -> p rc o", rc=8)[:, i * 2:(i + 1) * 2, :],
                in_=UT_in.rearrange("(rc p) o -> p rc o", p=128)[:, i * 2:(i + 1) * 2, :])
        ut_toks.append(p.add("gpsimd", ld_u, waits=(xv_all, last_mm_A), inc=dW, k=16))
    tok_UT = ut_toks[-1]

    rx_free = {}
    acc_free = {}
    last_store = None
    for tt in range(NT):
        bi = tt % 2

        def ldr(s, tt=tt, bi=bi):
            return s.dma_start(out=rx[bi][:], in_=x_in[tt * 128:(tt + 1) * 128, :])
        gate = (xv_all, last_mm_A) if tt < 2 else (rx_free.get(bi),)
        rtok = p.add("sync", ldr, waits=gate, inc=dX, k=16)

        ev8 = []
        for ot in range(8):
            bank = pb[ot % 4]

            def mmo(t, tt=tt, ot=ot, bank=bank):
                for rc in range(8):
                    inst = t.matmul(bank[:],
                                    xvT[:, rc * T + tt * 128: rc * T + tt * 128 + 128],
                                    UTb[:, rc * D + ot * 512: rc * D + ot * 512 + 512],
                                    start=(rc == 0), stop=(rc == 7))
                return inst
            mt = p.add("tensor", mmo,
                       waits=(xv_all, tok_UT, bank_free.get(id(bank))), inc=sT)

            def evo(v, tt=tt, ot=ot, bank=bank, bi=bi):
                return v.tensor_tensor(acc[bi][:, ot * 512:(ot + 1) * 512], bank.ap(),
                                       rx[bi][:, ot * 512:(ot + 1) * 512], ALU.add)
            vt = p.add("vector", evo, waits=(mt, rtok, acc_free.get(bi)), inc=sV)
            bank_free[id(bank)] = vt
            ev8.append(vt)
        rx_free[bi] = ev8[-1]

        lt = emit_ln(p, sems, acc[bi], sq, mcol, (bi % 2) * 4, tuple(ev8))

        def sto(s, tt=tt, bi=bi):
            return s.dma_start(out=out_ap[tt * 128:(tt + 1) * 128, :], in_=acc[bi][:])
        last_store = p.add("sync", sto, waits=(lt,), inc=dO, k=16)
        acc_free[bi] = last_store
    scB.close()
    return last_store


def build_l1ffn(debug=False, serial=False):
    b = Bld()
    p = Prog(b)
    p.serial = serial
    x_in = b.inp("x", [T, D])
    V_in = b.inp("V", [D, 1024])
    UT_in = b.inp("UT", [1024, D])
    S_in = b.inp("S_r", [128, 8])
    id_in = b.inp("ident", [128, 128])
    out_ap = b.outp("out", [T, D])

    sems = {k: b.sem(k) for k in ("dW", "dX", "dO", "T", "V", "A", "P")}
    p.all_sems = sems
    ident_sb = b.sb("ident_sb", [128, 128], F32)

    def ldi(s):
        return s.dma_start(out=ident_sb[:], in_=id_in)
    itok = p.add("sync", ldi, inc=sems["dX"], k=16)

    dbg_ap = b.outp("dbg_xvT", [128, 8 * T], BF16) if debug else None
    last = emit_l1ffn(p, b, sems, x_in, V_in, UT_in, S_in, ident_sb, itok, out_ap,
                      start_tok=itok, dbg_ap=dbg_ap)

    def fin(s, last=last):
        return s.nop()
    p.add("sync", fin, waits=(last,))
    p.finish()
    b.es.close()
    return b.nc


SCALE = float(1.0 / np.sqrt(128.0))


def emit_xpose_to_hbm(p, b, sems, x_in, xT_hbm, ident_sb, pbanks, start_tok,
                      pfx, ntok=T):
    scX = b.scope()
    xrow = [b.sbs(scX, f"{pfx}_xr{i}", [128, D], F32) for i in range(2)]
    xsup = b.sbs(scX, f"{pfx}_xsup", [128, 32 * 512], BF16)
    dX, dO = sems["dX"], sems["dO"]
    bank_free = {}
    xrow_free = {}
    out_tok = None
    for st in range(ntok // 512):
        ev_toks = []
        for q in range(4):
            bi = (st * 4 + q) % 2

            def ldx(s, st=st, q=q, bi=bi):
                return s.dma_start(out=xrow[bi][:],
                                   in_=x_in[(st * 512 + q * 128):(st * 512 + q * 128 + 128), :])
            xtok = p.add("sync", ldx, waits=(xrow_free.get(bi), start_tok), inc=dX, k=16)
            toks = emit_transpose_tile(
                p, sems, xrow[bi], ident_sb,
                lambda c0, q: AP(xsup, c0 * 512 + q * 128,
                                 [[32 * 512, 128], [512, 4], [1, 128]]),
                pbanks, bank_free, q, wait_x=(xtok if out_tok is None else xtok))
            xrow_free[bi] = sems["T"].tok()
            ev_toks.extend(toks)

        def sto(s, st=st):
            return s.dma_start(
                out=AP(xT_hbm.tensor, st * 512, [[32 * ntok, 128], [ntok, 32], [1, 512]]),
                in_=AP(xsup, 0, [[32 * 512, 128], [512, 32], [1, 512]]))
        out_tok = p.add("sync", sto, waits=(ev_toks[-1],), inc=dO, k=16)
        bank_free = {k: out_tok for k in bank_free}
    scX.close()
    return out_tok


def emit_l1attn(p, b, sems, x_in, Wq_in, Wk_in, Wv_in, Wo_in, cos_in, sinm_in,
                zcol_in, pad_in, dmask_in, prot_in, ident_sb, out_ap,
                replica_groups, start_tok=None, dbg=None, bf_weights=False):
    nc = b.nc
    dW, dX, dO = sems["dW"], sems["dX"], sems["dO"]
    sT, sV, sA = sems["T"], sems["V"], sems["A"]

    x2T_h = b.dram("a_x2T", [128, 32 * T], BF16)
    Q_h = b.dram("a_Q", [128, 32 * T], BF16)
    kvc = b.dram("a_kvc", [128, 32768], BF16)
    kvp = b.dram("a_kvp", [128, 32768], BF16)
    ctx_h = b.dram("a_ctx", [128, 32 * T], BF16)
    ao_h = b.dram("a_ao", [T, D], F32)

    onescol_b = b.sb("a_onescol", [128, 1], BF16)
    onesrow_b = b.sb("a_onesrow", [1, 512], BF16)
    ones1f = b.sb("a_ones1f", [1, 128], F32)
    Ib = b.sb("a_Ib", [128, 128], BF16)
    padb = b.sb("a_padb", [1, 4096], BF16)
    protb = b.sb("a_protb", [128, 128], BF16)
    dmaskb = b.sb("a_dmaskb", [128, 2048], BF16)
    zcol = b.sb("a_zcol", [128, 1], F32)
    cosb = b.sb("a_cos", [128, T], F32)
    sinmb = b.sb("a_sinm", [128, T], F32)
    negB = b.sb("a_negB", [1, 128], BF16)
    qn2m = b.sb("a_qn2m", [1, 128], F32)
    km2 = b.sb("a_km2", [1, 16], F32)
    skr = b.sb("a_skr", [1, 16], F32)
    echo_sb = b.sb("a_echo", [128, 16], BF16)
    pb = [b.ps(f"a_p{i}", [128, 512], F32) for i in range(8)]

    def ld_c1(g):
        g.memset(onescol_b[:], 1.0)
        g.memset(onesrow_b[:], 1.0)
        g.memset(ones1f[:], 1.0)
        g.tensor_copy(Ib[:], ident_sb[:, :])
        return g.dma_start(out=padb[:], in_=pad_in)

    p.add("gpsimd", ld_c1, waits=(start_tok,), inc=dW, k=16)

    def ld_c2(g):
        return g.dma_start(out=dmaskb[:], in_=dmask_in)
    p.add("gpsimd", ld_c2, inc=dW, k=16)

    def ld_c3(g):
        return g.dma_start(out=protb[:], in_=prot_in)
    ctok = p.add("gpsimd", ld_c3, inc=dW, k=16)

    def ld_z(s):
        return s.dma_start(out=zcol[:], in_=zcol_in)
    p.add("sync", ld_z, waits=(start_tok,), inc=dX, k=16)

    def ld_cos(s):
        return s.dma_start(out=cosb[:], in_=cos_in)
    p.add("sync", ld_cos, inc=dX, k=16)

    def ld_sin(s):
        return s.dma_start(out=sinmb[:], in_=sinm_in)
    ctok2 = p.add("sync", ld_sin, inc=dX, k=16)

    p.serial = "A" in getattr(p, 'serial_phases', '') or p.serial_all
    xT_done = emit_xpose_to_hbm(p, b, sems, x_in, x2T_h, ident_sb, pb[0:4],
                                start_tok, "a")

    p.serial = "B" in getattr(p, 'serial_phases', '') or p.serial_all
    kvo = b.dram("a_kvo", [128, 32768], BF16)
    scB = b.scope()
    Wkb = b.sbs(scB, "a_Wkb", [128, 32 * 1024], BF16)
    Wvb = b.sbs(scB, "a_Wvb", [128, 32 * 1024], BF16)
    xtile = [b.sbs(scB, f"a_xt{i}", [128, 512], BF16) for i in range(4)]
    rstage = [b.sbs(scB, f"a_rs{i}", [128, 512], F32) for i in range(2)]
    praw = b.sbs(scB, "a_praw", [128, 512], BF16)
    cstage = [b.sbs(scB, f"a_cs{i}", [128, 512], BF16) for i in range(2)]
    kst = [b.sbs(scB, f"a_kst{i}", [128, 512], BF16) for i in range(2)]

    wq_eng = "sync" if bf_weights else "gpsimd"
    wq_sem = dX if bf_weights else dW

    def echo(buf_tile, tok, eng="gpsimd", sem=None):
        def e(g, buf_tile=buf_tile):
            return g.dma_start(out=echo_sb[:], in_=buf_tile)
        return p.add(eng, e, waits=(tok,), inc=sem or dW, k=16)

    def ld_wk(g):
        return g.dma_start(
            out=Wkb.ap().rearrange("p (k o) -> p k o", k=32),
            in_=Wk_in.rearrange("(k p) o -> p k o", p=128))
    wk_tok = p.add(wq_eng, ld_wk, waits=(xT_done,), inc=wq_sem, k=16)
    wk_tok = echo(Wkb[:, 32 * 1024 - 16:], wk_tok, wq_eng, wq_sem)

    def ld_wv(g):
        return g.dma_start(
            out=Wvb.ap().rearrange("p (k o) -> p k o", k=32),
            in_=Wv_in.rearrange("(k p) o -> p k o", p=128))
    wv_tok = p.add(wq_eng, ld_wv, inc=wq_sem, k=16)
    wv_tok = echo(Wvb[:, 32 * 1024 - 16:], wv_tok, wq_eng, wq_sem)

    xt_free = {}
    reader = {}

    def stream_ktiles(nt, consume):
        base_serial = p.serial
        for k in range(32):
            bi = k % 4

            def ldt(s, k=k, nt=nt, bi=bi):
                return s.dma_start(
                    out=xtile[bi][:],
                    in_=AP(x2T_h.tensor, k * T + nt * 512, [[32 * T, 128], [1, 512]]))
            p.serial = base_serial or ("S" in p.serial_phases)
            tok = p.add("sync", ldt, waits=(xt_free.get(bi), xT_done), inc=dX, k=16)
            p.serial = base_serial or ("N" in p.serial_phases)
            consume(k, xtile[bi], tok)
            p.serial = base_serial
            xt_free[bi] = sems["T"].tok()

    def rope_evac(bank, nt, out_slice_ap_fn, extra_wait=()):
        if not isinstance(extra_wait, tuple):
            extra_wait = (extra_wait,)

        def rev1(v, bank=bank, nt=nt):
            cs = cosb[:, nt * 512:(nt + 1) * 512]
            v.tensor_tensor(rstage[0][:], bank.ap(), cs, ALU.mult)
            v.tensor_tensor(rstage[1][0:64, :],
                            AP(bank, 64 * 512, [[512, 64], [1, 512]]),
                            sinmb[0:64, nt * 512:(nt + 1) * 512], ALU.mult)
            return v.tensor_tensor(rstage[1][64:128, :],
                                   AP(bank, 0, [[512, 64], [1, 512]]),
                                   sinmb[64:128, nt * 512:(nt + 1) * 512], ALU.mult)
        t1 = p.add("vector", rev1, waits=extra_wait, inc=sV)

        def rev2(v):
            return v.tensor_tensor(out_slice_ap_fn(), rstage[0][:], rstage[1][:],
                                   ALU.add)
        return p.add("vector", rev2, waits=(t1,), inc=sV)

    p.serial = True
    kphase_gate = None
    kv_contrib_toks = []
    for nt in range(4):
        prev_ev = kphase_gate

        def consume_k(k, buf, tok, prev_ev=prev_ev):
            def mm(t, k=k, buf=buf):
                for koc in range(8):
                    inst = t.matmul(pb[koc][:],
                                    Wkb[:, k * 1024 + koc * 128: k * 1024 + koc * 128 + 128],
                                    buf[:], start=(k == 0), stop=(k == 31))
                return inst
            waits = [tok, wk_tok, ctok2]
            if k == 0 and prev_ev is not None:
                waits.append(prev_ev)
            p.add("tensor", mm, waits=tuple(waits), inc=sT)
        stream_ktiles(nt, consume_k)
        mm_tok = sems["T"].tok()
        for koc in range(8):
            ki = koc % 2
            dst = lambda ki=ki: kst[ki][:, :]
            et = rope_evac(pb[koc], nt, dst,
                           extra_wait=(mm_tok, reader.get(id(kst[ki]))))

            def down(s, koc=koc, nt=nt, ki=ki):
                return s.dma_start(
                    out=AP(kvo.tensor, koc * 2048 + nt * 512, [[32768, 128], [1, 512]]),
                    in_=kst[ki][:])
            dot = p.add("sync", down, waits=(et,), inc=dO, k=16)

            def contrib(v, ki=ki):
                return v.tensor_scalar(cstage[ki][:], kst[ki][:], zcol[:],
                                       None, ALU.mult)
            ct = p.add("vector", contrib,
                       waits=(et, reader.get(id(cstage[ki]))), inc=sV)

            def dcontrib(s, koc=koc, nt=nt, ki=ki):
                return s.dma_start(
                    out=AP(kvc.tensor, koc * 2048 + nt * 512, [[32768, 128], [1, 512]]),
                    in_=cstage[ki][:])
            dct = p.add("sync", dcontrib, waits=(ct,), inc=dO, k=16)
            reader[id(cstage[ki])] = dct
            reader[id(kst[ki])] = dct
            kv_contrib_toks.append(dct)
        kphase_gate = sems["V"].tok()

    p.serial = True
    vphase_gate = kphase_gate
    for nt in range(4):
        prev_ev = vphase_gate

        def consume_v(k, buf, tok, prev_ev=prev_ev):
            def mm(t, k=k, buf=buf):
                for tbq in range(4):
                    for ov in range(2):
                        inst = t.matmul(pb[tbq * 2 + ov][:],
                                        buf[:, tbq * 128:(tbq + 1) * 128],
                                        Wvb[:, k * 1024 + ov * 512: k * 1024 + ov * 512 + 512],
                                        start=(k == 0), stop=(k == 31))
                return inst
            waits = [tok, wv_tok]
            if k == 0:
                waits.append(prev_ev)
            p.add("tensor", mm, waits=tuple(waits), inc=sT)
        stream_ktiles(nt, consume_v)
        mm_tok = sems["T"].tok()
        for tbq in range(4):
            for ov in range(2):
                vb = tbq * 2 + ov
                kb = 16 + nt * 4 + tbq
                ki = vb % 2
                voff = 16384 + (kb - 16) * 1024 + ov * 512

                def vev(v, vb=vb, ki=ki):
                    return v.tensor_copy(kst[ki][:], pb[vb].ap())
                et = p.add("vector", vev,
                           waits=(mm_tok, reader.get(id(kst[ki]))), inc=sV)

                def vdo(s, voff=voff, ki=ki):
                    return s.dma_start(
                        out=AP(kvo.tensor, voff, [[32768, 128], [1, 512]]),
                        in_=kst[ki][:])
                dot = p.add("sync", vdo, waits=(et,), inc=dO, k=16)

                def vcontrib(v, ki=ki):
                    return v.tensor_scalar(cstage[ki][:], kst[ki][:],
                                           zcol[:], None, ALU.mult)
                ct = p.add("vector", vcontrib,
                           waits=(et, reader.get(id(cstage[ki]))), inc=sV)

                def vdc(s, voff=voff, ki=ki):
                    return s.dma_start(
                        out=AP(kvc.tensor, voff, [[32768, 128], [1, 512]]),
                        in_=cstage[ki][:])
                dt_ = p.add("sync", vdc, waits=(ct,), inc=dO, k=16)
                reader[id(cstage[ki])] = dt_
                reader[id(kst[ki])] = dt_
                kv_contrib_toks.append(dt_)
        vphase_gate = sems["V"].tok()

    scB.close()

    p.serial = True
    scB2 = b.scope()
    Wqs = b.sbs(scB2, "a_Wqs", [128, 32 * 512], BF16)
    qsq = b.sbs(scB2, "a_qsq", [128, 512], BF16)
    wq_prev = (vphase_gate, kv_contrib_toks[-1], sems["T"].tok())
    qphase_gate = vphase_gate
    prev_qmt = None
    prev_qxt = None
    prev_sqt = None
    for oset in range(8):
        def ld_wq(g, oset=oset):
            return g.dma_start(
                out=Wqs.ap().rearrange("p (k o) -> p k o", k=32),
                in_=Wq_in.rearrange("(k p) o -> p k o", p=128)[:, :, oset * 512:(oset + 1) * 512])
        wq_tok = p.add(wq_eng, ld_wq, waits=wq_prev, inc=wq_sem, k=16)
        wq_tok = echo(Wqs[:, 32 * 512 - 16:], wq_tok, wq_eng, wq_sem)
        for nt in range(4):
            prev_ev = qphase_gate

            def consume_q(k, buf, tok, wq_tok=wq_tok, prev_ev=prev_ev):
                def mm(t, k=k, buf=buf):
                    for j in range(4):
                        inst = t.matmul(pb[j][:],
                                        Wqs[:, k * 512 + j * 128: k * 512 + j * 128 + 128],
                                        buf[:], start=(k == 0), stop=(k == 31))
                    return inst
                waits = [tok, wq_tok]
                if k == 0:
                    waits.append(prev_ev)
                p.add("tensor", mm, waits=tuple(waits), inc=sT)
            stream_ktiles(nt, consume_q)
            mm_tok = sems["T"].tok()
            for j in range(4):
                h = oset * 4 + j
                cdst = lambda: cstage[0][:, :]
                et = rope_evac(pb[j], nt, cdst,
                               extra_wait=(mm_tok, reader.get(id(cstage[0])), prev_sqt))

                def qsto(s, h=h, nt=nt):
                    return s.dma_start(
                        out=AP(Q_h.tensor, h * T + nt * 512, [[32 * T, 128], [1, 512]]),
                        in_=cstage[0][:])
                qst = p.add("sync", qsto, waits=(et,), inc=dO, k=16)
                reader[id(cstage[0])] = qst

                def sqq(s_, j=j):
                    return s_.activation(qsq[:], cstage[0][:], AF.Square)
                sqt = p.add("scalar", sqq, waits=(et, prev_qmt), inc=sA)
                prev_sqt = sqt

                def qnmm(t, j=j):
                    return t.matmul(pb[6][0:1, :], onescol_b[:], qsq[:],
                                    start=True, stop=True)
                qmt = p.add("tensor", qnmm, waits=(sqt, prev_qxt), inc=sT)
                prev_qmt = qmt

                def qnmax(v, h=h, nt=nt):
                    return v.reduce_max(qn2m[0:1, h * 4 + nt: h * 4 + nt + 1],
                                        pb[6][0:1, :], axis=AX)
                qxt = p.add("vector", qnmax, waits=(qmt,), inc=sV)
                prev_qxt = qxt
            qphase_gate = sems["V"].tok()
        wq_prev = (sems["T"].tok(),)
    def coll(g):
        return g.collective_compute("AllReduce", ALU.add, replica_groups,
                                    [kvc.opt()], [kvp.opt()])
    coll_tok = p.add("gpsimd", coll, waits=(kv_contrib_toks[-1],), inc=sems["P"], k=1)
    qn_all = sems["V"].tok()
    q_all_t = sems["T"].tok()
    q_all_d = sems["dO"].tok()
    if dbg is not None:
        def dq(s):
            return s.dma_start(out=dbg["Q"], in_=Q_h)
        p.add("sync", dq, waits=(q_all_d,), inc=dO, k=16)
        def dko(s):
            return s.dma_start(out=dbg["kvo"], in_=kvo)
        p.add("sync", dko, waits=(q_all_d,), inc=dO, k=16)
        def dkp(s):
            return s.dma_start(out=dbg["kvp"], in_=kvp)
        p.add("sync", dkp, waits=(q_all_d, coll_tok), inc=dO, k=16)
    scB2.close()

    p.serial = "D" in getattr(p, 'serial_phases', '') or p.serial_all
    scD = b.scope()
    Kbuf = b.sbs(scD, "a_Kbuf", [128, 8 * 4096], BF16)
    Vbuf = b.sbs(scD, "a_Vbuf", [128, 32 * 1024], BF16)
    qt_b = [b.sbs(scD, f"a_qt{i}", [128, 512], BF16) for i in range(2)]
    prb = [b.sbs(scD, f"a_pr{i}", [128, 512], BF16) for i in range(2)]
    ksq = b.sbs(scD, "a_ksq", [128, 512], BF16)
    kn2r = b.sbs(scD, "a_kn2r", [1, 4096], F32)
    recr = b.sbs(scD, "a_recr", [1, 512], F32)
    ctxf = b.sbs(scD, "a_ctxf", [128, 512], F32)
    ctxs = [b.sbs(scD, f"a_ctxs{i}", [128, 512], BF16) for i in range(2)]
    qn2sq = b.sbs(scD, "a_qn2sq", [1, 128], F32)
    negBf = b.sbs(scD, "a_negBf", [1, 128], F32)

    dload_gate = (coll_tok, q_all_t, q_all_d, qn_all)

    def ld_pk(s):
        return s.dma_start(
            out=AP(Kbuf, 0, [[32768, 128], [4096, 8], [1, 2048]]),
            in_=AP(kvp.tensor, 0, [[32768, 128], [2048, 8], [1, 2048]]))
    ptok1 = p.add("sync", ld_pk, waits=dload_gate, inc=dX, k=16)

    def ld_ok(s):
        return s.dma_start(
            out=AP(Kbuf, 2048, [[32768, 128], [4096, 8], [1, 2048]]),
            in_=AP(kvo.tensor, 0, [[32768, 128], [2048, 8], [1, 2048]]))
    ptok2 = p.add("sync", ld_ok, waits=(ptok1,), inc=dX, k=16)

    def ld_pv(s):
        return s.dma_start(out=Vbuf[:, 0:16384],
                           in_=AP(kvp.tensor, 16384, [[32768, 128], [1, 16384]]))
    ptok3 = p.add("sync", ld_pv, waits=(ptok2,), inc=dX, k=16)

    def ld_ov(s):
        return s.dma_start(out=Vbuf[:, 16384:32768],
                           in_=AP(kvo.tensor, 16384, [[32768, 128], [1, 16384]]))
    ptok = p.add("sync", ld_ov, waits=(ptok3,), inc=dX, k=16)

    km_toks = []
    prev_knm = None
    prev_knev = None
    for g in range(8):
        evs = []
        for i in range(8):
            def sqk(s_, g=g, i=i):
                return s_.activation(ksq[:], Kbuf[:, g * 4096 + i * 512: g * 4096 + i * 512 + 512],
                                     AF.Square)
            st_ = p.add("scalar", sqk, waits=(ptok, prev_knm), inc=sA)

            def knm(t, g=g, i=i):
                return t.matmul(pb[7][0:1, :], onescol_b[:], ksq[:], start=True, stop=True)
            prev_knm = p.add("tensor", knm, waits=(st_, q_all_t, prev_knev), inc=sT)

            def knev(v, g=g, i=i):
                return v.tensor_copy(kn2r[0:1, i * 512:(i + 1) * 512], pb[7][0:1, :])
            prev_knev = p.add("vector", knev, waits=(prev_knm,), inc=sV)
            evs.append(prev_knev)

        def kmx(v, g=g):
            return v.reduce_max(km2[0:1, g:g + 1], kn2r[0:1, :], axis=AX)
        km_toks.append(p.add("vector", kmx, waits=(evs[-1],), inc=sV))

    def sk_sqrt(s_):
        return s_.activation(skr[0:1, 0:8], km2[0:1, 0:8], AF.Sqrt)
    skt = p.add("scalar", sk_sqrt, waits=(km_toks[-1],), inc=sA)

    def nb1(s_):
        return s_.activation(qn2sq[0:1, :], qn2m[0:1, :], AF.Sqrt)
    nbt1 = p.add("scalar", nb1, waits=(qn_all, skt), inc=sA)

    def nb2(v):
        return v.tensor_tensor(negBf[0:1, :], qn2sq[0:1, :],
                               AP(skr, 0, [[16, 1], [1, 8], [0, 16]]), ALU.mult)
    nbt2 = p.add("vector", nb2, waits=(nbt1,), inc=sV)

    def nb3(v):
        return v.tensor_scalar(negB[0:1, :], negBf[0:1, :], -SCALE * 1.01, None,
                               ALU.mult)
    nb_tok = p.add("vector", nb3, waits=(nbt2,), inc=sV)

    qt_free = {}
    probs_free = [None, None]
    sc_free = [None, None]
    sums_ctx_free = None
    last_ctx_dma = None
    d_start_gate = (nb_tok, skt, ptok, q_all_d, ctok)
    for h in range(32):
        g = h // 4
        for qt in range(4):
            kbs = 20 + 4 * qt
            bi_q = (h * 4 + qt) % 2

            def ldq(s, h=h, qt=qt, bi=bi_q):
                return s.dma_start(
                    out=qt_b[bi][:],
                    in_=AP(Q_h.tensor, h * T + qt * 512, [[32 * T, 128], [1, 512]]))
            qtok = p.add("sync", ldq, waits=(qt_free.get(bi_q), q_all_d), inc=dX, k=16)

            sums_av_toks = []
            for kb in range(kbs):
                sb_i = kb % 2
                bank = pb[sb_i]
                v = kb - 16 - 4 * qt

                def mmsc(t, h=h, qt=qt, kb=kb, g=g, v=v, bank=bank, bi_q=bi_q):
                    steps = ["kq", "shift"]
                    if kb < 16:
                        steps.append("pad")
                    if v >= 0:
                        steps.append("diag")
                    inst = None
                    for i, stp in enumerate(steps):
                        st_f, sp_f = (i == 0), (i == len(steps) - 1)
                        if stp == "kq":
                            inst = t.matmul(bank[:],
                                            Kbuf[:, g * 4096 + kb * 128: g * 4096 + kb * 128 + 128],
                                            qt_b[bi_q][:], start=st_f, stop=sp_f)
                        elif stp == "shift":
                            inst = t.matmul(bank[:], onesrow_b[0:1, 0:128],
                                            AP(negB, h * 4 + qt, [[128, 1], [0, 512]]),
                                            start=st_f, stop=sp_f)
                        elif stp == "pad":
                            inst = t.matmul(bank[:], AP(padb, kb * 128, [[4096, 1], [1, 128]]),
                                            onesrow_b[0:1, :], start=st_f, stop=sp_f)
                        else:
                            inst = t.matmul(bank[:], Ib[:, :],
                                            dmaskb[:, v * 512:(v + 1) * 512],
                                            start=st_f, stop=sp_f)
                    return inst
                mwaits = [qtok, sc_free[sb_i]]
                if kb == 0:
                    mwaits += list(d_start_gate)
                mt = p.add("tensor", mmsc, waits=tuple(mwaits), inc=sT)

                def expk(s_, sb_i=sb_i):
                    return s_.activation(prb[sb_i][:], pb[sb_i].ap(), AF.Exp, scale=SCALE)
                et = p.add("scalar", expk, waits=(mt, probs_free[sb_i]), inc=sA)
                sc_free[sb_i] = et

                def mmsums(t, sb_i=sb_i, kb=kb, g=g, kbs=kbs):
                    t.matmul(pb[2][0:1, :], onescol_b[:], prb[sb_i][:],
                             start=(kb == 0), stop=(kb == kbs - 1))
                    return t.matmul(pb[3][:], Vbuf[:, kb * 1024 + g * 128: kb * 1024 + g * 128 + 128],
                                    prb[sb_i][:], start=(kb == 0), stop=(kb == kbs - 1))
                savw = [et]
                if kb == 0:
                    savw.append(sums_ctx_free)
                savt = p.add("tensor", mmsums, waits=tuple(savw), inc=sT)
                probs_free[sb_i] = savt
                sums_av_toks.append(savt)
            qt_free[bi_q] = sums_av_toks[-1]

            def recip(ve, h=h):
                return ve.reciprocal(recr[0:1, :], pb[2][0:1, :])
            rct = p.add("vector", recip, waits=(sums_av_toks[-1],), inc=sV)

            def mmrb(t):
                return t.matmul(pb[4][:], ones1f[0:1, 0:128], recr[0:1, :],
                                start=True, stop=True)
            rbt = p.add("tensor", mmrb, waits=(rct,), inc=sT)

            def ctxev(s_):
                return s_.activation(ctxf[:], pb[3].ap(), AF.Copy)
            cet = p.add("scalar", ctxev, waits=(sums_av_toks[-1], last_ctx_dma), inc=sA)

            bi_c = (h * 4 + qt) % 2

            def ctxnrm(ve, bi_c=bi_c):
                return ve.tensor_tensor(ctxs[bi_c][:], ctxf[:], pb[4].ap(), ALU.mult)
            cnt = p.add("vector", ctxnrm,
                        waits=(rbt, cet, qt_free.get(('c', bi_c))), inc=sV)
            sums_ctx_free = cnt

            def ctxdma(s, h=h, qt=qt, bi_c=bi_c):
                return s.dma_start(
                    out=AP(ctx_h.tensor, h * T + qt * 512, [[32 * T, 128], [1, 512]]),
                    in_=ctxs[bi_c][:])
            last_ctx_dma = p.add("sync", ctxdma, waits=(cnt,), inc=dO, k=16)
            qt_free[('c', bi_c)] = last_ctx_dma
    if dbg is not None:
        def dcx(s):
            return s.dma_start(out=dbg["ctx"], in_=ctx_h)
        p.add("sync", dcx, waits=(last_ctx_dma,), inc=dO, k=16)
    d_done_t = sems["T"].tok()
    d_done_d = sems["dO"].tok()
    d_done_v = sems["V"].tok()
    scD.close()

    p.serial = "E" in getattr(p, 'serial_phases', '') or p.serial_all
    scE = b.scope()
    Wob = b.sbs(scE, "a_Wob", [128, 32 * 2048], BF16)
    ctt = [b.sbs(scE, f"a_ctt{i}", [128, 32 * 128], BF16) for i in range(2)]
    aost = [b.sbs(scE, f"a_aost{i}", [128, 2048], F32) for i in range(2)]
    e_gate = (d_done_t, d_done_d, d_done_v)
    ao_toks = []
    ctt_free = {}
    aost_free = {}
    wo_prev = None
    ebank_free = {}
    for oh in range(2):
        wo_toks = []
        for i in range(4):
            def ld_wo(g_, oh=oh, i=i):
                return g_.dma_start(
                    out=Wob.ap().rearrange("p (k o) -> p k o", k=32)[:, i * 8:(i + 1) * 8, :],
                    in_=Wo_in.rearrange("(k p) o -> p k o", p=128)[:, i * 8:(i + 1) * 8,
                                                                  oh * 2048:(oh + 1) * 2048])
            wo_toks.append(p.add("gpsimd", ld_wo,
                                 waits=e_gate + ((wo_prev,) if wo_prev else ()),
                                 inc=dW, k=16))
        wo_tok = wo_toks[-1]
        for tt in range(NT):
            bi = tt % 2

            def ldc(s, tt=tt, bi=bi):
                return s.dma_start(
                    out=ctt[bi][:],
                    in_=AP(ctx_h.tensor, tt * 128, [[32 * T, 128], [T, 32], [1, 128]]))
            ct = p.add("sync", ldc, waits=(ctt_free.get(bi),) + e_gate, inc=dX, k=16)

            ev_toks = []
            for otq in range(4):
                bank = pb[otq]

                def mmwo(t, tt=tt, otq=otq, bank=bank, bi=bi):
                    for hh in range(32):
                        inst = t.matmul(bank[:], ctt[bi][:, hh * 128:(hh + 1) * 128],
                                        Wob[:, hh * 2048 + otq * 512: hh * 2048 + otq * 512 + 512],
                                        start=(hh == 0), stop=(hh == 31))
                    return inst
                ttg = oh * NT + tt
                mt = p.add("tensor", mmwo, waits=(ct, wo_tok, ebank_free.get((ttg, otq))),
                           inc=sT)

                def aoev(s_, otq=otq, bank=bank, bi=bi):
                    return s_.activation(aost[bi][:, otq * 512:(otq + 1) * 512], bank.ap(),
                                         AF.Copy)
                evt = p.add("scalar", aoev, waits=(mt, aost_free.get(bi)), inc=sA)
                ebank_free[(ttg + 2, otq)] = evt
                ev_toks.append(evt)
            ctt_free[bi] = ev_toks[-1]

            def aodma(s, tt=tt, oh=oh, bi=bi):
                return s.dma_start(
                    out=AP(ao_h.tensor, tt * 128 * D + oh * 2048, [[D, 128], [1, 2048]]),
                    in_=aost[bi][:])
            adt = p.add("sync", aodma, waits=(ev_toks[-1],), inc=dO, k=16)
            aost_free[bi] = adt
            ao_toks.append(adt)
        wo_prev = sems["T"].tok()
    e_done = ao_toks[-1]
    scE.close()

    p.serial = "F" in getattr(p, 'serial_phases', '') or p.serial_all
    scF = b.scope()
    aot = [b.sbs(scF, f"a_aot{i}", [128, D], F32) for i in range(2)]
    rxf = [b.sbs(scF, f"a_rxf{i}", [128, D], F32) for i in range(2)]
    accf = [b.sbs(scF, f"a_accf{i}", [128, D], F32) for i in range(2)]
    sqf = b.sbs(scF, "a_sqf", [128, D], BF16)
    mcolf = b.sbs(scF, "a_mcolf", [128, 8], F32)
    last_store = None
    aot_free = {}
    accf_free = {}
    for tt in range(NT):
        bi = tt % 2

        def ldao(s, tt=tt, bi=bi):
            return s.dma_start(out=aot[bi][:], in_=ao_h[tt * 128:(tt + 1) * 128, :])
        at = p.add("sync", ldao, waits=(aot_free.get(bi), e_done), inc=dX, k=16)

        def ldrx(s, tt=tt, bi=bi):
            return s.dma_start(out=rxf[bi][:], in_=x_in[tt * 128:(tt + 1) * 128, :])
        rt = p.add("sync", ldrx, waits=(aot_free.get(('r', bi)), e_done), inc=dX, k=16)

        def addr(v, bi=bi):
            return v.tensor_tensor(accf[bi][:], aot[bi][:], rxf[bi][:], ALU.add)
        adt = p.add("vector", addr, waits=(at, rt, accf_free.get(bi)), inc=sV)
        aot_free[bi] = adt
        aot_free[('r', bi)] = adt

        lt = emit_ln(p, sems, accf[bi], sqf, mcolf, bi * 4, (adt,))

        def sto(s, tt=tt, bi=bi):
            return s.dma_start(out=out_ap[tt * 128:(tt + 1) * 128, :], in_=accf[bi][:])
        last_store = p.add("sync", sto, waits=(lt,), inc=dO, k=16)
        accf_free[bi] = last_store
    scF.close()
    return last_store


def build_l1attn(replica_groups, serial=False, serial_phases="", debug=False,
                 bf_weights=False):
    b = Bld()
    p = Prog(b)
    p.serial = serial
    p.serial_all = serial
    p.serial_phases = serial_phases
    x_in = b.inp("x", [T, D])
    wdt = BF16 if bf_weights else F32
    Wq_in = b.inp("Wq", [D, D], wdt)
    Wk_in = b.inp("Wk", [D, 1024], wdt)
    Wv_in = b.inp("Wv", [D, 1024], wdt)
    Wo_in = b.inp("Wo", [D, D])
    cos_in = b.inp("cos", [128, T])
    sinm_in = b.inp("sinm", [128, T])
    zcol_in = b.inp("zcol", [128, 1])
    pad_in = b.inp("pad", [1, 4096])
    dmask_in = b.inp("dmask", [128, 2048])
    prot_in = b.inp("prot", [128, 128], BF16)
    id_in = b.inp("ident", [128, 128])
    out_ap = b.outp("out", [T, D])

    sems = {k: b.sem(k) for k in ("dW", "dX", "dO", "T", "V", "A", "P")}
    p.all_sems = sems
    ident_sb = b.sb("ident_sb", [128, 128], F32)

    def ldi(s):
        return s.dma_start(out=ident_sb[:], in_=id_in)
    itok = p.add("sync", ldi, inc=sems["dX"], k=16)

    dbg = None
    if debug:
        dbg = {k: b.outp("dbg_" + k, [128, 32 * T], BF16) for k in ("Q", "ctx")}
        dbg["kvo"] = b.outp("dbg_kvo", [128, 32768], BF16)
        dbg["kvp"] = b.outp("dbg_kvp", [128, 32768], BF16)
    last = emit_l1attn(p, b, sems, x_in, Wq_in, Wk_in, Wv_in, Wo_in, cos_in,
                       sinm_in, zcol_in, pad_in, dmask_in, prot_in, ident_sb,
                       out_ap, replica_groups, start_tok=itok, dbg=dbg,
                       bf_weights=bf_weights)

    def fin(s, last=last):
        return s.nop()
    p.add("sync", fin, waits=(last,))
    p.finish()
    b.es.close()
    return b.nc


def l1attn_host_inputs(hc):
    pos = np.arange(hc * T, (hc + 1) * T, dtype=np.float64)
    invf = 1.0 / (10000.0 ** (np.arange(0, 128, 2, dtype=np.float64) / 128.0))
    ang = pos[None, :] * invf[:, None]
    cos = np.concatenate([np.cos(ang), np.cos(ang)], 0).astype(np.float32)
    sin = np.sin(ang)
    sinm = np.concatenate([-sin, sin], 0).astype(np.float32)
    zcol = np.full((128, 1), 1.0 if hc == 0 else 0.0, np.float32)
    pad = np.zeros((1, 4096), np.float32)
    if hc == 0:
        pad[0, :2048] = -1e30
    dmask = np.zeros((128, 2048), np.float32)
    for v in range(4):
        kl = np.arange(128)[:, None]
        qf = np.arange(512)[None, :]
        dmask[:, v * 512:(v + 1) * 512] = np.where(v * 128 + kl > qf, -1e30, 0.0)
    ident = np.eye(128, dtype=np.float32)
    prot = np.zeros((128, 128), np.float32)
    for j in range(128):
        prot[(j + 64) % 128, j] = -1.0 if j < 64 else 1.0
    return dict(cos=cos, sinm=sinm, zcol=zcol, pad=pad,
                dmask=dmask.astype(np.float32), ident=ident,
                prot=prot.astype(ml_dtypes.bfloat16))


_TRIPLES = [(1, 2, 3), (1, 4, 5), (1, 7, 6), (2, 4, 6), (2, 5, 7), (3, 4, 7), (3, 6, 5)]


def _oct_struct():
    Smat = np.zeros((8, 8, 8), np.float32)
    Smat[0, 0, 0] = 1.0
    for i in range(1, 8):
        Smat[0, i, i] = 1.0
        Smat[i, 0, i] = 1.0
        Smat[i, i, 0] = -1.0
    for a2, b2, c2 in _TRIPLES:
        for xx, yy, zz in ((a2, b2, c2), (b2, c2, a2), (c2, a2, b2)):
            Smat[xx, yy, zz] = 1.0
            Smat[yy, xx, zz] = -1.0
    return Smat


S_OCT = _oct_struct()
OCT_I = np.zeros((8, 8), np.int64)
OCT_SGN = np.zeros((8, 8), np.float32)
for _j in range(8):
    for _k in range(8):
        _i = int(np.nonzero(S_OCT[_j, :, _k])[0][0])
        OCT_I[_j, _k] = _i
        OCT_SGN[_j, _k] = S_OCT[_j, _i, _k]

HID8 = 1366
HCH = [128] * 10 + [106]
HCH = [128] * 10 + [86]
NHC = 11


def emit_quantize(p, b, sems, W_in, nj_rows, ncols, M_dram, pfx, start_tok):
    dW, dX, dO = sems["dW"], sems["dX"], sems["dO"]
    sT, sV, sA = sems["T"], sems["V"], sems["A"]
    old_serial = p.serial
    p.serial = True
    sc = b.scope()
    wt = [b.sbs(sc, f"{pfx}_wt{i}", [128, ncols], F32) for i in range(2)]
    stage = b.sbs(sc, f"{pfx}_st", [128, 8 * ncols], BF16)
    qv = b.sbs(sc, f"{pfx}_qv", [128, ncols], F32)
    q2 = b.sbs(sc, f"{pfx}_q2", [128, ncols], F32)
    cols = b.sbs(sc, f"{pfx}_cols", [128, 8], F32)
    onesc = b.sbs(sc, f"{pfx}_ones", [128, 1], F32)
    onesr = b.sbs(sc, f"{pfx}_onesr", [1, 128], F32)
    sc1 = b.sbs(sc, f"{pfx}_sc1", [1, 4], F32)
    pbq = b.pss(sc, f"{pfx}_pq", [128, 512], F32)

    if nj_rows == 512:
        rts = [(r * 128, 128) for r in range(4)]
    else:
        rts = []
        off = 0
        for sz in HCH:
            rts.append((off, sz))
            off += sz
    tiles = [(j, off, sz) for j in range(8) for (off, sz) in rts]

    def mset(g):
        g.memset(onesc[:], 1.0)
        g.memset(onesr[:], 1.0)
        return g.memset(cols[:], 0.0)
    t0 = p.add("gpsimd", mset, waits=(start_tok,), inc=sems["P"])

    for idx, (j, off, sz) in enumerate(tiles):
        bi = idx % 2
        if sz < 128:
            def zt(v, bi=bi):
                return v.memset(wt[bi][:, :], 0.0)
            p.add("vector", zt, waits=(), inc=sV)

        def ld(s, j=j, off=off, sz=sz, bi=bi):
            return s.dma_start(out=wt[bi][0:sz, :],
                               in_=W_in[j, off:off + sz, :])
        lt = p.add("sync", ld, waits=(t0,), inc=dX, k=16)

        def red(v, bi=bi, sz=sz):
            return v.reduce_sum(cols[:, 1:2], wt[bi][:, :], axis=AX,
                                apply_absolute_value=True)
        rt_ = p.add("vector", red, waits=(lt,), inc=sV)

        def acc(v):
            return v.tensor_tensor(cols[:, 0:1], cols[:, 0:1], cols[:, 1:2], ALU.add)
        p.add("vector", acc, waits=(rt_,), inc=sV)

    def tot(t):
        return t.matmul(pbq[0:1, 0:1], cols[:, 0:1], onesc[:], start=True, stop=True)
    tt_ = p.add("tensor", tot, waits=(), inc=sT)

    n_el = 8 * nj_rows * ncols

    def mkscale1(v, n_el=n_el):
        return v.tensor_scalar(sc1[0:1, 0:1], pbq[0:1, 0:1], 1.0 / n_el, None,
                               ALU.mult)
    sta = p.add("vector", mkscale1, waits=(tt_,), inc=sV)

    def mkscale2(v):
        return v.tensor_scalar(sc1[0:1, 1:2], sc1[0:1, 0:1], 0.5, None, ALU.mult)
    st_ = p.add("vector", mkscale2, waits=(sta,), inc=sV)

    def bc(t):
        t.matmul(pbq[:, 0:1], onesr[0:1, :], sc1[0:1, 0:1], start=True, stop=True)
        return t.matmul(pbq[:, 1:2], onesr[0:1, :], sc1[0:1, 1:2], start=True,
                        stop=True)
    bt = p.add("tensor", bc, waits=(st_,), inc=sT)

    def cpcols1(v):
        v.tensor_copy(cols[:, 2:3], pbq[:, 0:1])
        return v.tensor_copy(cols[:, 3:4], pbq[:, 1:2])
    cta = p.add("vector", cpcols1, waits=(bt,), inc=sV)

    def cpcols2(v):
        return v.tensor_scalar(cols[:, 4:5], pbq[:, 1:2], -1.0, None, ALU.mult)
    ct = p.add("vector", cpcols2, waits=(cta,), inc=sV)

    last = None
    for idx, (j, off, sz) in enumerate(tiles):
        bi = idx % 2

        def ld(s, j=j, off=off, sz=sz, bi=bi):
            return s.dma_start(out=wt[bi][0:sz, :],
                               in_=W_in[j, off:off + sz, :])
        lt = p.add("sync", ld, waits=(ct,), inc=dX, k=16)

        def quant1(v, bi=bi):
            v.tensor_scalar(qv[:], wt[bi][:], cols[:, 3:4], None, ALU.is_gt)
            return v.tensor_scalar(q2[:], wt[bi][:], cols[:, 4:5], None, ALU.is_lt)
        qta = p.add("vector", quant1, waits=(lt,), inc=sV)

        def quant2(v):
            v.tensor_tensor(qv[:], qv[:], q2[:], ALU.subtract)
            return v.tensor_scalar(qv[:], qv[:], cols[:, 2:3], None, ALU.mult)
        qt = p.add("vector", quant2, waits=(qta,), inc=sV)

        def mkstage(v, j=j):
            inst = None
            for k in range(8):
                sgn = float(OCT_SGN[j, k])
                inst = v.tensor_scalar(stage[:, k * ncols:(k + 1) * ncols], qv[:],
                                       sgn, None, ALU.mult)
            return inst
        mt = p.add("vector", mkstage, waits=(qt,), inc=sV)

        def sto(s, j=j, off=off, sz=sz):
            return s.dma_start(
                out=AP(M_dram.tensor, (j * nj_rows + off) * ncols,
                       [[ncols, sz], [8 * nj_rows * ncols, 8], [1, ncols]]),
                in_=AP(stage, 0, [[8 * ncols, sz], [ncols, 8], [1, ncols]]))
        last = p.add("sync", sto, waits=(mt,), inc=dO, k=16)
    sc.close()
    p.serial = old_serial
    return last


def emit_l0ffn(p, b, sems, x_in, wgT_in, wuT_in, wdT_in, ident_sb, out_ap,
               start_tok=None, dbg=None):
    nc = b.nc
    dW, dX, dO = sems["dW"], sems["dX"], sems["dO"]
    sT, sV, sA = sems["T"], sems["V"], sems["A"]

    Mg = b.dram("f0_Mg", [8 * 4096, HID8], BF16).tensor.ap() if False else b.dram("f0_Mg", [8 * 4096 * HID8], BF16)
    Mu = b.dram("f0_Mu", [8 * 4096 * HID8], BF16)
    Md = b.dram("f0_Md", [8 * 8 * HID8 * 512], BF16)
    g_h = b.dram("f0_g", [11264, T], BF16)
    h_h = b.dram("f0_h", [11264, T], BF16)
    dff_h = b.dram("f0_dff", [T, D], F32)

    qg = emit_quantize(p, b, sems, wgT_in, 512, HID8, Mg, "qg", start_tok)
    qu = emit_quantize(p, b, sems, wuT_in, 512, HID8, Mu, "qu", qg)
    qd = emit_quantize(p, b, sems, wdT_in, HID8, 512, Md, "qd", qu)

    ln_toks = []
    dff_done = []
    half_gate = None
    for hf in range(2):
        TH = 1024
        scX = b.scope()
        x1T = b.sbs(scX, f"f0_x1T{hf}", [128, 32 * TH], BF16)
        xrow = [b.sbs(scX, f"f0_xr{hf}_{i}", [128, D], F32) for i in range(2)]
        Mkb = b.sbs(scX, f"f0_Mkb{hf}", [128, 32 * HID8], BF16)
        gst = [b.sbs(scX, f"f0_gst{hf}_{i}", [128, 512], BF16) for i in range(2)]
        sg = b.sbs(scX, f"f0_sg{hf}", [128, 512], F32)
        pb = [b.pss(scX, f"f0_p{hf}_{i}", [128, 512], F32) for i in range(8)]

        bank_free = {}
        xrow_free = {}
        ev_last = None
        for st in range(2):
            for q in range(4):
                bi = (st * 4 + q) % 2
                r0 = hf * TH + st * 512 + q * 128

                def ldx(s, r0=r0, bi=bi):
                    return s.dma_start(out=xrow[bi][:], in_=x_in[r0:r0 + 128, :])
                xtok = p.add("sync", ldx,
                             waits=(xrow_free.get(bi), qd, start_tok, half_gate),
                             inc=dX, k=16)
                toks = emit_transpose_tile(
                    p, sems, xrow[bi], ident_sb,
                    lambda c0, q, st=st: AP(x1T, c0 * TH + st * 512 + q * 128,
                                            [[32 * TH, 128], [TH, 4], [1, 128]]),
                    pb[0:4], bank_free, q, wait_x=xtok)
                xrow_free[bi] = sems["T"].tok()
                ev_last = toks[-1]
        xT_ready = ev_last

        mk_prev = None
        for phase, M_dram, qtok in (("g", Mg, qg), ("u", Mu, qu)):
            for k in range(8):
                def ldm(g2, k=k, M_dram=M_dram):
                    return g2.dma_start(
                        out=AP(Mkb, 0, [[32 * HID8, 128], [HID8, 32], [1, HID8]]),
                        in_=AP(M_dram.tensor, k * 4096 * HID8,
                               [[HID8, 128], [128 * HID8, 32], [1, HID8]]))
                mtok = p.add("gpsimd", ldm,
                             waits=(qtok, xT_ready, mk_prev), inc=dW, k=16)

                ocs = [(i * 128, sz) for i, sz in enumerate(HCH)]
                for oi, (ooff, osz) in enumerate(ocs):
                    for nt in range(2):
                        bank = pb[(oi * 2 + nt) % 4]

                        def mm(t, k=k, ooff=ooff, osz=osz, nt=nt, bank=bank):
                            for c in range(32):
                                j, dc = c // 4, c % 4
                                srcc = int(OCT_I[j, k]) * 4 + dc
                                inst = t.matmul(
                                    bank[0:osz, :],
                                    Mkb[:, (j * 4 + dc) * HID8 + ooff:
                                        (j * 4 + dc) * HID8 + ooff + osz],
                                    x1T[:, srcc * TH + nt * 512: srcc * TH + nt * 512 + 512],
                                    start=(c == 0), stop=(c == 31))
                            return inst
                        mmt = p.add("tensor", mm,
                                    waits=(mtok, bank_free.get(id(bank))), inc=sT)

                        row0 = (k * NHC + oi) * 128
                        if phase == "g":
                            def evg(v, bank=bank, osz=osz, bi=(oi * 2 + nt) % 2):
                                return v.tensor_copy(gst[bi][0:osz, :], bank[0:osz, :])
                            et = p.add("vector", evg,
                                       waits=(mmt, bank_free.get(('g', (oi * 2 + nt) % 2))),
                                       inc=sV)

                            def stg(s, row0=row0, osz=osz, nt=nt, hf=hf,
                                    bi=(oi * 2 + nt) % 2):
                                return s.dma_start(
                                    out=AP(g_h.tensor, row0 * T + hf * TH + nt * 512,
                                           [[T, osz], [1, 512]]),
                                    in_=gst[bi][0:osz, :])
                            dt_ = p.add("sync", stg, waits=(et,), inc=dO, k=16)
                            bank_free[id(bank)] = et
                            bank_free[('g', (oi * 2 + nt) % 2)] = dt_
                        else:
                            bi = (oi * 2 + nt) % 2

                            def ldg(s, row0=row0, osz=osz, nt=nt, hf=hf, bi=bi):
                                return s.dma_start(
                                    out=gst[bi][0:osz, :],
                                    in_=AP(g_h.tensor, row0 * T + hf * TH + nt * 512,
                                           [[T, osz], [1, 512]]))
                            gtok = p.add("sync", ldg,
                                         waits=(bank_free.get(('u', bi)),), inc=dX, k=16)

                            def sil(s_, osz=osz, bi=bi):
                                return s_.activation(sg[0:osz, :], gst[bi][0:osz, :],
                                                     AF.Silu)
                            stok = p.add("scalar", sil, waits=(gtok,), inc=sA)

                            def evh(v, bank=bank, osz=osz, bi=bi):
                                return v.tensor_tensor(gst[bi][0:osz, :], sg[0:osz, :],
                                                       bank[0:osz, :], ALU.mult)
                            et = p.add("vector", evh, waits=(mmt, stok), inc=sV)

                            def sth(s, row0=row0, osz=osz, nt=nt, hf=hf, bi=bi):
                                return s.dma_start(
                                    out=AP(h_h.tensor, row0 * T + hf * TH + nt * 512,
                                           [[T, osz], [1, 512]]),
                                    in_=gst[bi][0:osz, :])
                            dt_ = p.add("sync", sth, waits=(et,), inc=dO, k=16)
                            bank_free[id(bank)] = et
                            bank_free[('u', bi)] = dt_
                mk_prev = sems["T"].tok()
            if phase == "g":
                g_done = sems["dO"].tok()
            else:
                h_done = sems["dO"].tok()
        scX.close()

        scD2 = b.scope()
        hb = b.sbs(scD2, f"f0_hb{hf}", [128, 88 * TH], BF16)
        mdt = [b.sbs(scD2, f"f0_mdt{hf}_{i}", [128, 512], BF16) for i in range(2)]
        dst_ = [b.sbs(scD2, f"f0_dst{hf}_{i}", [128, 512], F32) for i in range(2)]
        pb2 = [b.pss(scD2, f"f0_p2{hf}_{i}", [128, 512], F32) for i in range(8)]

        hrows = []
        for j in range(8):
            off = 0
            for sz in HCH:
                hrows.append((j, off, sz))
                off += sz

        def ldh(s, hf=hf):
            return s.dma_start(
                out=AP(hb, 0, [[88 * TH, 128], [TH, 88], [1, TH]]),
                in_=AP(h_h.tensor, hf * TH, [[T, 128], [128 * T, 88], [1, TH]]))
        htok = p.add("sync", ldh, waits=(h_done,), inc=dX, k=16)

        down_done = []
        md_free = {}
        bank2_free = {}
        for k in range(8):
            mm_toks = []
            for ci, (j, off, sz) in enumerate(hrows):
                bi = ci % 2
                mrow0 = (k * 8 + j) * HID8 + off

                def ldmd(s, mrow0=mrow0, sz=sz, bi=bi):
                    return s.dma_start(
                        out=mdt[bi][0:sz, :],
                        in_=AP(Md.tensor, mrow0 * 512, [[512, sz], [1, 512]]))
                mt_ = p.add("sync", ldmd, waits=(md_free.get(bi), htok), inc=dX, k=16)

                hcg = int(OCT_I[j, k]) * NHC + (off // 128)

                def mm(t, hcg=hcg, sz=sz, bi=bi, ci=ci):
                    for tc in range(8):
                        inst = t.matmul(
                            pb2[tc][:],
                            hb[0:sz, hcg * TH + tc * 128: hcg * TH + tc * 128 + 128],
                            mdt[bi][0:sz, :],
                            start=(ci == 0), stop=(ci == 87))
                    return inst
                ws = [mt_]
                if ci == 0:
                    ws += [bank2_free.get(tc) for tc in range(8)]
                mmt = p.add("tensor", mm, waits=tuple(ws), inc=sT)
                md_free[bi] = mmt
                mm_toks.append(mmt)
            for tc in range(8):
                bi = tc % 2

                def evd(v, tc=tc, bi=bi):
                    return v.tensor_copy(dst_[bi][:], pb2[tc].ap())
                et = p.add("vector", evd,
                           waits=(mm_toks[-1], bank2_free.get(('d', bi))), inc=sV)
                bank2_free[tc] = et

                def std(s, tc=tc, k=k, hf=hf, bi=bi):
                    return s.dma_start(
                        out=AP(dff_h.tensor, (hf * TH + tc * 128) * D + k * 512,
                               [[D, 128], [1, 512]]),
                        in_=dst_[bi][:])
                dt_ = p.add("sync", std, waits=(et,), inc=dO, k=16)
                bank2_free[('d', bi)] = dt_
                down_done.append(dt_)
        scD2.close()
        dff_done.append(down_done[-1])
        half_gate = down_done[-1]

    if dbg is not None:
        def d1(s):
            return s.dma_start(out=dbg["Mg"], in_=Mg)
        p.add("sync", d1, waits=(dff_done[-1],), inc=dO, k=16)
        def d2(s):
            return s.dma_start(out=dbg["g"], in_=g_h)
        p.add("sync", d2, waits=(dff_done[-1],), inc=dO, k=16)
        def d3(s):
            return s.dma_start(out=dbg["h"], in_=h_h)
        p.add("sync", d3, waits=(dff_done[-1],), inc=dO, k=16)
        def d4(s):
            return s.dma_start(out=dbg["dff"], in_=dff_h)
        p.add("sync", d4, waits=(dff_done[-1],), inc=dO, k=16)

    scF = b.scope()
    aot = [b.sbs(scF, f"f0_aot{i}", [128, D], F32) for i in range(2)]
    rxf = [b.sbs(scF, f"f0_rxf{i}", [128, D], F32) for i in range(2)]
    accf = [b.sbs(scF, f"f0_accf{i}", [128, D], F32) for i in range(2)]
    sqf = b.sbs(scF, "f0_sqf", [128, D], BF16)
    mcolf = b.sbs(scF, "f0_mcolf", [128, 8], F32)
    last_store = None
    aot_free = {}
    accf_free = {}
    for tt in range(NT):
        bi = tt % 2

        def ldao(s, tt=tt, bi=bi):
            return s.dma_start(out=aot[bi][:], in_=dff_h[tt * 128:(tt + 1) * 128, :])
        at = p.add("sync", ldao, waits=(aot_free.get(bi), dff_done[-1]), inc=dX, k=16)

        def ldrx(s, tt=tt, bi=bi):
            return s.dma_start(out=rxf[bi][:], in_=x_in[tt * 128:(tt + 1) * 128, :])
        rt = p.add("sync", ldrx, waits=(aot_free.get(('r', bi)),), inc=dX, k=16)

        def addr(v, bi=bi):
            return v.tensor_tensor(accf[bi][:], aot[bi][:], rxf[bi][:], ALU.add)
        adt = p.add("vector", addr, waits=(at, rt, accf_free.get(bi)), inc=sV)
        aot_free[bi] = adt
        aot_free[('r', bi)] = adt

        lt = emit_ln(p, sems, accf[bi], sqf, mcolf, bi * 4, (adt,))

        def sto(s, tt=tt, bi=bi):
            return s.dma_start(out=out_ap[tt * 128:(tt + 1) * 128, :], in_=accf[bi][:])
        last_store = p.add("sync", sto, waits=(lt,), inc=dO, k=16)
        accf_free[bi] = last_store
    scF.close()
    return last_store


def build_l0ffn(serial=False, debug=False):
    b = Bld()
    p = Prog(b)
    p.serial = serial
    p.serial_all = serial
    x_in = b.inp("x", [T, D])
    wgT_in = b.inp("wgT", [8, 512, HID8])
    wuT_in = b.inp("wuT", [8, 512, HID8])
    wdT_in = b.inp("wdT", [8, HID8, 512])
    id_in = b.inp("ident", [128, 128])
    out_ap = b.outp("out", [T, D])
    sems = {k2: b.sem(k2) for k2 in ("dW", "dX", "dO", "T", "V", "A", "P")}
    p.all_sems = sems
    ident_sb = b.sb("ident_sb", [128, 128], F32)

    def ldi(s):
        return s.dma_start(out=ident_sb[:], in_=id_in)
    itok = p.add("sync", ldi, inc=sems["dX"], k=16)
    dbg = None
    if debug:
        dbg = dict(Mg=b.outp("dbg_Mg", [8 * 4096 * HID8], BF16),
                   g=b.outp("dbg_g", [11264, T], BF16),
                   h=b.outp("dbg_h", [11264, T], BF16),
                   dff=b.outp("dbg_dff", [T, D]))
    last = emit_l0ffn(p, b, sems, x_in, wgT_in, wuT_in, wdT_in, ident_sb, out_ap,
                      start_tok=itok, dbg=dbg)

    def fin(s, last=last):
        return s.nop()
    p.add("sync", fin, waits=(last,))
    p.finish()
    b.es.close()
    return b.nc


NZ = 16
INTER = 8192
CONVD = 10240


def emit_l0mamba(p, b, sems, x_in, xhalo_in, wT_in, convw_in, convb_in, dtb_in,
                 Alog_in, Drow_in, normw_in, opT_in, zcol_in, m1z_in, umask_in,
                 lemask_in, upstrict_in, ident_sb, out_ap, replica_groups,
                 start_tok=None, dbg=None):
    _dbg = dbg
    p.serial = True
    """x_in [T,4096]; xhalo_in [3,4096] (zeros for half-0); wT_in [4096,18560]
    (host-transposed in_proj); convw_in [128,320] ([p, oc*4+k]); convb_in [128,80];
    dtb/Alog [128,1]; Drow_in [1,128]; normw_in [128,64] ([p, ic]); opT_in
    [8192,4096] (host-transposed out_proj); zcol (1 iff half-0); m1z (1-z);
    umask/lemask/upstrict [128,128] masks."""
    nc = b.nc
    dW, dX, dO = sems["dW"], sems["dX"], sems["dO"]
    sT, sV, sA, sP = sems["T"], sems["V"], sems["A"], sems["P"]

    hBC_h = b.dram("m_hBC", [CONVD, T], BF16)
    gate_h = b.dram("m_gate", [T, INTER], BF16)
    yT_h = b.dram("m_yT", [INTER, T], BF16)
    mo_h = b.dram("m_mo", [T, D], F32)
    stc_h = b.dram("m_stc", [128, 8192], F32)
    stp_h = b.dram("m_stp", [128, 8192], F32)

    cwb = b.sb("m_cwb", [128, 320], F32)
    cbb = b.sb("m_cbb", [128, 80], F32)
    dtbc = b.sb("m_dtbc", [128, 1], F32)
    Acol = b.sb("m_Acol", [128, 1], F32)
    zc = b.sb("m_zc", [128, 1], F32)
    m1z = b.sb("m_m1z", [128, 1], F32)
    umaskb = b.sb("m_umask", [128, 128], BF16)
    lemaskb = b.sb("m_lemask", [128, 128], F32)
    upstrb = b.sb("m_upstr", [128, 128], BF16)
    Dexp = b.sb("m_Dexp", [128, 128], F32)
    onesr = b.sb("m_onesr", [1, 128], F32)
    Ibz = b.sb("m_Ibz", [128, 128], BF16)
    scDT = b.scope()
    dtv = b.sbs(scDT, "m_dtv", [128, T], F32)

    def ldc(s):
        s.dma_start(out=cwb[:], in_=convw_in)
        return s.dma_start(out=cbb[:], in_=convb_in)
    def ldc1(s):
        return s.dma_start(out=cwb[:], in_=convw_in)
    c1 = p.add("sync", ldc1, waits=(start_tok,), inc=dX, k=16)

    def ldc2(s):
        return s.dma_start(out=cbb[:], in_=convb_in)
    p.add("sync", ldc2, inc=dX, k=16)

    def ldc3(s):
        return s.dma_start(out=dtbc[:], in_=dtb_in)
    p.add("sync", ldc3, inc=dX, k=16)

    def ldc4(s):
        return s.dma_start(out=Acol[:], in_=Alog_in)
    c4 = p.add("sync", ldc4, inc=dX, k=16)

    def ldc5(s):
        s.dma_start(out=zc[:], in_=zcol_in)
        return s.dma_start(out=m1z[:], in_=m1z_in)
    def ldc5a(s):
        return s.dma_start(out=zc[:], in_=zcol_in)
    p.add("sync", ldc5a, inc=dX, k=16)

    def ldc5b(s):
        return s.dma_start(out=m1z[:], in_=m1z_in)
    p.add("sync", ldc5b, inc=dX, k=16)

    def ldc6(g):
        return g.dma_start(out=umaskb[:], in_=umask_in)
    p.add("gpsimd", ldc6, waits=(start_tok,), inc=dW, k=16)

    def ldc7(s):
        return s.dma_start(out=lemaskb[:], in_=lemask_in)
    p.add("sync", ldc7, inc=dX, k=16)

    def ldc8(g):
        return g.dma_start(out=upstrb[:], in_=upstrict_in)
    c8 = p.add("gpsimd", ldc8, inc=dW, k=16)

    def mkA(s_):
        return s_.activation(Acol[:], Acol[:], AF.Exp)
    a1 = p.add("scalar", mkA, waits=(c4,), inc=sA)

    def mkA2(v):
        return v.tensor_scalar(Acol[:], Acol[:], -1.0, None, ALU.mult)
    atok = p.add("vector", mkA2, waits=(a1,), inc=sV)

    scd = b.scope()
    pD = b.pss(scd, "m_pD", [128, 128], F32)
    Drowb = b.sbs(scd, "m_Drow", [1, 128], F32)

    def mo1(g):
        return g.memset(onesr[:], 1.0)
    o1t = p.add("gpsimd", mo1, waits=(start_tok,), inc=sP)

    def ldD(s):
        return s.dma_start(out=Drowb[:], in_=Drow_in)
    dtk = p.add("sync", ldD, inc=dX, k=16)

    def mmD(t):
        return t.matmul(pD[:], onesr[0:1, :], Drowb[0:1, :], start=True, stop=True)
    dmt = p.add("tensor", mmD, waits=(o1t, dtk), inc=sT)

    def evD(v):
        return v.tensor_copy(Dexp[:], pD[:])
    dexp_tok = p.add("vector", evD, waits=(dmt,), inc=sV)
    scd.close()

    scA = b.scope()
    x0T = b.sbs(scA, "m_x0T", [128, 32 * 2051], BF16)
    scAx = b.scope()
    xrow = [b.sbs(scAx, f"m_xr{i}", [128, D], F32) for i in range(2)]
    pbA = [b.pss(scA, f"m_pA{i}", [128, 512], F32) for i in range(8)]

    def ldh(s):
        return s.dma_start(out=xrow[0][0:3, :], in_=xhalo_in)
    htok = p.add("sync", ldh, waits=(start_tok,), inc=dX, k=16)
    htoks = []
    for grp in range(8):
        def trh(t, grp=grp):
            for j in range(4):
                c = grp * 4 + j
                inst = t.transpose(AP(pbA[grp % 2], j * 128, [[512, 128], [1, 3]]),
                                   xrow[0][0:3, c * 128:(c + 1) * 128],
                                   ident_sb[0:3, 0:3])
            return inst
        tt_ = p.add("tensor", trh, waits=(htok, htoks[-2] if len(htoks) >= 2 else None),
                    inc=sT)

        def evh(v, grp=grp):
            return v.tensor_copy(
                AP(x0T, grp * 4 * 2051, [[32 * 2051, 128], [2051, 4], [1, 3]]),
                AP(pbA[grp % 2], 0, [[512, 128], [128, 4], [1, 3]]))
        htoks.append(p.add("vector", evh, waits=(tt_,), inc=sV))
    bank_freeA = {id(pbA[i]): htoks[-1] for i in range(2)}
    xrow_free = {0: sems["T"].tok()}
    for st in range(4):
        for q in range(4):
            bi = (st * 4 + q) % 2
            r0 = st * 512 + q * 128

            def ldx(s, r0=r0, bi=bi):
                return s.dma_start(out=xrow[bi][:], in_=x_in[r0:r0 + 128, :])
            xtok = p.add("sync", ldx, waits=(xrow_free.get(bi), htoks[-1]), inc=dX, k=16)
            toks = emit_transpose_tile(
                p, sems, xrow[bi], ident_sb,
                lambda c0, q, st=st: AP(x0T, c0 * 2051 + 3 + st * 512 + q * 128,
                                        [[32 * 2051, 128], [2051, 4], [1, 128]]),
                pbA[0:4], bank_freeA, q, wait_x=xtok)
            xrow_free[bi] = sems["T"].tok()
    xT_ready = sems["V"].tok()
    scAx.close()

    scB = b.scope()
    Wob = [b.sbs(scB, f"m_Wob{i}", [128, 32 * 128], BF16) for i in range(2)]
    projr = [b.sbs(scB, f"m_pr{i}", [128, 2051], F32) for i in range(2)]
    cacc = [b.sbs(scB, f"m_cacc{i}", [128, 2048], F32) for i in range(2)]
    hst = [b.sbs(scB, f"m_hst{i}", [128, 2048], BF16) for i in range(2)]
    ngroups = [(0, 3)] + [(3 + i * 512, 512) for i in range(4)]
    wfree = {}
    prfree = {}
    hstfree = {}
    for oc in range(81):
        bi = oc % 2

        def ldw(g, oc=oc, bi=bi):
            return g.dma_start(
                out=AP(Wob[bi], 0, [[32 * 128, 128], [128, 32], [1, 128]]),
                in_=AP(wT_in.tensor, 8192 + oc * 128,
                       [[18560, 128], [128 * 18560, 32], [1, 128]]))
        wtok = p.add("gpsimd", ldw, waits=(wfree.get(bi), c8), inc=dW, k=16)

        ev5 = []
        for ng, (coff, csz) in enumerate(ngroups):
            bank = pbA[4 + (oc * 5 + ng) % 2]

            def mmp(t, oc=oc, coff=coff, csz=csz, bi=bi, bank=bank):
                for k in range(32):
                    inst = t.matmul(bank[:, 0:csz],
                                    Wob[bi][:, k * 128:(k + 1) * 128],
                                    x0T[:, k * 2051 + coff: k * 2051 + coff + csz],
                                    start=(k == 0), stop=(k == 31))
                return inst
            mmt = p.add("tensor", mmp,
                        waits=(wtok, xT_ready, bank_freeA.get(id(bank))), inc=sT)

            def evp(s_, coff=coff, csz=csz, bi=bi, bank=bank):
                return s_.activation(projr[bi][:, coff:coff + csz], bank[:, 0:csz],
                                     AF.Copy)
            et = p.add("scalar", evp, waits=(mmt, prfree.get(bi) if ng == 0 else None),
                       inc=sA)
            bank_freeA[id(bank)] = et
            ev5.append(et)
        wfree[bi] = sems["T"].tok()

        if oc < 80:
            def cv0(v, oc=oc, bi=bi):
                return v.tensor_scalar(cacc[bi][:], projr[bi][:, 0:2048],
                                       cwb[:, oc * 4:oc * 4 + 1], None, ALU.mult)
            t0 = p.add("vector", cv0, waits=tuple(ev5) + ((c1,)), inc=sV)
            prev = t0
            for kk in range(1, 4):
                def cvk(v, oc=oc, kk=kk, bi=bi):
                    return v.scalar_tensor_tensor(
                        cacc[bi][:], projr[bi][:, kk:kk + 2048],
                        cwb[:, oc * 4 + kk:oc * 4 + kk + 1], cacc[bi][:],
                        ALU.mult, ALU.add)
                prev = p.add("vector", cvk, waits=(prev,), inc=sV)
            prfree[bi] = prev

            def sil(s_, oc=oc, bi=bi):
                return s_.activation(hst[bi][:], cacc[bi][:], AF.Silu,
                                     bias=cbb[:, oc:oc + 1])
            stok = p.add("scalar", sil, waits=(prev, hstfree.get(bi)), inc=sA)

            def sth(s, oc=oc, bi=bi):
                return s.dma_start(out=AP(hBC_h.tensor, oc * 128 * T, [[T, 128], [1, T]]),
                                   in_=hst[bi][:])
            hstfree[bi] = p.add("sync", sth, waits=(stok,), inc=dO, k=16)
        else:
            def d1(v, bi=bi):
                return v.tensor_scalar(projr[bi][:, 3:2051], projr[bi][:, 3:2051],
                                       dtbc[:], None, ALU.add)
            t1 = p.add("vector", d1, waits=tuple(ev5), inc=sV)

            def d2(v, bi=bi):
                return v.tensor_scalar_max(cacc[bi][:], projr[bi][:, 3:2051], 0.0)
            t2 = p.add("vector", d2, waits=(t1,), inc=sV)

            def d3b(s_, bi=bi):
                return s_.activation(cacc[1][:], projr[bi][:, 3:2051], AF.Abs)
            t3 = p.add("scalar", d3b, waits=(t2,), inc=sA)

            def d4(s_):
                return s_.activation(cacc[1][:], cacc[1][:], AF.Exp, scale=-1.0)
            t4 = p.add("scalar", d4, waits=(t3,), inc=sA)

            def d5(v):
                return v.tensor_scalar(cacc[1][:], cacc[1][:], 1.0, None, ALU.add)
            t5 = p.add("vector", d5, waits=(t4,), inc=sV)

            def d6(s_):
                return s_.activation(cacc[1][:], cacc[1][:], AF.Ln)
            t6 = p.add("scalar", d6, waits=(t5,), inc=sA)

            def d7(v, bi=bi):
                return v.tensor_tensor(dtv[:], cacc[bi][:], cacc[1][:], ALU.add)
            t7 = p.add("vector", d7, waits=(t6,), inc=sV)

            def d8(v):
                return v.tensor_scalar_min(dtv[:], dtv[:], 100.0)
            dt_tok = p.add("vector", d8, waits=(t7,), inc=sV)
    hbc_done = sems["dO"].tok()
    dtv_tok = sems["V"].tok()
    scB.close()

    scC2 = b.scope()
    Wgb = b.sbs(scC2, "m_Wgb", [128, 32 * 512], BF16)
    gstg = [b.sbs(scC2, f"m_gstg{i}", [128, 512], BF16) for i in range(2)]
    gfree = {}
    gstf = {}
    wg_prev = (hbc_done,)
    for gf in range(16):
        def ldwg(g, gf=gf):
            return g.dma_start(
                out=AP(Wgb, 0, [[32 * 512, 128], [512, 32], [1, 512]]),
                in_=AP(wT_in.tensor, gf * 512,
                       [[18560, 128], [128 * 18560, 32], [1, 512]]))
        wgt = p.add("gpsimd", ldwg, waits=wg_prev, inc=dW, k=16)
        for tc in range(16):
            bank = pbA[6 + tc % 2]

            def mmg(t, gf=gf, tc=tc, bank=bank):
                for k in range(32):
                    inst = t.matmul(bank[:],
                                    x0T[:, k * 2051 + 3 + tc * 128: k * 2051 + 3 + tc * 128 + 128],
                                    Wgb[:, k * 512:(k + 1) * 512],
                                    start=(k == 0), stop=(k == 31))
                return inst
            mmt = p.add("tensor", mmg, waits=(wgt, gfree.get(tc % 2)), inc=sT)

            def evg(v, tc=tc, bank=bank, bi=tc % 2):
                return v.tensor_copy(gstg[bi][:], bank.ap())
            et = p.add("vector", evg, waits=(mmt, gstf.get(tc % 2)), inc=sV)
            gfree[tc % 2] = et

            def stg(s, gf=gf, tc=tc, bi=tc % 2):
                return s.dma_start(
                    out=AP(gate_h.tensor, tc * 128 * INTER + gf * 512,
                           [[INTER, 128], [1, 512]]),
                    in_=gstg[bi][:])
            gstf[tc % 2] = p.add("sync", stg, waits=(et,), inc=dO, k=16)
        wg_prev = (sems["T"].tok(),)
    gate_done = sems["dO"].tok()
    scC2.close()
    scA.close()

    scP = b.scope()
    prevf = b.sbs(scP, "m_prevf", [128, 8192], F32)
    prevbf = b.sbs(scP, "m_prevbf", [128, 8192], BF16)
    at_s = b.sbs(scP, "m_at", [128, 128], F32)
    acum = b.sbs(scP, "m_acum", [128, 128], F32)
    dec_s = b.sbs(scP, "m_dec", [128, 128], F32)
    E_s = b.sbs(scP, "m_E", [128, 128], F32)
    dtT = b.sbs(scP, "m_dtT", [128, 128], F32)
    decT = b.sbs(scP, "m_decT", [128, 128], F32)
    ET = b.sbs(scP, "m_ET", [128, 128], F32)
    aT = b.sbs(scP, "m_aT", [128, 128], F32)
    gT = b.sbs(scP, "m_gT", [128, 128], F32)
    grow = b.sbs(scP, "m_grow", [1, 128], F32)
    nc8 = b.sbs(scP, "m_nc8", [128, 16], F32)

    def ssd_pass(full, init_tok, dt_tok, hbc_done, gate_done):
        scZ = b.scope()
        hbz = b.sbs(scZ, f"m_hbz{int(full)}", [128, 80 * 128], BF16)
        hs_tok = b.sbs(scZ, f"m_hstk{int(full)}", [128, 64 * 128], BF16)
        B_tok = b.sbs(scZ, f"m_Btok{int(full)}", [128, 1024], BF16)
        xch = b.sbs(scZ, f"m_xch{int(full)}", [128, 8192], BF16)
        Bdec = b.sbs(scZ, f"m_Bdec{int(full)}", [128, 8192], BF16)
        tmp1 = b.sbs(scZ, f"m_tmp1{int(full)}", [128, 512], F32)
        if full:
            gz = b.sbs(scZ, "m_gz", [128, 8192], BF16)
            rhsD2 = b.sbs(scZ, "m_rhsD2", [128, 8192], BF16)
            Msb = rhsD2
            Sgsb = b.sbs(scZ, "m_Sgsb", [128, 1024], BF16)
            ybuf = b.sbs(scZ, "m_ybuf", [128, 8192], F32)
            ystage = b.sbs(scZ, "m_ystage", [128, 1024], BF16)
        pbZ = [b.pss(scZ, f"m_pZ{int(full)}_{i}", [128, 512], F32) for i in range(6)]
        pbB = b.pss(scZ, f"m_pB{int(full)}", [128, 1024], BF16)

        hbz_free = None
        gz_free = None
        ytoks = []
        old_serial = p.serial
        p.serial = True
        for z in range(NZ):
            def ldhbz(s, z=z):
                return s.dma_start(
                    out=AP(hbz, 0, [[80 * 128, 128], [128, 80], [1, 128]]),
                    in_=AP(hBC_h.tensor, z * 128, [[T, 128], [128 * T, 80], [1, 128]]))
            hbt = p.add("sync", ldhbz, waits=(hbz_free, hbc_done, init_tok), inc=dX, k=16)
            if full:
                def ldgz(s, z=z):
                    return s.dma_start(out=gz[:],
                                       in_=gate_h[z * 128:(z + 1) * 128, :])
                gzt = p.add("sync", ldgz, waits=(gz_free, gate_done), inc=dX, k=16)

            def mka(v, z=z):
                return v.tensor_scalar(at_s[:], dtv[:, z * 128:(z + 1) * 128],
                                       Acol[:], None, ALU.mult)
            t1 = p.add("vector", mka, waits=(dt_tok, atok), inc=sV)

            def mkacum(v):
                return v.tensor_tensor_scan(acum[:], at_s[:], at_s[:], 0.0,
                                            ALU.add, ALU.bypass)
            t2 = p.add("vector", mkacum, waits=(t1,), inc=sV)

            def mkdec(s_):
                return s_.activation(dec_s[:], acum[:], AF.Exp, scale=-1.0,
                                     bias=acum[:, 127:128])
            t3 = p.add("scalar", mkdec, waits=(t2,), inc=sA)

            def mkE(s_):
                return s_.activation(E_s[:], acum[:], AF.Exp)
            t4 = p.add("scalar", mkE, waits=(t2,), inc=sA)

            trs = [(dtv, dtT, ("slice", z)), (dec_s, decT, None), (E_s, ET, None),
                   (at_s, aT, None)]
            ttoks = []
            for i, (srct, dstt, mode) in enumerate(trs):
                def trf(t, srct=srct, mode=mode, i=i):
                    if mode is not None:
                        srcap = srct[:, mode[1] * 128:(mode[1] + 1) * 128]
                    else:
                        srcap = srct[:, :]
                    return t.transpose(
                        AP(pbZ[0], (i % 4) * 128, [[512, 128], [1, 128]]),
                        srcap, ident_sb[:, :])
                tt_ = p.add("tensor", trf, waits=(t3, t4), inc=sT)
                ttoks.append(tt_)

            def evtr(v):
                v.tensor_copy(dtT[:], AP(pbZ[0], 0, [[512, 128], [1, 128]]))
                v.tensor_copy(decT[:], AP(pbZ[0], 128, [[512, 128], [1, 128]]))
                v.tensor_copy(ET[:], AP(pbZ[0], 256, [[512, 128], [1, 128]]))
                return v.tensor_copy(aT[:], AP(pbZ[0], 384, [[512, 128], [1, 128]]))
            trt = p.add("vector", evtr, waits=(ttoks[-1],), inc=sV)

            def trg(t):
                return t.transpose(AP(pbZ[1], 0, [[512, 1], [1, 128]]),
                                   acum[:, 127:128], ident_sb[:, :])
            g1 = p.add("tensor", trg, waits=(t2, trt), inc=sT)

            def evg2(v):
                return v.tensor_copy(grow[0:1, :],
                                     AP(pbZ[1], 0, [[512, 1], [1, 128]]))
            g2 = p.add("vector", evg2, waits=(g1,), inc=sV)

            def mmg2(t):
                return t.matmul(pbZ[1][:, 0:128], onesr[0:1, :], grow[0:1, :],
                                start=True, stop=True)
            g3 = p.add("tensor", mmg2, waits=(g2,), inc=sT)

            def evg3(s_):
                return s_.activation(gT[:], pbZ[1][:, 0:128], AF.Exp)
            gT_tok = p.add("scalar", evg3, waits=(g3,), inc=sA)

            hst_toks = []
            for grp in range(8):
                def trh(t, grp=grp):
                    for j in range(8):
                        c = grp * 8 + j
                        inst = t.transpose(pbB[:, j * 128:(j + 1) * 128],
                                           hbz[:, c * 128:(c + 1) * 128], umaskb[:, :]
                                           if False else Ibz[:, :])
                    return inst
                tt_ = p.add("tensor", trh, waits=(hbt, hst_toks[-1] if hst_toks else None),
                            inc=sT)

                def evh(v, grp=grp):
                    return v.tensor_copy(hs_tok[:, grp * 1024:(grp + 1) * 1024], pbB[:])
                hst_toks.append(p.add("vector", evh, waits=(tt_,), inc=sV))

            def trB(t):
                for j in range(8):
                    inst = t.transpose(pbB[:, j * 128:(j + 1) * 128],
                                       hbz[:, (64 + j) * 128:(64 + j + 1) * 128],
                                       Ibz[:, :])
                return inst
            tb_ = p.add("tensor", trB, waits=(hbt, hst_toks[-1]), inc=sT)

            def evB(v):
                return v.tensor_copy(B_tok[:], pbB[:])
            btok_t = p.add("vector", evB, waits=(tb_,), inc=sV)

            def mkxch(v):
                return v.tensor_tensor(
                    xch[:],
                    AP(hs_tok, 0, [[64 * 128, 128], [1, 8192]]),
                    AP(dtT, 0, [[128, 128], [1, 128], [0, 64]]), ALU.mult)
            xch_tok = p.add("vector", mkxch, waits=(hst_toks[-1], trt), inc=sV)

            if full:
                def cprev(g):
                    return g.tensor_copy(prevbf[:], prevf[:])
                pbf_tok = p.add("gpsimd", cprev, waits=(init_tok,), inc=sP)

                sg_toks = []
                for g_ in range(8):
                    def mmsg(t, g_=g_):
                        return t.matmul(pbZ[2][:, 0:128],
                                        hbz[:, (64 + g_) * 128:(64 + g_ + 1) * 128],
                                        hbz[:, (72 + g_) * 128:(72 + g_ + 1) * 128],
                                        start=True, stop=True)
                    m1 = p.add("tensor", mmsg, waits=(hbt, sg_toks[-1] if sg_toks else None),
                               inc=sT)

                    def evsg(v, g_=g_):
                        return v.tensor_tensor(Sgsb[:, g_ * 128:(g_ + 1) * 128],
                                               pbZ[2][:, 0:128], umaskb[:, :], ALU.mult)
                    sg_toks.append(p.add("vector", evsg, waits=(m1,), inc=sV))
                sg_all = sg_toks[-1]

            if full:
                def gsil(s_):
                    return s_.activation(gz[:], gz[:], AF.Silu)
                gz_sil = p.add("scalar", gsil, waits=(gzt,), inc=sA)

            def pscale(g):
                return g.tensor_tensor(
                    prevf[:], prevf[:],
                    AP(gT, 0, [[128, 128], [1, 128], [0, 64]]), ALU.mult)
            u1_tok = p.add("gpsimd", pscale,
                           waits=(gT_tok, init_tok) + ((pbf_tok,) if full else ()),
                           inc=sP)

            for hh in range(2):
                def mkbdec(g, hh=hh):
                    return g.tensor_tensor(
                        Bdec[:],
                        AP(B_tok, hh * 4 * 128, [[1024, 128], [128, 4], [0, 16], [1, 128]]),
                        AP(decT, hh * 64, [[128, 128], [16, 4], [1, 16], [0, 128]]),
                        ALU.mult)
                bd_tok = p.add("gpsimd", mkbdec, waits=(btok_t, trt), inc=sP)

                if full:
                    def mkrhs(g, hh=hh):
                        return g.tensor_tensor(
                            rhsD2[:],
                            AP(aT, hh * 64, [[128, 128], [1, 64], [0, 128]]),
                            AP(lemaskb, 0, [[128, 128], [0, 64], [1, 128]]),
                            ALU.mult)
                    rhs_tok = p.add("gpsimd", mkrhs, waits=(trt,), inc=sP)
                    m_toks = []
                    for i in range(16):
                        def mmd2(t, i=i):
                            return t.matmul(pbZ[3 + i % 2][:], upstrb[:, :],
                                            rhsD2[:, i * 512:(i + 1) * 512],
                                            start=True, stop=True)
                        md = p.add("tensor", mmd2,
                                   waits=(rhs_tok, c8, m_toks[-2] if len(m_toks) >= 2 else None),
                                   inc=sT)

                        def expd(s_, i=i):
                            return s_.activation(ystage[:, 0:512], pbZ[3 + i % 2][:],
                                                 AF.Exp)
                        ed = p.add("scalar", expd, waits=(md,), inc=sA)

                        def mkm(v, i=i, hh=hh):
                            g_ = (hh * 64 + i * 4) // 16
                            return v.tensor_tensor(
                                Msb[:, i * 512:(i + 1) * 512], ystage[:, 0:512],
                                AP(Sgsb, g_ * 128, [[1024, 128], [0, 4], [1, 128]]),
                                ALU.mult)
                        m_toks.append(p.add("vector", mkm, waits=(ed, sg_all), inc=sV))
                    m_all = m_toks[-1]

                for hg in range(8):
                    def mmy(t, hh=hh, hg=hg, full=full):
                        inst = None
                        for h8 in range(8):
                            hloc = hh * 64 + hg * 8 + h8
                            g_ = hloc // 16
                            hi = hg * 8 + h8
                            if full:
                                t.matmul(pbZ[3][:, h8 * 64:(h8 + 1) * 64],
                                         Msb[:, hi * 128:(hi + 1) * 128],
                                         xch[:, hloc * 64:(hloc + 1) * 64],
                                         start=True, stop=True)
                                t.matmul(pbZ[4][:, h8 * 64:(h8 + 1) * 64],
                                         hbz[:, (72 + g_) * 128:(72 + g_ + 1) * 128],
                                         prevbf[:, hloc * 64:(hloc + 1) * 64],
                                         start=True, stop=True)
                            inst = t.matmul(pbZ[5][:, h8 * 64:(h8 + 1) * 64],
                                            Bdec[:, hi * 128:(hi + 1) * 128],
                                            xch[:, hloc * 64:(hloc + 1) * 64],
                                            start=True, stop=True)
                        return inst
                    ws = [xch_tok, bd_tok, u1_tok]
                    if full:
                        ws += [m_all, pbf_tok]
                    mmt = p.add("tensor", mmy, waits=tuple(ws), inc=sT)

                    def padd(v, hh=hh, hg=hg):
                        off = (hh * 64 + hg * 8) * 64
                        return v.tensor_tensor(prevf[:, off:off + 512],
                                               prevf[:, off:off + 512],
                                               pbZ[5][:], ALU.add)
                    pu = p.add("vector", padd, waits=(mmt, u1_tok), inc=sV)

                    if full:
                        def yev1(v, hh=hh, hg=hg):
                            return v.tensor_tensor(
                                tmp1[:], pbZ[4][:],
                                AP(ET, hh * 64 + hg * 8, [[128, 128], [1, 8], [0, 64]]),
                                ALU.mult)
                        y1t = p.add("vector", yev1, waits=(mmt,), inc=sV)

                        def yev2(v, hh=hh, hg=hg):
                            off = (hh * 64 + hg * 8) * 64
                            return v.tensor_tensor(ybuf[:, off:off + 512], tmp1[:],
                                                   pbZ[3][:], ALU.add)
                        y2t = p.add("vector", yev2, waits=(y1t,), inc=sV)

                        def yev3(g, hh=hh, hg=hg):
                            off = (hh * 64 + hg * 8) * 64
                            return g.tensor_tensor(
                                tmp1[:],
                                AP(hs_tok, off, [[64 * 128, 128], [1, 512]]),
                                AP(Dexp, hh * 64 + hg * 8, [[128, 128], [1, 8], [0, 64]]),
                                ALU.mult)
                        y3t = p.add("gpsimd", yev3, waits=(y2t, dexp_tok), inc=sP)

                        def yev4(v, hh=hh, hg=hg):
                            off = (hh * 64 + hg * 8) * 64
                            return v.tensor_tensor(ybuf[:, off:off + 512],
                                                   ybuf[:, off:off + 512], tmp1[:],
                                                   ALU.add)
                        p.add("vector", yev4, waits=(y3t,), inc=sV)

            if full:
                def gmul(v):
                    return v.tensor_tensor(ybuf[:], ybuf[:], gz[:], ALU.mult)
                gm = p.add("vector", gmul, waits=(gz_sil, sems["V"].tok()), inc=sV)

                sq_toks = []
                for gg_ in range(8):
                    def sqg(s_, gg_=gg_):
                        return s_.activation(rhsD2[:, 0:1024],
                                             ybuf[:, gg_ * 1024:(gg_ + 1) * 1024],
                                             AF.Square, accum_out=nc8[:, gg_:gg_ + 1])
                    sq_toks.append(p.add("scalar", sqg,
                                         waits=(gm, sq_toks[-1] if sq_toks else None),
                                         inc=sA))

                def veps2(v):
                    return v.tensor_scalar(nc8[:, 8:16], nc8[:, 0:8], 1.0 / 1024.0,
                                           EPS, ALU.mult, ALU.add)
                ve = p.add("vector", veps2, waits=(sq_toks[-1],), inc=sV)

                def rstd2(s_):
                    s_.activation(nc8[:, 8:16], nc8[:, 8:16], AF.Ln)
                    return s_.activation(nc8[:, 8:16], nc8[:, 8:16], AF.Exp, scale=-0.5)
                rs = p.add("scalar", rstd2, waits=(ve,), inc=sA)

                nrm_toks = []
                for gg_ in range(8):
                    def nrmg(v, gg_=gg_):
                        return v.tensor_scalar(gz[:, gg_ * 1024:(gg_ + 1) * 1024],
                                               ybuf[:, gg_ * 1024:(gg_ + 1) * 1024],
                                               nc8[:, 8 + gg_:9 + gg_], None, ALU.mult)
                    nrm_toks.append(p.add("vector", nrmg, waits=(rs,), inc=sV))
                nrm_all = nrm_toks[-1]

                ztoks = []
                for grp in range(8):
                    def try_(t, grp=grp):
                        for j in range(8):
                            c = grp * 8 + j
                            inst = t.transpose(pbB[:, j * 128:(j + 1) * 128],
                                               gz[:, c * 128:(c + 1) * 128], Ibz[:, :])
                        return inst
                    tt_ = p.add("tensor", try_, waits=(nrm_all,
                                                       ztoks[-1] if ztoks else None), inc=sT)

                    def evy(v, grp=grp):
                        return v.tensor_copy(ystage[:], pbB[:])
                    ee = p.add("vector", evy, waits=(tt_,), inc=sV)

                    def sty(s, grp=grp, z=z):
                        return s.dma_start(
                            out=AP(yT_h.tensor, grp * 8 * 128 * T + z * 128,
                                   [[T, 128], [128 * T, 8], [1, 128]]),
                            in_=AP(ystage, 0, [[1024, 128], [128, 8], [1, 128]]))
                    ztoks.append(p.add("sync", sty, waits=(ee,), inc=dO, k=16))
                ytoks.append(ztoks[-1])
                gz_free = ztoks[-1]
            hbz_free = sems["T"].tok()
        p.serial = old_serial
        scZ.close()
        return ytoks[-1] if full else sems["V"].tok()

    def mkIbz(g):
        return g.tensor_copy(Ibz[:], ident_sb[:, :])
    ibz_tok = p.add("gpsimd", mkIbz, waits=(start_tok,), inc=sP)

    def pz(g):
        return g.memset(prevf[:], 0.0)
    pz_tok = p.add("gpsimd", pz, waits=(ibz_tok,), inc=sP)

    dt_tok = sems["V"].tok()
    sp_done = ssd_pass(False, pz_tok, dt_tok, hbc_done, None)

    scC = b.scope()
    cstg = b.sbs(scC, "m_cstg", [128, 8192], F32)

    def mkcontrib(v):
        return v.tensor_scalar(cstg[:], prevf[:], zc[:], None, ALU.mult)
    cb = p.add("vector", mkcontrib, waits=(sp_done, sems["V"].tok()), inc=sV)

    def stc(s):
        return s.dma_start(out=stc_h, in_=cstg[:])
    cdma = p.add("sync", stc, waits=(cb,), inc=dO, k=16)

    def coll(g):
        return g.collective_compute("AllReduce", ALU.add, replica_groups,
                                    [stc_h.opt()], [stp_h.opt()])
    ctok_ = p.add("gpsimd", coll, waits=(cdma,), inc=sP, k=1)

    def ldprior(s):
        return s.dma_start(out=cstg[:], in_=stp_h)
    pl = p.add("sync", ldprior, waits=(ctok_,), inc=dX, k=16)

    def mkinit(v):
        return v.tensor_scalar(prevf[:], cstg[:], m1z[:], None, ALU.mult)
    init2 = p.add("vector", mkinit, waits=(pl,), inc=sV)
    scC.close()

    y_done = ssd_pass(True, init2, dt_tok, hbc_done, gate_done)
    scP.close()

    scE = b.scope()
    Wop = b.sbs(scE, "m_Wop", [128, 64 * 1024], BF16)
    nwc = b.sbs(scE, "m_nwc", [128, 64], F32)
    ytt = [b.sbs(scE, f"m_ytt{i}", [128, 64 * 128], BF16) for i in range(2)]
    est = [b.sbs(scE, f"m_est{i}", [128, 1024], F32) for i in range(2)]
    pbE = [b.pss(scE, f"m_pE{i}", [128, 512], F32) for i in range(4)]

    def ldnw(s):
        return s.dma_start(out=nwc[:], in_=normw_in)
    nwt = p.add("sync", ldnw, waits=(y_done,), inc=dX, k=16)

    mo_toks = []
    wq_prev2 = (y_done,)
    ytt_free = {}
    est_free = {}
    ebank_free = {}
    for oq in range(4):
        def ldop(g, oq=oq):
            return g.dma_start(
                out=AP(Wop, 0, [[64 * 1024, 128], [1024, 64], [1, 1024]]),
                in_=AP(opT_in.tensor, oq * 1024,
                       [[4096, 128], [128 * 4096, 64], [1, 1024]]))
        wt_ = p.add("gpsimd", ldop, waits=wq_prev2, inc=dW, k=16)

        nw_toks = []
        for ic in range(64):
            def scw(v, ic=ic):
                return v.tensor_scalar(Wop[:, ic * 1024:(ic + 1) * 1024],
                                       Wop[:, ic * 1024:(ic + 1) * 1024],
                                       nwc[:, ic:ic + 1], None, ALU.mult)
            nw_toks.append(p.add("vector", scw, waits=(wt_, nwt), inc=sV))
        nw_all = nw_toks[-1]

        for tc in range(16):
            bi = tc % 2

            def ldy(s, tc=tc, bi=bi):
                return s.dma_start(
                    out=AP(ytt[bi], 0, [[64 * 128, 128], [128, 64], [1, 128]]),
                    in_=AP(yT_h.tensor, tc * 128, [[T, 128], [128 * T, 64], [1, 128]]))
            yt_ = p.add("sync", ldy, waits=(ytt_free.get(bi), y_done), inc=dX, k=16)

            ev2 = []
            for on in range(2):
                bank = pbE[(tc * 2 + on) % 4]

                def mmo2(t, tc=tc, on=on, bank=bank, bi=bi):
                    for ic in range(64):
                        inst = t.matmul(bank[:],
                                        ytt[bi][:, ic * 128:(ic + 1) * 128],
                                        Wop[:, ic * 1024 + on * 512: ic * 1024 + on * 512 + 512],
                                        start=(ic == 0), stop=(ic == 63))
                    return inst
                mmt = p.add("tensor", mmo2,
                            waits=(yt_, nw_all, ebank_free.get((oq, tc, on))), inc=sT)
                ebank_free[(oq, tc + 2, on)] = mmt

                def evo2(s_, on=on, bank=bank, bi=bi):
                    return s_.activation(est[bi][:, on * 512:(on + 1) * 512], bank.ap(),
                                         AF.Copy)
                ev2.append(p.add("scalar", evo2, waits=(mmt, est_free.get(bi)), inc=sA))
            ytt_free[bi] = ev2[-1]

            def stmo(s, oq=oq, tc=tc, bi=bi):
                return s.dma_start(
                    out=AP(mo_h.tensor, tc * 128 * D + oq * 1024, [[D, 128], [1, 1024]]),
                    in_=est[bi][:])
            est_free[bi] = p.add("sync", stmo, waits=(ev2[-1],), inc=dO, k=16)
            mo_toks.append(est_free[bi])
        wq_prev2 = (sems["T"].tok(),)
    mo_done = mo_toks[-1]
    scE.close()

    if _dbg is not None:
        def dmp(nm, src_ap, wait):
            def dd(s, nm=nm, src_ap=src_ap):
                return s.dma_start(out=_dbg[nm], in_=src_ap)
            return p.add("sync", dd, waits=(wait,), inc=dO, k=16)
        dmp("hBC", hBC_h, mo_done)
        dmp("gate", gate_h, mo_done)
        dmp("yT", yT_h, mo_done)
        dmp("mo", mo_h, mo_done)
        dmp("stc", stc_h, mo_done)
        dmp("stp", stp_h, mo_done)
        def ddt(s):
            return s.dma_start(out=_dbg["dtv"], in_=dtv[:])
        p.add("sync", ddt, waits=(mo_done,), inc=dO, k=16)

    scF = b.scope()
    aot = [b.sbs(scF, f"m_aot{i}", [128, D], F32) for i in range(2)]
    rxf = [b.sbs(scF, f"m_rxf{i}", [128, D], F32) for i in range(2)]
    accf = [b.sbs(scF, f"m_accf{i}", [128, D], F32) for i in range(2)]
    sqf = b.sbs(scF, "m_sqf", [128, D], BF16)
    mcolf = b.sbs(scF, "m_mcolf", [128, 8], F32)
    last_store = None
    aot_free = {}
    accf_free = {}
    for tt in range(NT):
        bi = tt % 2

        def ldao(s, tt=tt, bi=bi):
            return s.dma_start(out=aot[bi][:], in_=mo_h[tt * 128:(tt + 1) * 128, :])
        at = p.add("sync", ldao, waits=(aot_free.get(bi), mo_done), inc=dX, k=16)

        def ldrx(s, tt=tt, bi=bi):
            return s.dma_start(out=rxf[bi][:], in_=x_in[tt * 128:(tt + 1) * 128, :])
        rt = p.add("sync", ldrx, waits=(aot_free.get(('r', bi)), mo_done), inc=dX, k=16)

        def addr(v, bi=bi):
            return v.tensor_tensor(accf[bi][:], aot[bi][:], rxf[bi][:], ALU.add)
        adt = p.add("vector", addr, waits=(at, rt, accf_free.get(bi)), inc=sV)
        aot_free[bi] = adt
        aot_free[('r', bi)] = adt

        lt = emit_ln(p, sems, accf[bi], sqf, mcolf, bi * 4, (adt,))

        def sto(s, tt=tt, bi=bi):
            return s.dma_start(out=out_ap[tt * 128:(tt + 1) * 128, :], in_=accf[bi][:])
        last_store = p.add("sync", sto, waits=(lt,), inc=dO, k=16)
        accf_free[bi] = last_store
    scF.close()
    scDT.close()
    p.serial = False
    return last_store


def build_l0mamba(replica_groups, serial=False, debug=False):
    b = Bld()
    p = Prog(b)
    p.serial = serial
    p.serial_all = serial
    x_in = b.inp("x", [T, D])
    xhalo_in = b.inp("xhalo", [3, D])
    wT_in = b.inp("wT", [D, 18560])
    convw_in = b.inp("convw", [128, 320])
    convb_in = b.inp("convb", [128, 80])
    dtb_in = b.inp("dtb", [128, 1])
    Alog_in = b.inp("Alog", [128, 1])
    Drow_in = b.inp("Drow", [1, 128])
    normw_in = b.inp("normw", [128, 64])
    opT_in = b.inp("opT", [INTER, D])
    zcol_in = b.inp("zcol", [128, 1])
    m1z_in = b.inp("m1z", [128, 1])
    umask_in = b.inp("umask", [128, 128])
    lemask_in = b.inp("lemask", [128, 128])
    upstrict_in = b.inp("upstrict", [128, 128], BF16)
    id_in = b.inp("ident", [128, 128])
    out_ap = b.outp("out", [T, D])
    sems = {k2: b.sem(k2) for k2 in ("dW", "dX", "dO", "T", "V", "A", "P")}
    p.all_sems = sems
    ident_sb = b.sb("ident_sb", [128, 128], F32)

    def ldi(s):
        return s.dma_start(out=ident_sb[:], in_=id_in)
    itok = p.add("sync", ldi, inc=sems["dX"], k=16)
    dbg = None
    if debug:
        dbg = dict(hBC=b.outp("dbg_hBC", [CONVD, T], BF16),
                   gate=b.outp("dbg_gate", [T, INTER], BF16),
                   yT=b.outp("dbg_yT", [INTER, T], BF16),
                   mo=b.outp("dbg_mo", [T, D]),
                   stc=b.outp("dbg_stc", [128, 8192]),
                   stp=b.outp("dbg_stp", [128, 8192]),
                   dtv=b.outp("dbg_dtv", [128, T]))
    last = emit_l0mamba(p, b, sems, x_in, xhalo_in, wT_in, convw_in, convb_in,
                        dtb_in, Alog_in, Drow_in, normw_in, opT_in, zcol_in,
                        m1z_in, umask_in, lemask_in, upstrict_in, ident_sb,
                        out_ap, replica_groups, start_tok=itok, dbg=dbg)

    def fin(s, last=last):
        return s.nop()
    p.add("sync", fin, waits=(last,))
    p.finish()
    b.es.close()
    return b.nc


def l0mamba_host_inputs(hc):
    umask = np.triu(np.ones((128, 128), np.float32))
    lemask = np.triu(np.ones((128, 128), np.float32))
    upstrict = np.triu(np.ones((128, 128), np.float32), 1)
    ups = np.tril(np.ones((128, 128), np.float32), -1).T
    ups = (np.arange(128)[:, None] > np.arange(128)[None, :]).astype(np.float32)
    zcol = np.full((128, 1), 1.0 if hc == 0 else 0.0, np.float32)
    m1z = np.full((128, 1), 0.0 if hc == 0 else 1.0, np.float32)
    return dict(umask=umask, lemask=lemask,
                upstrict=ups.astype(ml_dtypes.bfloat16),
                zcol=zcol, m1z=m1z, ident=np.eye(128, dtype=np.float32))




RG = [[0, 1], [2, 3], [4, 5], [6, 7]]
_cache = {}


def _progs():
    if "m" not in _cache:
        _cache["m"] = build_l0mamba(RG)
        _cache["f0"] = build_l0ffn()
        _cache["a"] = build_l1attn(RG)
        _cache["f1"] = build_l1ffn()
    return _cache


def kernel(L0_attn_x, L0_attn_in_proj_w, L0_attn_conv_w, L0_attn_conv_b,
           L0_attn_dt_bias, L0_attn_A_log, L0_attn_D, L0_attn_norm_w,
           L0_attn_out_proj_w, L0_ffn_w_gate, L0_ffn_w_up, L0_ffn_w_down,
           L1_attn_Wq, L1_attn_Wk, L1_attn_Wv, L1_attn_Wo, L1_ffn_U, L1_ffn_S,
           L1_ffn_V):
    x = np.ascontiguousarray(np.asarray(L0_attn_x, dtype=np.float32))
    progs = _progs()

    wT = np.ascontiguousarray(np.asarray(L0_attn_in_proj_w).T)
    convw_r = np.ascontiguousarray(
        np.asarray(L0_attn_conv_w)[:, 0, :].reshape(80, 128, 4).transpose(1, 0, 2).reshape(128, 320))
    convb_r = np.ascontiguousarray(np.asarray(L0_attn_conv_b).reshape(80, 128).T)
    dtb = np.asarray(L0_attn_dt_bias).reshape(128, 1)
    alog = np.asarray(L0_attn_A_log).reshape(128, 1)
    drow = np.asarray(L0_attn_D).reshape(1, 128)
    normw_r = np.ascontiguousarray(np.asarray(L0_attn_norm_w).reshape(64, 128).T)
    opT = np.ascontiguousarray(np.asarray(L0_attn_out_proj_w).T)
    wgT = np.ascontiguousarray(np.transpose(np.asarray(L0_ffn_w_gate), (0, 2, 1)))
    wuT = np.ascontiguousarray(np.transpose(np.asarray(L0_ffn_w_up), (0, 2, 1)))
    wdT = np.ascontiguousarray(np.transpose(np.asarray(L0_ffn_w_down), (0, 2, 1)))
    Wq = np.ascontiguousarray(np.asarray(L1_attn_Wq))
    Wk = np.ascontiguousarray(np.asarray(L1_attn_Wk))
    Wv = np.ascontiguousarray(np.asarray(L1_attn_Wv))
    Wo = np.ascontiguousarray(np.asarray(L1_attn_Wo))
    UT = np.ascontiguousarray(np.asarray(L1_ffn_U).T)
    S_r = np.ascontiguousarray(np.asarray(L1_ffn_S).reshape(8, 128).T)
    Vw = np.ascontiguousarray(np.asarray(L1_ffn_V))
    ident = np.eye(128, dtype=np.float32)

    def shard(c):
        bb, hc = c // 2, c % 2
        return np.ascontiguousarray(x[bb, hc * T:(hc + 1) * T])

    def halo(c):
        bb, hc = c // 2, c % 2
        if hc == 0:
            return np.zeros((3, D), np.float32)
        return np.ascontiguousarray(x[bb, T - 3:T])

    in_maps = []
    for c in range(NCORES):
        hc = c % 2
        dd = dict(x=shard(c), xhalo=halo(c), wT=wT, convw=convw_r, convb=convb_r,
                  dtb=dtb, Alog=alog, Drow=drow, normw=normw_r, opT=opT)
        dd.update(l0mamba_host_inputs(hc))
        in_maps.append(dd)
    res = run_bass_kernel_spmd(progs["m"], in_maps, core_ids=list(range(NCORES)))
    x1 = [res.results[c]["out"] for c in range(NCORES)]

    in_maps = [dict(x=x1[c], wgT=wgT, wuT=wuT, wdT=wdT, ident=ident)
               for c in range(NCORES)]
    res = run_bass_kernel_spmd(progs["f0"], in_maps, core_ids=list(range(NCORES)))
    x2 = [res.results[c]["out"] for c in range(NCORES)]

    in_maps = []
    for c in range(NCORES):
        hc = c % 2
        dd = dict(x=x2[c], Wq=Wq, Wk=Wk, Wv=Wv, Wo=Wo)
        dd.update(l1attn_host_inputs(hc))
        in_maps.append(dd)
    res = run_bass_kernel_spmd(progs["a"], in_maps, core_ids=list(range(NCORES)))
    x3 = [res.results[c]["out"] for c in range(NCORES)]

    in_maps = [dict(x=x3[c], V=Vw, UT=UT, S_r=S_r, ident=ident)
               for c in range(NCORES)]
    res = run_bass_kernel_spmd(progs["f1"], in_maps, core_ids=list(range(NCORES)))
    x4 = [res.results[c]["out"] for c in range(NCORES)]

    out = np.zeros((4, 4096, D), np.float32)
    for c in range(NCORES):
        bb, hc = c // 2, c % 2
        out[bb, hc * T:(hc + 1) * T] = x4[c]
    return out


# version 02 (final)
# speedup vs baseline: 1.1406x; 1.1406x over previous
import numpy as np
import ml_dtypes
import concourse.bass as bass
import concourse.mybir as mybir
from concourse.ap import AP
from concourse.bass_utils import run_bass_kernel_spmd
from contextlib import ExitStack

F32 = mybir.dt.float32
BF16 = mybir.dt.bfloat16
AF = mybir.ActivationFunctionType
ALU = mybir.AluOpType
AX = mybir.AxisListType.X

NCORES = 8
T = 2048
D = 4096
NT = T // 128
EPS = 1e-5
BF = ml_dtypes.bfloat16


class Sem:
    def __init__(self, h):
        self.h = h
        self.n = 0

    def tok(self):
        return (self.h, self.n)


class Bld:
    def __init__(self):
        self.nc = bass.Bass("TRN2", target_bir_lowering=False, debug=False)
        self.es = ExitStack()
        self._ns = 0

    def sem(self, name=None):
        self._ns += 1
        return Sem(self.es.enter_context(self.nc.semaphore(name or f"sem{self._ns}")))

    def sb(self, name, shape, dt):
        return self.es.enter_context(self.nc.sbuf_tensor(name, shape, dt))

    def scope(self):
        return ExitStack()

    def sbs(self, sc, name, shape, dt):
        return sc.enter_context(self.nc.sbuf_tensor(name, shape, dt))

    def pss(self, sc, name, shape, dt):
        return sc.enter_context(self.nc.psum_tensor(name, shape, dt))

    def ps(self, name, shape, dt):
        return self.es.enter_context(self.nc.psum_tensor(name, shape, dt))

    def dram(self, name, shape, dt, kind="Internal"):
        return self.nc.dram_tensor(name, shape, dt, kind=kind).ap()

    def inp(self, name, shape, dt=F32):
        return self.dram(name, shape, dt, kind="ExternalInput")

    def outp(self, name, shape, dt=F32):
        return self.dram(name, shape, dt, kind="ExternalOutput")


class Prog:
    ENGS = ("tensor", "vector", "scalar", "gpsimd", "sync")

    def __init__(self, bld):
        self.b = bld
        self.q = {e: [] for e in self.ENGS}
        self.serial = False
        self.serial_all = False
        self.serial_phases = ""
        self.all_sems = None

    def add(self, eng, fn, waits=(), inc=None, k=1):
        waits = tuple(w for w in waits if w is not None)
        if self.serial and self.all_sems:
            waits = waits + tuple(s.tok() for s in self.all_sems.values() if s.n > 0)
        tok = None
        semh = None
        if inc is not None:
            inc.n += k
            tok = (inc.h, inc.n)
            semh = inc.h
        self.q[eng].append((waits, fn, semh, k))
        return tok

    def finish(self):
        nc = self.b.nc
        with nc.Block() as block:
            for e in self.ENGS:
                ops = self.q[e]
                if not ops:
                    continue

                def mk(ops):
                    def run(eng):
                        for waits, fn, semh, k in ops:
                            for h, v in waits:
                                eng.wait_ge(h, v)
                            inst = fn(eng)
                            if semh is not None:
                                inst.then_inc(semh, k)
                    return run
                getattr(block, e)(mk(ops))


def emit_transpose_tile(p, sems, xrow, ident, x3T_ap_fn, banks, bank_free, q,
                        wait_x, nchunks=32, dt_src=F32):
    toks = []
    ngr = nchunks // 4
    for g in range(ngr):
        bank = banks[g % len(banks)]

        def tr(t, bank=bank, g=g, xrow=xrow, ident=ident):
            for j in range(4):
                c = g * 4 + j
                inst = t.transpose(bank[:, j * 128:(j + 1) * 128],
                                   xrow[:, c * 128:(c + 1) * 128], ident[:, :])
            return inst
        tt = p.add("tensor", tr, waits=(wait_x, bank_free.get(id(bank))), inc=sems["T"])

        def ev(v, bank=bank, g=g, q=q):
            return v.tensor_copy(x3T_ap_fn(g * 4, q),
                                 AP(bank, 0, [[512, 128], [128, 4], [1, 128]]))
        vt = p.add("vector", ev, waits=(tt,), inc=sems["V"])
        bank_free[id(bank)] = vt
        toks.append(vt)
    return toks


def emit_ln(p, sems, acc, sq, mcol, col0, tile_waits, out_dma=None, out_sem=None,
            extra=None):
    c = col0

    def red(v, acc=acc, c=c):
        return v.reduce_sum(mcol[:, c:c + 1], acc[:], axis=AX)
    t1 = p.add("vector", red, waits=tile_waits, inc=sems["V"])

    def mean(v, c=c):
        return v.tensor_scalar(mcol[:, c + 1:c + 2], mcol[:, c:c + 1], 1.0 / D, None,
                               ALU.mult)
    t1b = p.add("vector", mean, waits=(t1,), inc=sems["V"])

    def sub(v, acc=acc, c=c):
        return v.tensor_scalar(acc[:], acc[:], mcol[:, c + 1:c + 2], None, ALU.subtract)
    t2 = p.add("vector", sub, waits=(t1b,), inc=sems["V"])

    def sqr(s, acc=acc, c=c):
        return s.activation(sq[:], acc[:], AF.Square, accum_out=mcol[:, c + 2:c + 3])
    t3 = p.add("scalar", sqr, waits=(t2,), inc=sems["A"])

    def veps(v, c=c):
        return v.tensor_scalar(mcol[:, c + 3:c + 4], mcol[:, c + 2:c + 3],
                               1.0 / D, EPS, ALU.mult, ALU.add)
    t3b = p.add("vector", veps, waits=(t3,), inc=sems["V"])

    def rstd(s, c=c):
        s.activation(mcol[:, c + 3:c + 4], mcol[:, c + 3:c + 4], AF.Ln)
        return s.activation(mcol[:, c + 3:c + 4], mcol[:, c + 3:c + 4], AF.Exp,
                            scale=-0.5)
    t4 = p.add("scalar", rstd, waits=(t3b,), inc=sems["A"])

    def nrm(v, acc=acc, c=c):
        return v.tensor_scalar(acc[:], acc[:], mcol[:, c + 3:c + 4], None, ALU.mult)
    t5 = p.add("vector", nrm, waits=(t4,), inc=sems["V"])
    return t5


def emit_l1ffn(p, b, sems, x_in, V_in, UT_in, S_in, ident_sb, ident_tok, out_ap,
               start_tok=None, dbg_ap=None):
    nc = b.nc
    xvT = b.sb("f1_xvT", [128, 8 * T], BF16)
    Scol = b.sb("f1_Scol", [128, 8], F32)
    pb = [b.ps(f"f1_p{i}", [128, 512], F32) for i in range(6)]
    scA = b.scope()
    xrow = [b.sbs(scA, f"f1_xrow{i}", [128, D], F32) for i in range(2)]
    x3T = b.sbs(scA, "f1_x3T", [128, 32 * 512], BF16)
    Vb = b.sbs(scA, "f1_Vb", [128, 32 * 1024], BF16)
    dW, dX, dO = sems["dW"], sems["dX"], sems["dO"]
    sT, sV = sems["T"], sems["V"]
    def ld_s(g):
        return g.dma_start(out=Scol[:], in_=S_in)
    tok_S = p.add("gpsimd", ld_s, waits=(start_tok,), inc=dW, k=16)
    v_toks = []
    for i in range(4):
        def ld_v(g, i=i):
            return g.dma_start(
                out=Vb.ap().rearrange("p (k r) -> p k r", k=32)[:, i * 8:(i + 1) * 8, :],
                in_=V_in.rearrange("(k p) r -> p k r", p=128)[:, i * 8:(i + 1) * 8, :])
        v_toks.append(p.add("gpsimd", ld_v, inc=dW, k=16))
    tok_V = v_toks[-1]

    bank_free = {}
    xrow_free = {}
    xv_done = []
    for st in range(4):
        ev_toks_st = []
        for q in range(4):
            bi = (st * 4 + q) % 2

            def ldx(s, st=st, q=q, bi=bi):
                return s.dma_start(out=xrow[bi][:],
                                   in_=x_in[(st * 512 + q * 128):(st * 512 + q * 128 + 128), :])
            xtok = p.add("sync", ldx, waits=(xrow_free.get(bi), start_tok), inc=dX, k=16)

            toks = emit_transpose_tile(
                p, sems, xrow[bi], ident_sb,
                lambda c0, q, st=st: AP(x3T, c0 * 512 + q * 128,
                                        [[32 * 512, 128], [512, 4], [1, 128]]),
                pb[0:4], bank_free, q, wait_x=xtok,
            )
            xrow_free[bi] = sems["T"].tok()
            ev_toks_st.extend(toks)
        x3T_ready = ev_toks_st[-1]
        for rc in range(8):
            bank = pb[4 + rc % 2]

            def mmxv(t, bank=bank, rc=rc):
                for k in range(32):
                    inst = t.matmul(bank[:],
                                    Vb[:, k * 1024 + rc * 128: k * 1024 + rc * 128 + 128],
                                    x3T[:, k * 512:(k + 1) * 512],
                                    start=(k == 0), stop=(k == 31))
                return inst
            mt = p.add("tensor", mmxv,
                       waits=(x3T_ready, tok_V, bank_free.get(id(bank))), inc=sT)

            def evxv(v, bank=bank, rc=rc, st=st):
                return v.tensor_scalar(xvT[:, rc * T + st * 512: rc * T + st * 512 + 512],
                                       bank.ap(), Scol[:, rc:rc + 1], None, ALU.mult,
                                       accum_out=None)
            vt = p.add("vector", evxv, waits=(mt, tok_S), inc=sV)
            bank_free[id(bank)] = vt
            xv_done.append(vt)
    xv_all = xv_done[-1]
    if dbg_ap is not None:
        def dmp_sto(s):
            return s.dma_start(out=dbg_ap, in_=xvT[:])
        p.add("sync", dmp_sto, waits=(xv_all,), inc=sems["dO"], k=16)
    last_mm_A = sems["T"].tok()
    scA.close()

    scB = b.scope()
    UTb = b.sbs(scB, "f1_UTb", [128, 8 * D], BF16)
    rx = [b.sbs(scB, f"f1_rx{i}", [128, D], F32) for i in range(2)]
    acc = [b.sbs(scB, f"f1_acc{i}", [128, D], F32) for i in range(2)]
    sq = b.sbs(scB, "f1_sq", [128, D], BF16)
    mcol = b.sbs(scB, "f1_mcol", [128, 8], F32)

    ut_toks = []
    for i in range(4):
        def ld_u(g, i=i):
            return g.dma_start(
                out=UTb.ap().rearrange("p (rc o) -> p rc o", rc=8)[:, i * 2:(i + 1) * 2, :],
                in_=UT_in.rearrange("(rc p) o -> p rc o", p=128)[:, i * 2:(i + 1) * 2, :])
        ut_toks.append(p.add("gpsimd", ld_u, waits=(xv_all, last_mm_A), inc=dW, k=16))
    tok_UT = ut_toks[-1]

    rx_free = {}
    acc_free = {}
    last_store = None
    for tt in range(NT):
        bi = tt % 2

        def ldr(s, tt=tt, bi=bi):
            return s.dma_start(out=rx[bi][:], in_=x_in[tt * 128:(tt + 1) * 128, :])
        gate = (xv_all, last_mm_A) if tt < 2 else (rx_free.get(bi),)
        rtok = p.add("sync", ldr, waits=gate, inc=dX, k=16)

        ev8 = []
        for ot in range(8):
            bank = pb[ot % 4]

            def mmo(t, tt=tt, ot=ot, bank=bank):
                for rc in range(8):
                    inst = t.matmul(bank[:],
                                    xvT[:, rc * T + tt * 128: rc * T + tt * 128 + 128],
                                    UTb[:, rc * D + ot * 512: rc * D + ot * 512 + 512],
                                    start=(rc == 0), stop=(rc == 7))
                return inst
            mt = p.add("tensor", mmo,
                       waits=(xv_all, tok_UT, bank_free.get(id(bank))), inc=sT)

            def evo(v, tt=tt, ot=ot, bank=bank, bi=bi):
                return v.tensor_tensor(acc[bi][:, ot * 512:(ot + 1) * 512], bank.ap(),
                                       rx[bi][:, ot * 512:(ot + 1) * 512], ALU.add)
            vt = p.add("vector", evo, waits=(mt, rtok, acc_free.get(bi)), inc=sV)
            bank_free[id(bank)] = vt
            ev8.append(vt)
        rx_free[bi] = ev8[-1]

        lt = emit_ln(p, sems, acc[bi], sq, mcol, (bi % 2) * 4, tuple(ev8))

        def sto(s, tt=tt, bi=bi):
            return s.dma_start(out=out_ap[tt * 128:(tt + 1) * 128, :], in_=acc[bi][:])
        last_store = p.add("sync", sto, waits=(lt,), inc=dO, k=16)
        acc_free[bi] = last_store
    scB.close()
    return last_store


def build_l1ffn(debug=False, serial=False):
    b = Bld()
    p = Prog(b)
    p.serial = serial
    x_in = b.inp("x", [T, D])
    V_in = b.inp("V", [D, 1024])
    UT_in = b.inp("UT", [1024, D])
    S_in = b.inp("S_r", [128, 8])
    id_in = b.inp("ident", [128, 128])
    out_ap = b.outp("out", [T, D])

    sems = {k: b.sem(k) for k in ("dW", "dX", "dO", "T", "V", "A", "P")}
    p.all_sems = sems
    ident_sb = b.sb("ident_sb", [128, 128], F32)

    def ldi(s):
        return s.dma_start(out=ident_sb[:], in_=id_in)
    itok = p.add("sync", ldi, inc=sems["dX"], k=16)

    dbg_ap = b.outp("dbg_xvT", [128, 8 * T], BF16) if debug else None
    last = emit_l1ffn(p, b, sems, x_in, V_in, UT_in, S_in, ident_sb, itok, out_ap,
                      start_tok=itok, dbg_ap=dbg_ap)

    def fin(s, last=last):
        return s.nop()
    p.add("sync", fin, waits=(last,))
    p.finish()
    b.es.close()
    return b.nc


SCALE = float(1.0 / np.sqrt(128.0))


def emit_xpose_to_hbm(p, b, sems, x_in, xT_hbm, ident_sb, pbanks, start_tok,
                      pfx, ntok=T):
    scX = b.scope()
    xrow = [b.sbs(scX, f"{pfx}_xr{i}", [128, D], F32) for i in range(2)]
    xsup = b.sbs(scX, f"{pfx}_xsup", [128, 32 * 512], BF16)
    dX, dO = sems["dX"], sems["dO"]
    bank_free = {}
    xrow_free = {}
    out_tok = None
    for st in range(ntok // 512):
        ev_toks = []
        for q in range(4):
            bi = (st * 4 + q) % 2

            def ldx(s, st=st, q=q, bi=bi):
                return s.dma_start(out=xrow[bi][:],
                                   in_=x_in[(st * 512 + q * 128):(st * 512 + q * 128 + 128), :])
            xtok = p.add("sync", ldx, waits=(xrow_free.get(bi), start_tok), inc=dX, k=16)
            toks = emit_transpose_tile(
                p, sems, xrow[bi], ident_sb,
                lambda c0, q: AP(xsup, c0 * 512 + q * 128,
                                 [[32 * 512, 128], [512, 4], [1, 128]]),
                pbanks, bank_free, q, wait_x=(xtok if out_tok is None else xtok))
            xrow_free[bi] = sems["T"].tok()
            ev_toks.extend(toks)

        def sto(s, st=st):
            return s.dma_start(
                out=AP(xT_hbm.tensor, st * 512, [[32 * ntok, 128], [ntok, 32], [1, 512]]),
                in_=AP(xsup, 0, [[32 * 512, 128], [512, 32], [1, 512]]))
        out_tok = p.add("sync", sto, waits=(ev_toks[-1],), inc=dO, k=16)
        bank_free = {k: out_tok for k in bank_free}
    scX.close()
    return out_tok


def emit_l1attn(p, b, sems, x_in, Wq_in, Wk_in, Wv_in, Wo_in, cos_in, sinm_in,
                zcol_in, pad_in, dmask_in, prot_in, ident_sb, out_ap,
                replica_groups, start_tok=None, dbg=None, bf_weights=False):
    nc = b.nc
    dW, dX, dO = sems["dW"], sems["dX"], sems["dO"]
    sT, sV, sA = sems["T"], sems["V"], sems["A"]

    x2T_h = b.dram("a_x2T", [128, 32 * T], BF16)
    Q_h = b.dram("a_Q", [128, 32 * T], BF16)
    kvc = b.dram("a_kvc", [128, 32768], BF16)
    kvp = b.dram("a_kvp", [128, 32768], BF16)
    ctx_h = b.dram("a_ctx", [128, 32 * T], BF16)
    ao_h = b.dram("a_ao", [T, D], F32)

    onescol_b = b.sb("a_onescol", [128, 1], BF16)
    onesrow_b = b.sb("a_onesrow", [1, 512], BF16)
    ones1f = b.sb("a_ones1f", [1, 128], F32)
    Ib = b.sb("a_Ib", [128, 128], BF16)
    padb = b.sb("a_padb", [1, 4096], BF16)
    protb = b.sb("a_protb", [128, 128], BF16)
    dmaskb = b.sb("a_dmaskb", [128, 2048], BF16)
    zcol = b.sb("a_zcol", [128, 1], F32)
    cosb = b.sb("a_cos", [128, T], F32)
    sinmb = b.sb("a_sinm", [128, T], F32)
    negB = b.sb("a_negB", [1, 128], BF16)
    qn2m = b.sb("a_qn2m", [1, 128], F32)
    km2 = b.sb("a_km2", [1, 16], F32)
    skr = b.sb("a_skr", [1, 16], F32)
    echo_sb = b.sb("a_echo", [128, 16], BF16)
    pb = [b.ps(f"a_p{i}", [128, 512], F32) for i in range(8)]

    def ld_c1(g):
        g.memset(onescol_b[:], 1.0)
        g.memset(onesrow_b[:], 1.0)
        g.memset(ones1f[:], 1.0)
        g.tensor_copy(Ib[:], ident_sb[:, :])
        return g.dma_start(out=padb[:], in_=pad_in)

    p.add("gpsimd", ld_c1, waits=(start_tok,), inc=dW, k=16)

    def ld_c2(g):
        return g.dma_start(out=dmaskb[:], in_=dmask_in)
    p.add("gpsimd", ld_c2, inc=dW, k=16)

    def ld_c3(g):
        return g.dma_start(out=protb[:], in_=prot_in)
    ctok = p.add("gpsimd", ld_c3, inc=dW, k=16)

    def ld_z(s):
        return s.dma_start(out=zcol[:], in_=zcol_in)
    p.add("sync", ld_z, waits=(start_tok,), inc=dX, k=16)

    def ld_cos(s):
        return s.dma_start(out=cosb[:], in_=cos_in)
    p.add("sync", ld_cos, inc=dX, k=16)

    def ld_sin(s):
        return s.dma_start(out=sinmb[:], in_=sinm_in)
    ctok2 = p.add("sync", ld_sin, inc=dX, k=16)

    p.serial = "A" in getattr(p, 'serial_phases', '') or p.serial_all
    xT_done = emit_xpose_to_hbm(p, b, sems, x_in, x2T_h, ident_sb, pb[0:4],
                                start_tok, "a")

    p.serial = "B" in getattr(p, 'serial_phases', '') or p.serial_all
    kvo = b.dram("a_kvo", [128, 32768], BF16)
    scB = b.scope()
    Wkb = b.sbs(scB, "a_Wkb", [128, 32 * 1024], BF16)
    Wvb = b.sbs(scB, "a_Wvb", [128, 32 * 1024], BF16)
    xtile = [b.sbs(scB, f"a_xt{i}", [128, 512], BF16) for i in range(4)]
    rstage = [b.sbs(scB, f"a_rs{i}", [128, 512], F32) for i in range(2)]
    praw = b.sbs(scB, "a_praw", [128, 512], BF16)
    cstage = [b.sbs(scB, f"a_cs{i}", [128, 512], BF16) for i in range(2)]
    kst = [b.sbs(scB, f"a_kst{i}", [128, 512], BF16) for i in range(2)]

    wq_eng = "sync" if bf_weights else "gpsimd"
    wq_sem = dX if bf_weights else dW

    def echo(buf_tile, tok, eng="gpsimd", sem=None):
        def e(g, buf_tile=buf_tile):
            return g.dma_start(out=echo_sb[:], in_=buf_tile)
        return p.add(eng, e, waits=(tok,), inc=sem or dW, k=16)

    def ld_wk(g):
        return g.dma_start(
            out=Wkb.ap().rearrange("p (k o) -> p k o", k=32),
            in_=Wk_in.rearrange("(k p) o -> p k o", p=128))
    wk_tok = p.add(wq_eng, ld_wk, waits=(xT_done,), inc=wq_sem, k=16)
    wk_tok = echo(Wkb[:, 32 * 1024 - 16:], wk_tok, wq_eng, wq_sem)

    def ld_wv(g):
        return g.dma_start(
            out=Wvb.ap().rearrange("p (k o) -> p k o", k=32),
            in_=Wv_in.rearrange("(k p) o -> p k o", p=128))
    wv_tok = p.add(wq_eng, ld_wv, inc=wq_sem, k=16)
    wv_tok = echo(Wvb[:, 32 * 1024 - 16:], wv_tok, wq_eng, wq_sem)

    xt_free = {}
    reader = {}

    def stream_ktiles(nt, consume):
        base_serial = p.serial
        for k in range(32):
            bi = k % 4

            def ldt(s, k=k, nt=nt, bi=bi):
                return s.dma_start(
                    out=xtile[bi][:],
                    in_=AP(x2T_h.tensor, k * T + nt * 512, [[32 * T, 128], [1, 512]]))
            p.serial = base_serial or ("S" in p.serial_phases)
            tok = p.add("sync", ldt, waits=(xt_free.get(bi), xT_done), inc=dX, k=16)
            p.serial = base_serial or ("N" in p.serial_phases)
            consume(k, xtile[bi], tok)
            p.serial = base_serial
            xt_free[bi] = sems["T"].tok()

    def rope_evac(bank, nt, out_slice_ap_fn, extra_wait=()):
        if not isinstance(extra_wait, tuple):
            extra_wait = (extra_wait,)

        def rev1(v, bank=bank, nt=nt):
            cs = cosb[:, nt * 512:(nt + 1) * 512]
            v.tensor_tensor(rstage[0][:], bank.ap(), cs, ALU.mult)
            v.tensor_tensor(rstage[1][0:64, :],
                            AP(bank, 64 * 512, [[512, 64], [1, 512]]),
                            sinmb[0:64, nt * 512:(nt + 1) * 512], ALU.mult)
            return v.tensor_tensor(rstage[1][64:128, :],
                                   AP(bank, 0, [[512, 64], [1, 512]]),
                                   sinmb[64:128, nt * 512:(nt + 1) * 512], ALU.mult)
        t1 = p.add("vector", rev1, waits=extra_wait, inc=sV)

        def rev2(v):
            return v.tensor_tensor(out_slice_ap_fn(), rstage[0][:], rstage[1][:],
                                   ALU.add)
        return p.add("vector", rev2, waits=(t1,), inc=sV)

    p.serial = True
    kphase_gate = None
    kv_contrib_toks = []
    for nt in range(4):
        prev_ev = kphase_gate

        def consume_k(k, buf, tok, prev_ev=prev_ev):
            def mm(t, k=k, buf=buf):
                for koc in range(8):
                    inst = t.matmul(pb[koc][:],
                                    Wkb[:, k * 1024 + koc * 128: k * 1024 + koc * 128 + 128],
                                    buf[:], start=(k == 0), stop=(k == 31))
                return inst
            waits = [tok, wk_tok, ctok2]
            if k == 0 and prev_ev is not None:
                waits.append(prev_ev)
            p.add("tensor", mm, waits=tuple(waits), inc=sT)
        stream_ktiles(nt, consume_k)
        mm_tok = sems["T"].tok()
        for koc in range(8):
            ki = koc % 2
            dst = lambda ki=ki: kst[ki][:, :]
            et = rope_evac(pb[koc], nt, dst,
                           extra_wait=(mm_tok, reader.get(id(kst[ki]))))

            def down(s, koc=koc, nt=nt, ki=ki):
                return s.dma_start(
                    out=AP(kvo.tensor, koc * 2048 + nt * 512, [[32768, 128], [1, 512]]),
                    in_=kst[ki][:])
            dot = p.add("sync", down, waits=(et,), inc=dO, k=16)

            def contrib(v, ki=ki):
                return v.tensor_scalar(cstage[ki][:], kst[ki][:], zcol[:],
                                       None, ALU.mult)
            ct = p.add("vector", contrib,
                       waits=(et, reader.get(id(cstage[ki]))), inc=sV)

            def dcontrib(s, koc=koc, nt=nt, ki=ki):
                return s.dma_start(
                    out=AP(kvc.tensor, koc * 2048 + nt * 512, [[32768, 128], [1, 512]]),
                    in_=cstage[ki][:])
            dct = p.add("sync", dcontrib, waits=(ct,), inc=dO, k=16)
            reader[id(cstage[ki])] = dct
            reader[id(kst[ki])] = dct
            kv_contrib_toks.append(dct)
        kphase_gate = sems["V"].tok()

    p.serial = True
    vphase_gate = kphase_gate
    for nt in range(4):
        prev_ev = vphase_gate

        def consume_v(k, buf, tok, prev_ev=prev_ev):
            def mm(t, k=k, buf=buf):
                for tbq in range(4):
                    for ov in range(2):
                        inst = t.matmul(pb[tbq * 2 + ov][:],
                                        buf[:, tbq * 128:(tbq + 1) * 128],
                                        Wvb[:, k * 1024 + ov * 512: k * 1024 + ov * 512 + 512],
                                        start=(k == 0), stop=(k == 31))
                return inst
            waits = [tok, wv_tok]
            if k == 0:
                waits.append(prev_ev)
            p.add("tensor", mm, waits=tuple(waits), inc=sT)
        stream_ktiles(nt, consume_v)
        mm_tok = sems["T"].tok()
        for tbq in range(4):
            for ov in range(2):
                vb = tbq * 2 + ov
                kb = 16 + nt * 4 + tbq
                ki = vb % 2
                voff = 16384 + (kb - 16) * 1024 + ov * 512

                def vev(v, vb=vb, ki=ki):
                    return v.tensor_copy(kst[ki][:], pb[vb].ap())
                et = p.add("vector", vev,
                           waits=(mm_tok, reader.get(id(kst[ki]))), inc=sV)

                def vdo(s, voff=voff, ki=ki):
                    return s.dma_start(
                        out=AP(kvo.tensor, voff, [[32768, 128], [1, 512]]),
                        in_=kst[ki][:])
                dot = p.add("sync", vdo, waits=(et,), inc=dO, k=16)

                def vcontrib(v, ki=ki):
                    return v.tensor_scalar(cstage[ki][:], kst[ki][:],
                                           zcol[:], None, ALU.mult)
                ct = p.add("vector", vcontrib,
                           waits=(et, reader.get(id(cstage[ki]))), inc=sV)

                def vdc(s, voff=voff, ki=ki):
                    return s.dma_start(
                        out=AP(kvc.tensor, voff, [[32768, 128], [1, 512]]),
                        in_=cstage[ki][:])
                dt_ = p.add("sync", vdc, waits=(ct,), inc=dO, k=16)
                reader[id(cstage[ki])] = dt_
                reader[id(kst[ki])] = dt_
                kv_contrib_toks.append(dt_)
        vphase_gate = sems["V"].tok()

    scB.close()

    p.serial = True
    scB2 = b.scope()
    Wqs = b.sbs(scB2, "a_Wqs", [128, 32 * 512], BF16)
    qsq = b.sbs(scB2, "a_qsq", [128, 512], BF16)
    wq_prev = (vphase_gate, kv_contrib_toks[-1], sems["T"].tok())
    qphase_gate = vphase_gate
    prev_qmt = None
    prev_qxt = None
    prev_sqt = None
    for oset in range(8):
        def ld_wq(g, oset=oset):
            return g.dma_start(
                out=Wqs.ap().rearrange("p (k o) -> p k o", k=32),
                in_=Wq_in.rearrange("(k p) o -> p k o", p=128)[:, :, oset * 512:(oset + 1) * 512])
        wq_tok = p.add(wq_eng, ld_wq, waits=wq_prev, inc=wq_sem, k=16)
        wq_tok = echo(Wqs[:, 32 * 512 - 16:], wq_tok, wq_eng, wq_sem)
        for nt in range(4):
            prev_ev = qphase_gate

            def consume_q(k, buf, tok, wq_tok=wq_tok, prev_ev=prev_ev):
                def mm(t, k=k, buf=buf):
                    for j in range(4):
                        inst = t.matmul(pb[j][:],
                                        Wqs[:, k * 512 + j * 128: k * 512 + j * 128 + 128],
                                        buf[:], start=(k == 0), stop=(k == 31))
                    return inst
                waits = [tok, wq_tok]
                if k == 0:
                    waits.append(prev_ev)
                p.add("tensor", mm, waits=tuple(waits), inc=sT)
            stream_ktiles(nt, consume_q)
            mm_tok = sems["T"].tok()
            for j in range(4):
                h = oset * 4 + j
                cdst = lambda: cstage[0][:, :]
                et = rope_evac(pb[j], nt, cdst,
                               extra_wait=(mm_tok, reader.get(id(cstage[0])), prev_sqt))

                def qsto(s, h=h, nt=nt):
                    return s.dma_start(
                        out=AP(Q_h.tensor, h * T + nt * 512, [[32 * T, 128], [1, 512]]),
                        in_=cstage[0][:])
                qst = p.add("sync", qsto, waits=(et,), inc=dO, k=16)
                reader[id(cstage[0])] = qst

                def sqq(s_, j=j):
                    return s_.activation(qsq[:], cstage[0][:], AF.Square)
                sqt = p.add("scalar", sqq, waits=(et, prev_qmt), inc=sA)
                prev_sqt = sqt

                def qnmm(t, j=j):
                    return t.matmul(pb[6][0:1, :], onescol_b[:], qsq[:],
                                    start=True, stop=True)
                qmt = p.add("tensor", qnmm, waits=(sqt, prev_qxt), inc=sT)
                prev_qmt = qmt

                def qnmax(v, h=h, nt=nt):
                    return v.reduce_max(qn2m[0:1, h * 4 + nt: h * 4 + nt + 1],
                                        pb[6][0:1, :], axis=AX)
                qxt = p.add("vector", qnmax, waits=(qmt,), inc=sV)
                prev_qxt = qxt
            qphase_gate = sems["V"].tok()
        wq_prev = (sems["T"].tok(),)
    def coll(g):
        return g.collective_compute("AllReduce", ALU.add, replica_groups,
                                    [kvc.opt()], [kvp.opt()])
    coll_tok = p.add("gpsimd", coll, waits=(kv_contrib_toks[-1],), inc=sems["P"], k=1)
    qn_all = sems["V"].tok()
    q_all_t = sems["T"].tok()
    q_all_d = sems["dO"].tok()
    if dbg is not None:
        def dq(s):
            return s.dma_start(out=dbg["Q"], in_=Q_h)
        p.add("sync", dq, waits=(q_all_d,), inc=dO, k=16)
        def dko(s):
            return s.dma_start(out=dbg["kvo"], in_=kvo)
        p.add("sync", dko, waits=(q_all_d,), inc=dO, k=16)
        def dkp(s):
            return s.dma_start(out=dbg["kvp"], in_=kvp)
        p.add("sync", dkp, waits=(q_all_d, coll_tok), inc=dO, k=16)
    scB2.close()

    p.serial = "D" in getattr(p, 'serial_phases', '') or p.serial_all
    scD = b.scope()
    Kbuf = b.sbs(scD, "a_Kbuf", [128, 8 * 4096], BF16)
    Vbuf = b.sbs(scD, "a_Vbuf", [128, 32 * 1024], BF16)
    qt_b = [b.sbs(scD, f"a_qt{i}", [128, 512], BF16) for i in range(2)]
    prb = [b.sbs(scD, f"a_pr{i}", [128, 512], BF16) for i in range(2)]
    ksq = b.sbs(scD, "a_ksq", [128, 512], BF16)
    kn2r = b.sbs(scD, "a_kn2r", [1, 4096], F32)
    recr = b.sbs(scD, "a_recr", [1, 512], F32)
    ctxf = b.sbs(scD, "a_ctxf", [128, 512], F32)
    ctxs = [b.sbs(scD, f"a_ctxs{i}", [128, 512], BF16) for i in range(2)]
    qn2sq = b.sbs(scD, "a_qn2sq", [1, 128], F32)
    negBf = b.sbs(scD, "a_negBf", [1, 128], F32)

    dload_gate = (coll_tok, q_all_t, q_all_d, qn_all)

    def ld_pk(s):
        return s.dma_start(
            out=AP(Kbuf, 0, [[32768, 128], [4096, 8], [1, 2048]]),
            in_=AP(kvp.tensor, 0, [[32768, 128], [2048, 8], [1, 2048]]))
    ptok1 = p.add("sync", ld_pk, waits=dload_gate, inc=dX, k=16)

    def ld_ok(s):
        return s.dma_start(
            out=AP(Kbuf, 2048, [[32768, 128], [4096, 8], [1, 2048]]),
            in_=AP(kvo.tensor, 0, [[32768, 128], [2048, 8], [1, 2048]]))
    ptok2 = p.add("sync", ld_ok, waits=(ptok1,), inc=dX, k=16)

    def ld_pv(s):
        return s.dma_start(out=Vbuf[:, 0:16384],
                           in_=AP(kvp.tensor, 16384, [[32768, 128], [1, 16384]]))
    ptok3 = p.add("sync", ld_pv, waits=(ptok2,), inc=dX, k=16)

    def ld_ov(s):
        return s.dma_start(out=Vbuf[:, 16384:32768],
                           in_=AP(kvo.tensor, 16384, [[32768, 128], [1, 16384]]))
    ptok = p.add("sync", ld_ov, waits=(ptok3,), inc=dX, k=16)

    km_toks = []
    prev_knm = None
    prev_knev = None
    for g in range(8):
        evs = []
        for i in range(8):
            def sqk(s_, g=g, i=i):
                return s_.activation(ksq[:], Kbuf[:, g * 4096 + i * 512: g * 4096 + i * 512 + 512],
                                     AF.Square)
            st_ = p.add("scalar", sqk, waits=(ptok, prev_knm), inc=sA)

            def knm(t, g=g, i=i):
                return t.matmul(pb[7][0:1, :], onescol_b[:], ksq[:], start=True, stop=True)
            prev_knm = p.add("tensor", knm, waits=(st_, q_all_t, prev_knev), inc=sT)

            def knev(v, g=g, i=i):
                return v.tensor_copy(kn2r[0:1, i * 512:(i + 1) * 512], pb[7][0:1, :])
            prev_knev = p.add("vector", knev, waits=(prev_knm,), inc=sV)
            evs.append(prev_knev)

        def kmx(v, g=g):
            return v.reduce_max(km2[0:1, g:g + 1], kn2r[0:1, :], axis=AX)
        km_toks.append(p.add("vector", kmx, waits=(evs[-1],), inc=sV))

    def sk_sqrt(s_):
        return s_.activation(skr[0:1, 0:8], km2[0:1, 0:8], AF.Sqrt)
    skt = p.add("scalar", sk_sqrt, waits=(km_toks[-1],), inc=sA)

    def nb1(s_):
        return s_.activation(qn2sq[0:1, :], qn2m[0:1, :], AF.Sqrt)
    nbt1 = p.add("scalar", nb1, waits=(qn_all, skt), inc=sA)

    def nb2(v):
        return v.tensor_tensor(negBf[0:1, :], qn2sq[0:1, :],
                               AP(skr, 0, [[16, 1], [1, 8], [0, 16]]), ALU.mult)
    nbt2 = p.add("vector", nb2, waits=(nbt1,), inc=sV)

    def nb3(v):
        return v.tensor_scalar(negB[0:1, :], negBf[0:1, :], -SCALE * 1.01, None,
                               ALU.mult)
    nb_tok = p.add("vector", nb3, waits=(nbt2,), inc=sV)

    qt_free = {}
    probs_free = [None, None]
    sc_free = [None, None]
    sums_ctx_free = None
    last_ctx_dma = None
    d_start_gate = (nb_tok, skt, ptok, q_all_d, ctok)
    for h in range(32):
        g = h // 4
        for qt in range(4):
            kbs = 20 + 4 * qt
            bi_q = (h * 4 + qt) % 2

            def ldq(s, h=h, qt=qt, bi=bi_q):
                return s.dma_start(
                    out=qt_b[bi][:],
                    in_=AP(Q_h.tensor, h * T + qt * 512, [[32 * T, 128], [1, 512]]))
            qtok = p.add("sync", ldq, waits=(qt_free.get(bi_q), q_all_d), inc=dX, k=16)

            sums_av_toks = []
            for kb in range(kbs):
                sb_i = kb % 2
                bank = pb[sb_i]
                v = kb - 16 - 4 * qt

                def mmsc(t, h=h, qt=qt, kb=kb, g=g, v=v, bank=bank, bi_q=bi_q):
                    steps = ["kq", "shift"]
                    if kb < 16:
                        steps.append("pad")
                    if v >= 0:
                        steps.append("diag")
                    inst = None
                    for i, stp in enumerate(steps):
                        st_f, sp_f = (i == 0), (i == len(steps) - 1)
                        if stp == "kq":
                            inst = t.matmul(bank[:],
                                            Kbuf[:, g * 4096 + kb * 128: g * 4096 + kb * 128 + 128],
                                            qt_b[bi_q][:], start=st_f, stop=sp_f)
                        elif stp == "shift":
                            inst = t.matmul(bank[:], onesrow_b[0:1, 0:128],
                                            AP(negB, h * 4 + qt, [[128, 1], [0, 512]]),
                                            start=st_f, stop=sp_f)
                        elif stp == "pad":
                            inst = t.matmul(bank[:], AP(padb, kb * 128, [[4096, 1], [1, 128]]),
                                            onesrow_b[0:1, :], start=st_f, stop=sp_f)
                        else:
                            inst = t.matmul(bank[:], Ib[:, :],
                                            dmaskb[:, v * 512:(v + 1) * 512],
                                            start=st_f, stop=sp_f)
                    return inst
                mwaits = [qtok, sc_free[sb_i]]
                if kb == 0:
                    mwaits += list(d_start_gate)
                mt = p.add("tensor", mmsc, waits=tuple(mwaits), inc=sT)

                def expk(s_, sb_i=sb_i):
                    return s_.activation(prb[sb_i][:], pb[sb_i].ap(), AF.Exp, scale=SCALE)
                et = p.add("scalar", expk, waits=(mt, probs_free[sb_i]), inc=sA)
                sc_free[sb_i] = et

                def mmsums(t, sb_i=sb_i, kb=kb, g=g, kbs=kbs):
                    t.matmul(pb[2][0:1, :], onescol_b[:], prb[sb_i][:],
                             start=(kb == 0), stop=(kb == kbs - 1))
                    return t.matmul(pb[3][:], Vbuf[:, kb * 1024 + g * 128: kb * 1024 + g * 128 + 128],
                                    prb[sb_i][:], start=(kb == 0), stop=(kb == kbs - 1))
                savw = [et]
                if kb == 0:
                    savw.append(sums_ctx_free)
                savt = p.add("tensor", mmsums, waits=tuple(savw), inc=sT)
                probs_free[sb_i] = savt
                sums_av_toks.append(savt)
            qt_free[bi_q] = sums_av_toks[-1]

            def recip(ve, h=h):
                return ve.reciprocal(recr[0:1, :], pb[2][0:1, :])
            rct = p.add("vector", recip, waits=(sums_av_toks[-1],), inc=sV)

            def mmrb(t):
                return t.matmul(pb[4][:], ones1f[0:1, 0:128], recr[0:1, :],
                                start=True, stop=True)
            rbt = p.add("tensor", mmrb, waits=(rct,), inc=sT)

            def ctxev(s_):
                return s_.activation(ctxf[:], pb[3].ap(), AF.Copy)
            cet = p.add("scalar", ctxev, waits=(sums_av_toks[-1], last_ctx_dma), inc=sA)

            bi_c = (h * 4 + qt) % 2

            def ctxnrm(ve, bi_c=bi_c):
                return ve.tensor_tensor(ctxs[bi_c][:], ctxf[:], pb[4].ap(), ALU.mult)
            cnt = p.add("vector", ctxnrm,
                        waits=(rbt, cet, qt_free.get(('c', bi_c))), inc=sV)
            sums_ctx_free = cnt

            def ctxdma(s, h=h, qt=qt, bi_c=bi_c):
                return s.dma_start(
                    out=AP(ctx_h.tensor, h * T + qt * 512, [[32 * T, 128], [1, 512]]),
                    in_=ctxs[bi_c][:])
            last_ctx_dma = p.add("sync", ctxdma, waits=(cnt,), inc=dO, k=16)
            qt_free[('c', bi_c)] = last_ctx_dma
    if dbg is not None:
        def dcx(s):
            return s.dma_start(out=dbg["ctx"], in_=ctx_h)
        p.add("sync", dcx, waits=(last_ctx_dma,), inc=dO, k=16)
    d_done_t = sems["T"].tok()
    d_done_d = sems["dO"].tok()
    d_done_v = sems["V"].tok()
    scD.close()

    p.serial = "E" in getattr(p, 'serial_phases', '') or p.serial_all
    scE = b.scope()
    Wob = b.sbs(scE, "a_Wob", [128, 32 * 2048], BF16)
    ctt = [b.sbs(scE, f"a_ctt{i}", [128, 32 * 128], BF16) for i in range(2)]
    aost = [b.sbs(scE, f"a_aost{i}", [128, 2048], F32) for i in range(2)]
    e_gate = (d_done_t, d_done_d, d_done_v)
    ao_toks = []
    ctt_free = {}
    aost_free = {}
    wo_prev = None
    ebank_free = {}
    for oh in range(2):
        wo_toks = []
        for i in range(4):
            def ld_wo(g_, oh=oh, i=i):
                return g_.dma_start(
                    out=Wob.ap().rearrange("p (k o) -> p k o", k=32)[:, i * 8:(i + 1) * 8, :],
                    in_=Wo_in.rearrange("(k p) o -> p k o", p=128)[:, i * 8:(i + 1) * 8,
                                                                  oh * 2048:(oh + 1) * 2048])
            wo_toks.append(p.add("gpsimd", ld_wo,
                                 waits=e_gate + ((wo_prev,) if wo_prev else ()),
                                 inc=dW, k=16))
        wo_tok = wo_toks[-1]
        for tt in range(NT):
            bi = tt % 2

            def ldc(s, tt=tt, bi=bi):
                return s.dma_start(
                    out=ctt[bi][:],
                    in_=AP(ctx_h.tensor, tt * 128, [[32 * T, 128], [T, 32], [1, 128]]))
            ct = p.add("sync", ldc, waits=(ctt_free.get(bi),) + e_gate, inc=dX, k=16)

            ev_toks = []
            for otq in range(4):
                bank = pb[otq]

                def mmwo(t, tt=tt, otq=otq, bank=bank, bi=bi):
                    for hh in range(32):
                        inst = t.matmul(bank[:], ctt[bi][:, hh * 128:(hh + 1) * 128],
                                        Wob[:, hh * 2048 + otq * 512: hh * 2048 + otq * 512 + 512],
                                        start=(hh == 0), stop=(hh == 31))
                    return inst
                ttg = oh * NT + tt
                mt = p.add("tensor", mmwo, waits=(ct, wo_tok, ebank_free.get((ttg, otq))),
                           inc=sT)

                def aoev(s_, otq=otq, bank=bank, bi=bi):
                    return s_.activation(aost[bi][:, otq * 512:(otq + 1) * 512], bank.ap(),
                                         AF.Copy)
                evt = p.add("scalar", aoev, waits=(mt, aost_free.get(bi)), inc=sA)
                ebank_free[(ttg + 2, otq)] = evt
                ev_toks.append(evt)
            ctt_free[bi] = ev_toks[-1]

            def aodma(s, tt=tt, oh=oh, bi=bi):
                return s.dma_start(
                    out=AP(ao_h.tensor, tt * 128 * D + oh * 2048, [[D, 128], [1, 2048]]),
                    in_=aost[bi][:])
            adt = p.add("sync", aodma, waits=(ev_toks[-1],), inc=dO, k=16)
            aost_free[bi] = adt
            ao_toks.append(adt)
        wo_prev = sems["T"].tok()
    e_done = ao_toks[-1]
    scE.close()

    p.serial = "F" in getattr(p, 'serial_phases', '') or p.serial_all
    scF = b.scope()
    aot = [b.sbs(scF, f"a_aot{i}", [128, D], F32) for i in range(2)]
    rxf = [b.sbs(scF, f"a_rxf{i}", [128, D], F32) for i in range(2)]
    accf = [b.sbs(scF, f"a_accf{i}", [128, D], F32) for i in range(2)]
    sqf = b.sbs(scF, "a_sqf", [128, D], BF16)
    mcolf = b.sbs(scF, "a_mcolf", [128, 8], F32)
    last_store = None
    aot_free = {}
    accf_free = {}
    for tt in range(NT):
        bi = tt % 2

        def ldao(s, tt=tt, bi=bi):
            return s.dma_start(out=aot[bi][:], in_=ao_h[tt * 128:(tt + 1) * 128, :])
        at = p.add("sync", ldao, waits=(aot_free.get(bi), e_done), inc=dX, k=16)

        def ldrx(s, tt=tt, bi=bi):
            return s.dma_start(out=rxf[bi][:], in_=x_in[tt * 128:(tt + 1) * 128, :])
        rt = p.add("sync", ldrx, waits=(aot_free.get(('r', bi)), e_done), inc=dX, k=16)

        def addr(v, bi=bi):
            return v.tensor_tensor(accf[bi][:], aot[bi][:], rxf[bi][:], ALU.add)
        adt = p.add("vector", addr, waits=(at, rt, accf_free.get(bi)), inc=sV)
        aot_free[bi] = adt
        aot_free[('r', bi)] = adt

        lt = emit_ln(p, sems, accf[bi], sqf, mcolf, bi * 4, (adt,))

        def sto(s, tt=tt, bi=bi):
            return s.dma_start(out=out_ap[tt * 128:(tt + 1) * 128, :], in_=accf[bi][:])
        last_store = p.add("sync", sto, waits=(lt,), inc=dO, k=16)
        accf_free[bi] = last_store
    scF.close()
    return last_store


def build_l1attn(replica_groups, serial=False, serial_phases="", debug=False,
                 bf_weights=False):
    b = Bld()
    p = Prog(b)
    p.serial = serial
    p.serial_all = serial
    p.serial_phases = serial_phases
    x_in = b.inp("x", [T, D])
    wdt = BF16 if bf_weights else F32
    Wq_in = b.inp("Wq", [D, D], wdt)
    Wk_in = b.inp("Wk", [D, 1024], wdt)
    Wv_in = b.inp("Wv", [D, 1024], wdt)
    Wo_in = b.inp("Wo", [D, D])
    cos_in = b.inp("cos", [128, T])
    sinm_in = b.inp("sinm", [128, T])
    zcol_in = b.inp("zcol", [128, 1])
    pad_in = b.inp("pad", [1, 4096])
    dmask_in = b.inp("dmask", [128, 2048])
    prot_in = b.inp("prot", [128, 128], BF16)
    id_in = b.inp("ident", [128, 128])
    out_ap = b.outp("out", [T, D])

    sems = {k: b.sem(k) for k in ("dW", "dX", "dO", "T", "V", "A", "P")}
    p.all_sems = sems
    ident_sb = b.sb("ident_sb", [128, 128], F32)

    def ldi(s):
        return s.dma_start(out=ident_sb[:], in_=id_in)
    itok = p.add("sync", ldi, inc=sems["dX"], k=16)

    dbg = None
    if debug:
        dbg = {k: b.outp("dbg_" + k, [128, 32 * T], BF16) for k in ("Q", "ctx")}
        dbg["kvo"] = b.outp("dbg_kvo", [128, 32768], BF16)
        dbg["kvp"] = b.outp("dbg_kvp", [128, 32768], BF16)
    last = emit_l1attn(p, b, sems, x_in, Wq_in, Wk_in, Wv_in, Wo_in, cos_in,
                       sinm_in, zcol_in, pad_in, dmask_in, prot_in, ident_sb,
                       out_ap, replica_groups, start_tok=itok, dbg=dbg,
                       bf_weights=bf_weights)

    def fin(s, last=last):
        return s.nop()
    p.add("sync", fin, waits=(last,))
    p.finish()
    b.es.close()
    return b.nc


def l1attn_host_inputs(hc):
    pos = np.arange(hc * T, (hc + 1) * T, dtype=np.float64)
    invf = 1.0 / (10000.0 ** (np.arange(0, 128, 2, dtype=np.float64) / 128.0))
    ang = pos[None, :] * invf[:, None]
    cos = np.concatenate([np.cos(ang), np.cos(ang)], 0).astype(np.float32)
    sin = np.sin(ang)
    sinm = np.concatenate([-sin, sin], 0).astype(np.float32)
    zcol = np.full((128, 1), 1.0 if hc == 0 else 0.0, np.float32)
    pad = np.zeros((1, 4096), np.float32)
    if hc == 0:
        pad[0, :2048] = -1e30
    dmask = np.zeros((128, 2048), np.float32)
    for v in range(4):
        kl = np.arange(128)[:, None]
        qf = np.arange(512)[None, :]
        dmask[:, v * 512:(v + 1) * 512] = np.where(v * 128 + kl > qf, -1e30, 0.0)
    ident = np.eye(128, dtype=np.float32)
    prot = np.zeros((128, 128), np.float32)
    for j in range(128):
        prot[(j + 64) % 128, j] = -1.0 if j < 64 else 1.0
    return dict(cos=cos, sinm=sinm, zcol=zcol, pad=pad,
                dmask=dmask.astype(np.float32), ident=ident,
                prot=prot.astype(ml_dtypes.bfloat16))


_TRIPLES = [(1, 2, 3), (1, 4, 5), (1, 7, 6), (2, 4, 6), (2, 5, 7), (3, 4, 7), (3, 6, 5)]


def _oct_struct():
    Smat = np.zeros((8, 8, 8), np.float32)
    Smat[0, 0, 0] = 1.0
    for i in range(1, 8):
        Smat[0, i, i] = 1.0
        Smat[i, 0, i] = 1.0
        Smat[i, i, 0] = -1.0
    for a2, b2, c2 in _TRIPLES:
        for xx, yy, zz in ((a2, b2, c2), (b2, c2, a2), (c2, a2, b2)):
            Smat[xx, yy, zz] = 1.0
            Smat[yy, xx, zz] = -1.0
    return Smat


S_OCT = _oct_struct()
OCT_I = np.zeros((8, 8), np.int64)
OCT_SGN = np.zeros((8, 8), np.float32)
for _j in range(8):
    for _k in range(8):
        _i = int(np.nonzero(S_OCT[_j, :, _k])[0][0])
        OCT_I[_j, _k] = _i
        OCT_SGN[_j, _k] = S_OCT[_j, _i, _k]

HID8 = 1366
HCH = [128] * 10 + [106]
HCH = [128] * 10 + [86]
NHC = 11


def emit_quantize(p, b, sems, W_in, nj_rows, ncols, M_dram, pfx, start_tok):
    dW, dX, dO = sems["dW"], sems["dX"], sems["dO"]
    sT, sV, sA = sems["T"], sems["V"], sems["A"]
    old_serial = p.serial
    p.serial = True
    sc = b.scope()
    wt = [b.sbs(sc, f"{pfx}_wt{i}", [128, ncols], F32) for i in range(2)]
    stage = b.sbs(sc, f"{pfx}_st", [128, 8 * ncols], BF16)
    qv = b.sbs(sc, f"{pfx}_qv", [128, ncols], F32)
    q2 = b.sbs(sc, f"{pfx}_q2", [128, ncols], F32)
    cols = b.sbs(sc, f"{pfx}_cols", [128, 8], F32)
    onesc = b.sbs(sc, f"{pfx}_ones", [128, 1], F32)
    onesr = b.sbs(sc, f"{pfx}_onesr", [1, 128], F32)
    sc1 = b.sbs(sc, f"{pfx}_sc1", [1, 4], F32)
    pbq = b.pss(sc, f"{pfx}_pq", [128, 512], F32)

    if nj_rows == 512:
        rts = [(r * 128, 128) for r in range(4)]
    else:
        rts = []
        off = 0
        for sz in HCH:
            rts.append((off, sz))
            off += sz
    tiles = [(j, off, sz) for j in range(8) for (off, sz) in rts]

    def mset(g):
        g.memset(onesc[:], 1.0)
        g.memset(onesr[:], 1.0)
        return g.memset(cols[:], 0.0)
    t0 = p.add("gpsimd", mset, waits=(start_tok,), inc=sems["P"])

    for idx, (j, off, sz) in enumerate(tiles):
        bi = idx % 2
        if sz < 128:
            def zt(v, bi=bi):
                return v.memset(wt[bi][:, :], 0.0)
            p.add("vector", zt, waits=(), inc=sV)

        def ld(s, j=j, off=off, sz=sz, bi=bi):
            return s.dma_start(out=wt[bi][0:sz, :],
                               in_=W_in[j, off:off + sz, :])
        lt = p.add("sync", ld, waits=(t0,), inc=dX, k=16)

        def red(v, bi=bi, sz=sz):
            return v.reduce_sum(cols[:, 1:2], wt[bi][:, :], axis=AX,
                                apply_absolute_value=True)
        rt_ = p.add("vector", red, waits=(lt,), inc=sV)

        def acc(v):
            return v.tensor_tensor(cols[:, 0:1], cols[:, 0:1], cols[:, 1:2], ALU.add)
        p.add("vector", acc, waits=(rt_,), inc=sV)

    def tot(t):
        return t.matmul(pbq[0:1, 0:1], cols[:, 0:1], onesc[:], start=True, stop=True)
    tt_ = p.add("tensor", tot, waits=(), inc=sT)

    n_el = 8 * nj_rows * ncols

    def mkscale1(v, n_el=n_el):
        return v.tensor_scalar(sc1[0:1, 0:1], pbq[0:1, 0:1], 1.0 / n_el, None,
                               ALU.mult)
    sta = p.add("vector", mkscale1, waits=(tt_,), inc=sV)

    def mkscale2(v):
        return v.tensor_scalar(sc1[0:1, 1:2], sc1[0:1, 0:1], 0.5, None, ALU.mult)
    st_ = p.add("vector", mkscale2, waits=(sta,), inc=sV)

    def bc(t):
        t.matmul(pbq[:, 0:1], onesr[0:1, :], sc1[0:1, 0:1], start=True, stop=True)
        return t.matmul(pbq[:, 1:2], onesr[0:1, :], sc1[0:1, 1:2], start=True,
                        stop=True)
    bt = p.add("tensor", bc, waits=(st_,), inc=sT)

    def cpcols1(v):
        v.tensor_copy(cols[:, 2:3], pbq[:, 0:1])
        return v.tensor_copy(cols[:, 3:4], pbq[:, 1:2])
    cta = p.add("vector", cpcols1, waits=(bt,), inc=sV)

    def cpcols2(v):
        return v.tensor_scalar(cols[:, 4:5], pbq[:, 1:2], -1.0, None, ALU.mult)
    ct = p.add("vector", cpcols2, waits=(cta,), inc=sV)

    last = None
    for idx, (j, off, sz) in enumerate(tiles):
        bi = idx % 2

        def ld(s, j=j, off=off, sz=sz, bi=bi):
            return s.dma_start(out=wt[bi][0:sz, :],
                               in_=W_in[j, off:off + sz, :])
        lt = p.add("sync", ld, waits=(ct,), inc=dX, k=16)

        def quant1(v, bi=bi):
            v.tensor_scalar(qv[:], wt[bi][:], cols[:, 3:4], None, ALU.is_gt)
            return v.tensor_scalar(q2[:], wt[bi][:], cols[:, 4:5], None, ALU.is_lt)
        qta = p.add("vector", quant1, waits=(lt,), inc=sV)

        def quant2(v):
            v.tensor_tensor(qv[:], qv[:], q2[:], ALU.subtract)
            return v.tensor_scalar(qv[:], qv[:], cols[:, 2:3], None, ALU.mult)
        qt = p.add("vector", quant2, waits=(qta,), inc=sV)

        def mkstage(v, j=j):
            inst = None
            for k in range(8):
                sgn = float(OCT_SGN[j, k])
                inst = v.tensor_scalar(stage[:, k * ncols:(k + 1) * ncols], qv[:],
                                       sgn, None, ALU.mult)
            return inst
        mt = p.add("vector", mkstage, waits=(qt,), inc=sV)

        def sto(s, j=j, off=off, sz=sz):
            return s.dma_start(
                out=AP(M_dram.tensor, (j * nj_rows + off) * ncols,
                       [[ncols, sz], [8 * nj_rows * ncols, 8], [1, ncols]]),
                in_=AP(stage, 0, [[8 * ncols, sz], [ncols, 8], [1, ncols]]))
        last = p.add("sync", sto, waits=(mt,), inc=dO, k=16)
    sc.close()
    p.serial = old_serial
    return last


def emit_l0ffn(p, b, sems, x_in, wgT_in, wuT_in, wdT_in, ident_sb, out_ap,
               start_tok=None, dbg=None):
    nc = b.nc
    dW, dX, dO = sems["dW"], sems["dX"], sems["dO"]
    sT, sV, sA = sems["T"], sems["V"], sems["A"]

    Mg = b.dram("f0_Mg", [8 * 4096, HID8], BF16).tensor.ap() if False else b.dram("f0_Mg", [8 * 4096 * HID8], BF16)
    Mu = b.dram("f0_Mu", [8 * 4096 * HID8], BF16)
    Md = b.dram("f0_Md", [8 * 8 * HID8 * 512], BF16)
    g_h = b.dram("f0_g", [11264, T], BF16)
    h_h = b.dram("f0_h", [11264, T], BF16)
    dff_h = b.dram("f0_dff", [T, D], F32)

    qg = emit_quantize(p, b, sems, wgT_in, 512, HID8, Mg, "qg", start_tok)
    qu = emit_quantize(p, b, sems, wuT_in, 512, HID8, Mu, "qu", qg)
    qd = emit_quantize(p, b, sems, wdT_in, HID8, 512, Md, "qd", qu)

    ln_toks = []
    dff_done = []
    half_gate = None
    for hf in range(2):
        TH = 1024
        scX = b.scope()
        x1T = b.sbs(scX, f"f0_x1T{hf}", [128, 32 * TH], BF16)
        xrow = [b.sbs(scX, f"f0_xr{hf}_{i}", [128, D], F32) for i in range(2)]
        Mkb = b.sbs(scX, f"f0_Mkb{hf}", [128, 32 * HID8], BF16)
        gst = [b.sbs(scX, f"f0_gst{hf}_{i}", [128, 512], BF16) for i in range(2)]
        sg = b.sbs(scX, f"f0_sg{hf}", [128, 512], F32)
        pb = [b.pss(scX, f"f0_p{hf}_{i}", [128, 512], F32) for i in range(8)]

        bank_free = {}
        xrow_free = {}
        ev_last = None
        for st in range(2):
            for q in range(4):
                bi = (st * 4 + q) % 2
                r0 = hf * TH + st * 512 + q * 128

                def ldx(s, r0=r0, bi=bi):
                    return s.dma_start(out=xrow[bi][:], in_=x_in[r0:r0 + 128, :])
                xtok = p.add("sync", ldx,
                             waits=(xrow_free.get(bi), qd, start_tok, half_gate),
                             inc=dX, k=16)
                toks = emit_transpose_tile(
                    p, sems, xrow[bi], ident_sb,
                    lambda c0, q, st=st: AP(x1T, c0 * TH + st * 512 + q * 128,
                                            [[32 * TH, 128], [TH, 4], [1, 128]]),
                    pb[0:4], bank_free, q, wait_x=xtok)
                xrow_free[bi] = sems["T"].tok()
                ev_last = toks[-1]
        xT_ready = ev_last

        mk_prev = None
        for phase, M_dram, qtok in (("g", Mg, qg), ("u", Mu, qu)):
            for k in range(8):
                def ldm(g2, k=k, M_dram=M_dram):
                    return g2.dma_start(
                        out=AP(Mkb, 0, [[32 * HID8, 128], [HID8, 32], [1, HID8]]),
                        in_=AP(M_dram.tensor, k * 4096 * HID8,
                               [[HID8, 128], [128 * HID8, 32], [1, HID8]]))
                mtok = p.add("gpsimd", ldm,
                             waits=(qtok, xT_ready, mk_prev), inc=dW, k=16)

                ocs = [(i * 128, sz) for i, sz in enumerate(HCH)]
                for oi, (ooff, osz) in enumerate(ocs):
                    for nt in range(2):
                        bank = pb[(oi * 2 + nt) % 4]

                        def mm(t, k=k, ooff=ooff, osz=osz, nt=nt, bank=bank):
                            for c in range(32):
                                j, dc = c // 4, c % 4
                                srcc = int(OCT_I[j, k]) * 4 + dc
                                inst = t.matmul(
                                    bank[0:osz, :],
                                    Mkb[:, (j * 4 + dc) * HID8 + ooff:
                                        (j * 4 + dc) * HID8 + ooff + osz],
                                    x1T[:, srcc * TH + nt * 512: srcc * TH + nt * 512 + 512],
                                    start=(c == 0), stop=(c == 31))
                            return inst
                        mmt = p.add("tensor", mm,
                                    waits=(mtok, bank_free.get(id(bank))), inc=sT)

                        row0 = (k * NHC + oi) * 128
                        if phase == "g":
                            def evg(v, bank=bank, osz=osz, bi=(oi * 2 + nt) % 2):
                                return v.tensor_copy(gst[bi][0:osz, :], bank[0:osz, :])
                            et = p.add("vector", evg,
                                       waits=(mmt, bank_free.get(('g', (oi * 2 + nt) % 2))),
                                       inc=sV)

                            def stg(s, row0=row0, osz=osz, nt=nt, hf=hf,
                                    bi=(oi * 2 + nt) % 2):
                                return s.dma_start(
                                    out=AP(g_h.tensor, row0 * T + hf * TH + nt * 512,
                                           [[T, osz], [1, 512]]),
                                    in_=gst[bi][0:osz, :])
                            dt_ = p.add("sync", stg, waits=(et,), inc=dO, k=16)
                            bank_free[id(bank)] = et
                            bank_free[('g', (oi * 2 + nt) % 2)] = dt_
                        else:
                            bi = (oi * 2 + nt) % 2

                            def ldg(s, row0=row0, osz=osz, nt=nt, hf=hf, bi=bi):
                                return s.dma_start(
                                    out=gst[bi][0:osz, :],
                                    in_=AP(g_h.tensor, row0 * T + hf * TH + nt * 512,
                                           [[T, osz], [1, 512]]))
                            gtok = p.add("sync", ldg,
                                         waits=(bank_free.get(('u', bi)),), inc=dX, k=16)

                            def sil(s_, osz=osz, bi=bi):
                                return s_.activation(sg[0:osz, :], gst[bi][0:osz, :],
                                                     AF.Silu)
                            stok = p.add("scalar", sil, waits=(gtok,), inc=sA)

                            def evh(v, bank=bank, osz=osz, bi=bi):
                                return v.tensor_tensor(gst[bi][0:osz, :], sg[0:osz, :],
                                                       bank[0:osz, :], ALU.mult)
                            et = p.add("vector", evh, waits=(mmt, stok), inc=sV)

                            def sth(s, row0=row0, osz=osz, nt=nt, hf=hf, bi=bi):
                                return s.dma_start(
                                    out=AP(h_h.tensor, row0 * T + hf * TH + nt * 512,
                                           [[T, osz], [1, 512]]),
                                    in_=gst[bi][0:osz, :])
                            dt_ = p.add("sync", sth, waits=(et,), inc=dO, k=16)
                            bank_free[id(bank)] = et
                            bank_free[('u', bi)] = dt_
                mk_prev = sems["T"].tok()
            if phase == "g":
                g_done = sems["dO"].tok()
            else:
                h_done = sems["dO"].tok()
        scX.close()

        scD2 = b.scope()
        hb = b.sbs(scD2, f"f0_hb{hf}", [128, 88 * TH], BF16)
        mdt = [b.sbs(scD2, f"f0_mdt{hf}_{i}", [128, 512], BF16) for i in range(2)]
        dst_ = [b.sbs(scD2, f"f0_dst{hf}_{i}", [128, 512], F32) for i in range(2)]
        pb2 = [b.pss(scD2, f"f0_p2{hf}_{i}", [128, 512], F32) for i in range(8)]

        hrows = []
        for j in range(8):
            off = 0
            for sz in HCH:
                hrows.append((j, off, sz))
                off += sz

        def ldh(s, hf=hf):
            return s.dma_start(
                out=AP(hb, 0, [[88 * TH, 128], [TH, 88], [1, TH]]),
                in_=AP(h_h.tensor, hf * TH, [[T, 128], [128 * T, 88], [1, TH]]))
        htok = p.add("sync", ldh, waits=(h_done,), inc=dX, k=16)

        down_done = []
        md_free = {}
        bank2_free = {}
        for k in range(8):
            mm_toks = []
            for ci, (j, off, sz) in enumerate(hrows):
                bi = ci % 2
                mrow0 = (k * 8 + j) * HID8 + off

                def ldmd(s, mrow0=mrow0, sz=sz, bi=bi):
                    return s.dma_start(
                        out=mdt[bi][0:sz, :],
                        in_=AP(Md.tensor, mrow0 * 512, [[512, sz], [1, 512]]))
                mt_ = p.add("sync", ldmd, waits=(md_free.get(bi), htok), inc=dX, k=16)

                hcg = int(OCT_I[j, k]) * NHC + (off // 128)

                def mm(t, hcg=hcg, sz=sz, bi=bi, ci=ci):
                    for tc in range(8):
                        inst = t.matmul(
                            pb2[tc][:],
                            hb[0:sz, hcg * TH + tc * 128: hcg * TH + tc * 128 + 128],
                            mdt[bi][0:sz, :],
                            start=(ci == 0), stop=(ci == 87))
                    return inst
                ws = [mt_]
                if ci == 0:
                    ws += [bank2_free.get(tc) for tc in range(8)]
                mmt = p.add("tensor", mm, waits=tuple(ws), inc=sT)
                md_free[bi] = mmt
                mm_toks.append(mmt)
            for tc in range(8):
                bi = tc % 2

                def evd(v, tc=tc, bi=bi):
                    return v.tensor_copy(dst_[bi][:], pb2[tc].ap())
                et = p.add("vector", evd,
                           waits=(mm_toks[-1], bank2_free.get(('d', bi))), inc=sV)
                bank2_free[tc] = et

                def std(s, tc=tc, k=k, hf=hf, bi=bi):
                    return s.dma_start(
                        out=AP(dff_h.tensor, (hf * TH + tc * 128) * D + k * 512,
                               [[D, 128], [1, 512]]),
                        in_=dst_[bi][:])
                dt_ = p.add("sync", std, waits=(et,), inc=dO, k=16)
                bank2_free[('d', bi)] = dt_
                down_done.append(dt_)
        scD2.close()
        dff_done.append(down_done[-1])
        half_gate = down_done[-1]

    if dbg is not None:
        def d1(s):
            return s.dma_start(out=dbg["Mg"], in_=Mg)
        p.add("sync", d1, waits=(dff_done[-1],), inc=dO, k=16)
        def d2(s):
            return s.dma_start(out=dbg["g"], in_=g_h)
        p.add("sync", d2, waits=(dff_done[-1],), inc=dO, k=16)
        def d3(s):
            return s.dma_start(out=dbg["h"], in_=h_h)
        p.add("sync", d3, waits=(dff_done[-1],), inc=dO, k=16)
        def d4(s):
            return s.dma_start(out=dbg["dff"], in_=dff_h)
        p.add("sync", d4, waits=(dff_done[-1],), inc=dO, k=16)

    scF = b.scope()
    aot = [b.sbs(scF, f"f0_aot{i}", [128, D], F32) for i in range(2)]
    rxf = [b.sbs(scF, f"f0_rxf{i}", [128, D], F32) for i in range(2)]
    accf = [b.sbs(scF, f"f0_accf{i}", [128, D], F32) for i in range(2)]
    sqf = b.sbs(scF, "f0_sqf", [128, D], BF16)
    mcolf = b.sbs(scF, "f0_mcolf", [128, 8], F32)
    last_store = None
    aot_free = {}
    accf_free = {}
    for tt in range(NT):
        bi = tt % 2

        def ldao(s, tt=tt, bi=bi):
            return s.dma_start(out=aot[bi][:], in_=dff_h[tt * 128:(tt + 1) * 128, :])
        at = p.add("sync", ldao, waits=(aot_free.get(bi), dff_done[-1]), inc=dX, k=16)

        def ldrx(s, tt=tt, bi=bi):
            return s.dma_start(out=rxf[bi][:], in_=x_in[tt * 128:(tt + 1) * 128, :])
        rt = p.add("sync", ldrx, waits=(aot_free.get(('r', bi)),), inc=dX, k=16)

        def addr(v, bi=bi):
            return v.tensor_tensor(accf[bi][:], aot[bi][:], rxf[bi][:], ALU.add)
        adt = p.add("vector", addr, waits=(at, rt, accf_free.get(bi)), inc=sV)
        aot_free[bi] = adt
        aot_free[('r', bi)] = adt

        lt = emit_ln(p, sems, accf[bi], sqf, mcolf, bi * 4, (adt,))

        def sto(s, tt=tt, bi=bi):
            return s.dma_start(out=out_ap[tt * 128:(tt + 1) * 128, :], in_=accf[bi][:])
        last_store = p.add("sync", sto, waits=(lt,), inc=dO, k=16)
        accf_free[bi] = last_store
    scF.close()
    return last_store


def build_l0ffn(serial=False, debug=False):
    b = Bld()
    p = Prog(b)
    p.serial = serial
    p.serial_all = serial
    x_in = b.inp("x", [T, D])
    wgT_in = b.inp("wgT", [8, 512, HID8])
    wuT_in = b.inp("wuT", [8, 512, HID8])
    wdT_in = b.inp("wdT", [8, HID8, 512])
    id_in = b.inp("ident", [128, 128])
    out_ap = b.outp("out", [T, D])
    sems = {k2: b.sem(k2) for k2 in ("dW", "dX", "dO", "T", "V", "A", "P")}
    p.all_sems = sems
    ident_sb = b.sb("ident_sb", [128, 128], F32)

    def ldi(s):
        return s.dma_start(out=ident_sb[:], in_=id_in)
    itok = p.add("sync", ldi, inc=sems["dX"], k=16)
    dbg = None
    if debug:
        dbg = dict(Mg=b.outp("dbg_Mg", [8 * 4096 * HID8], BF16),
                   g=b.outp("dbg_g", [11264, T], BF16),
                   h=b.outp("dbg_h", [11264, T], BF16),
                   dff=b.outp("dbg_dff", [T, D]))
    last = emit_l0ffn(p, b, sems, x_in, wgT_in, wuT_in, wdT_in, ident_sb, out_ap,
                      start_tok=itok, dbg=dbg)

    def fin(s, last=last):
        return s.nop()
    p.add("sync", fin, waits=(last,))
    p.finish()
    b.es.close()
    return b.nc


NZ = 16
INTER = 8192
CONVD = 10240


def emit_l0mamba(p, b, sems, x_in, xhalo_in, wT_in, convw_in, convb_in, dtb_in,
                 Alog_in, Drow_in, normw_in, opT_in, zcol_in, m1z_in, umask_in,
                 lemask_in, upstrict_in, ident_sb, out_ap, replica_groups,
                 start_tok=None, dbg=None):
    _dbg = dbg
    p.serial = True
    """x_in [T,4096]; xhalo_in [3,4096] (zeros for half-0); wT_in [4096,18560]
    (host-transposed in_proj); convw_in [128,320] ([p, oc*4+k]); convb_in [128,80];
    dtb/Alog [128,1]; Drow_in [1,128]; normw_in [128,64] ([p, ic]); opT_in
    [8192,4096] (host-transposed out_proj); zcol (1 iff half-0); m1z (1-z);
    umask/lemask/upstrict [128,128] masks."""
    nc = b.nc
    dW, dX, dO = sems["dW"], sems["dX"], sems["dO"]
    sT, sV, sA, sP = sems["T"], sems["V"], sems["A"], sems["P"]

    hBC_h = b.dram("m_hBC", [CONVD, T], BF16)
    gate_h = b.dram("m_gate", [T, INTER], BF16)
    yT_h = b.dram("m_yT", [INTER, T], BF16)
    mo_h = b.dram("m_mo", [T, D], F32)
    stc_h = b.dram("m_stc", [128, 8192], F32)
    stp_h = b.dram("m_stp", [128, 8192], F32)

    cwb = b.sb("m_cwb", [128, 320], F32)
    cbb = b.sb("m_cbb", [128, 80], F32)
    dtbc = b.sb("m_dtbc", [128, 1], F32)
    Acol = b.sb("m_Acol", [128, 1], F32)
    zc = b.sb("m_zc", [128, 1], F32)
    m1z = b.sb("m_m1z", [128, 1], F32)
    umaskb = b.sb("m_umask", [128, 128], BF16)
    lemaskb = b.sb("m_lemask", [128, 128], F32)
    upstrb = b.sb("m_upstr", [128, 128], BF16)
    Dexp = b.sb("m_Dexp", [128, 128], F32)
    onesr = b.sb("m_onesr", [1, 128], F32)
    Ibz = b.sb("m_Ibz", [128, 128], BF16)
    scDT = b.scope()
    dtv = b.sbs(scDT, "m_dtv", [128, T], F32)

    def ldc(s):
        s.dma_start(out=cwb[:], in_=convw_in)
        return s.dma_start(out=cbb[:], in_=convb_in)
    def ldc1(s):
        return s.dma_start(out=cwb[:], in_=convw_in)
    c1 = p.add("sync", ldc1, waits=(start_tok,), inc=dX, k=16)

    def ldc2(s):
        return s.dma_start(out=cbb[:], in_=convb_in)
    p.add("sync", ldc2, inc=dX, k=16)

    def ldc3(s):
        return s.dma_start(out=dtbc[:], in_=dtb_in)
    p.add("sync", ldc3, inc=dX, k=16)

    def ldc4(s):
        return s.dma_start(out=Acol[:], in_=Alog_in)
    c4 = p.add("sync", ldc4, inc=dX, k=16)

    def ldc5(s):
        s.dma_start(out=zc[:], in_=zcol_in)
        return s.dma_start(out=m1z[:], in_=m1z_in)
    def ldc5a(s):
        return s.dma_start(out=zc[:], in_=zcol_in)
    p.add("sync", ldc5a, inc=dX, k=16)

    def ldc5b(s):
        return s.dma_start(out=m1z[:], in_=m1z_in)
    p.add("sync", ldc5b, inc=dX, k=16)

    def ldc6(g):
        return g.dma_start(out=umaskb[:], in_=umask_in)
    p.add("gpsimd", ldc6, waits=(start_tok,), inc=dW, k=16)

    def ldc7(s):
        return s.dma_start(out=lemaskb[:], in_=lemask_in)
    p.add("sync", ldc7, inc=dX, k=16)

    def ldc8(g):
        return g.dma_start(out=upstrb[:], in_=upstrict_in)
    c8 = p.add("gpsimd", ldc8, inc=dW, k=16)

    def mkA(s_):
        return s_.activation(Acol[:], Acol[:], AF.Exp)
    a1 = p.add("scalar", mkA, waits=(c4,), inc=sA)

    def mkA2(v):
        return v.tensor_scalar(Acol[:], Acol[:], -1.0, None, ALU.mult)
    atok = p.add("vector", mkA2, waits=(a1,), inc=sV)

    scd = b.scope()
    pD = b.pss(scd, "m_pD", [128, 128], F32)
    Drowb = b.sbs(scd, "m_Drow", [1, 128], F32)

    def mo1(g):
        return g.memset(onesr[:], 1.0)
    o1t = p.add("gpsimd", mo1, waits=(start_tok,), inc=sP)

    def ldD(s):
        return s.dma_start(out=Drowb[:], in_=Drow_in)
    dtk = p.add("sync", ldD, inc=dX, k=16)

    def mmD(t):
        return t.matmul(pD[:], onesr[0:1, :], Drowb[0:1, :], start=True, stop=True)
    dmt = p.add("tensor", mmD, waits=(o1t, dtk), inc=sT)

    def evD(v):
        return v.tensor_copy(Dexp[:], pD[:])
    dexp_tok = p.add("vector", evD, waits=(dmt,), inc=sV)
    scd.close()

    scA = b.scope()
    x0T = b.sbs(scA, "m_x0T", [128, 32 * 2051], BF16)
    scAx = b.scope()
    xrow = [b.sbs(scAx, f"m_xr{i}", [128, D], F32) for i in range(2)]
    pbA = [b.pss(scA, f"m_pA{i}", [128, 512], F32) for i in range(8)]

    def ldh(s):
        return s.dma_start(out=xrow[0][0:3, :], in_=xhalo_in)
    htok = p.add("sync", ldh, waits=(start_tok,), inc=dX, k=16)
    htoks = []
    for grp in range(8):
        def trh(t, grp=grp):
            for j in range(4):
                c = grp * 4 + j
                inst = t.transpose(AP(pbA[grp % 2], j * 128, [[512, 128], [1, 3]]),
                                   xrow[0][0:3, c * 128:(c + 1) * 128],
                                   ident_sb[0:3, 0:3])
            return inst
        tt_ = p.add("tensor", trh, waits=(htok, htoks[-2] if len(htoks) >= 2 else None),
                    inc=sT)

        def evh(v, grp=grp):
            return v.tensor_copy(
                AP(x0T, grp * 4 * 2051, [[32 * 2051, 128], [2051, 4], [1, 3]]),
                AP(pbA[grp % 2], 0, [[512, 128], [128, 4], [1, 3]]))
        htoks.append(p.add("vector", evh, waits=(tt_,), inc=sV))
    bank_freeA = {id(pbA[i]): htoks[-1] for i in range(2)}
    xrow_free = {0: sems["T"].tok()}
    for st in range(4):
        for q in range(4):
            bi = (st * 4 + q) % 2
            r0 = st * 512 + q * 128

            def ldx(s, r0=r0, bi=bi):
                return s.dma_start(out=xrow[bi][:], in_=x_in[r0:r0 + 128, :])
            xtok = p.add("sync", ldx, waits=(xrow_free.get(bi), htoks[-1]), inc=dX, k=16)
            toks = emit_transpose_tile(
                p, sems, xrow[bi], ident_sb,
                lambda c0, q, st=st: AP(x0T, c0 * 2051 + 3 + st * 512 + q * 128,
                                        [[32 * 2051, 128], [2051, 4], [1, 128]]),
                pbA[0:4], bank_freeA, q, wait_x=xtok)
            xrow_free[bi] = sems["T"].tok()
    xT_ready = sems["V"].tok()
    scAx.close()

    scB = b.scope()
    Wob = [b.sbs(scB, f"m_Wob{i}", [128, 32 * 128], BF16) for i in range(2)]
    projr = [b.sbs(scB, f"m_pr{i}", [128, 2051], F32) for i in range(2)]
    cacc = [b.sbs(scB, f"m_cacc{i}", [128, 2048], F32) for i in range(2)]
    hst = [b.sbs(scB, f"m_hst{i}", [128, 2048], BF16) for i in range(2)]
    ngroups = [(0, 3)] + [(3 + i * 512, 512) for i in range(4)]
    wfree = {}
    prfree = {}
    hstfree = {}
    for oc in range(81):
        bi = oc % 2

        def ldw(g, oc=oc, bi=bi):
            return g.dma_start(
                out=AP(Wob[bi], 0, [[32 * 128, 128], [128, 32], [1, 128]]),
                in_=AP(wT_in.tensor, 8192 + oc * 128,
                       [[18560, 128], [128 * 18560, 32], [1, 128]]))
        wtok = p.add("gpsimd", ldw, waits=(wfree.get(bi), c8), inc=dW, k=16)

        ev5 = []
        for ng, (coff, csz) in enumerate(ngroups):
            bank = pbA[4 + (oc * 5 + ng) % 2]

            def mmp(t, oc=oc, coff=coff, csz=csz, bi=bi, bank=bank):
                for k in range(32):
                    inst = t.matmul(bank[:, 0:csz],
                                    Wob[bi][:, k * 128:(k + 1) * 128],
                                    x0T[:, k * 2051 + coff: k * 2051 + coff + csz],
                                    start=(k == 0), stop=(k == 31))
                return inst
            mmt = p.add("tensor", mmp,
                        waits=(wtok, xT_ready, bank_freeA.get(id(bank))), inc=sT)

            def evp(s_, coff=coff, csz=csz, bi=bi, bank=bank):
                return s_.activation(projr[bi][:, coff:coff + csz], bank[:, 0:csz],
                                     AF.Copy)
            et = p.add("scalar", evp, waits=(mmt, prfree.get(bi) if ng == 0 else None),
                       inc=sA)
            bank_freeA[id(bank)] = et
            ev5.append(et)
        wfree[bi] = sems["T"].tok()

        if oc < 80:
            def cv0(v, oc=oc, bi=bi):
                return v.tensor_scalar(cacc[bi][:], projr[bi][:, 0:2048],
                                       cwb[:, oc * 4:oc * 4 + 1], None, ALU.mult)
            t0 = p.add("vector", cv0, waits=tuple(ev5) + ((c1,)), inc=sV)
            prev = t0
            for kk in range(1, 4):
                def cvk(v, oc=oc, kk=kk, bi=bi):
                    return v.scalar_tensor_tensor(
                        cacc[bi][:], projr[bi][:, kk:kk + 2048],
                        cwb[:, oc * 4 + kk:oc * 4 + kk + 1], cacc[bi][:],
                        ALU.mult, ALU.add)
                prev = p.add("vector", cvk, waits=(prev,), inc=sV)
            prfree[bi] = prev

            def sil(s_, oc=oc, bi=bi):
                return s_.activation(hst[bi][:], cacc[bi][:], AF.Silu,
                                     bias=cbb[:, oc:oc + 1])
            stok = p.add("scalar", sil, waits=(prev, hstfree.get(bi)), inc=sA)

            def sth(s, oc=oc, bi=bi):
                return s.dma_start(out=AP(hBC_h.tensor, oc * 128 * T, [[T, 128], [1, T]]),
                                   in_=hst[bi][:])
            hstfree[bi] = p.add("sync", sth, waits=(stok,), inc=dO, k=16)
        else:
            def d1(v, bi=bi):
                return v.tensor_scalar(projr[bi][:, 3:2051], projr[bi][:, 3:2051],
                                       dtbc[:], None, ALU.add)
            t1 = p.add("vector", d1, waits=tuple(ev5), inc=sV)

            def d2(v, bi=bi):
                return v.tensor_scalar_max(cacc[bi][:], projr[bi][:, 3:2051], 0.0)
            t2 = p.add("vector", d2, waits=(t1,), inc=sV)

            def d3b(s_, bi=bi):
                return s_.activation(cacc[1][:], projr[bi][:, 3:2051], AF.Abs)
            t3 = p.add("scalar", d3b, waits=(t2,), inc=sA)

            def d4(s_):
                return s_.activation(cacc[1][:], cacc[1][:], AF.Exp, scale=-1.0)
            t4 = p.add("scalar", d4, waits=(t3,), inc=sA)

            def d5(v):
                return v.tensor_scalar(cacc[1][:], cacc[1][:], 1.0, None, ALU.add)
            t5 = p.add("vector", d5, waits=(t4,), inc=sV)

            def d6(s_):
                return s_.activation(cacc[1][:], cacc[1][:], AF.Ln)
            t6 = p.add("scalar", d6, waits=(t5,), inc=sA)

            def d7(v, bi=bi):
                return v.tensor_tensor(dtv[:], cacc[bi][:], cacc[1][:], ALU.add)
            t7 = p.add("vector", d7, waits=(t6,), inc=sV)

            def d8(v):
                return v.tensor_scalar_min(dtv[:], dtv[:], 100.0)
            dt_tok = p.add("vector", d8, waits=(t7,), inc=sV)
    hbc_done = sems["dO"].tok()
    dtv_tok = sems["V"].tok()
    scB.close()

    scC2 = b.scope()
    Wgb = b.sbs(scC2, "m_Wgb", [128, 32 * 512], BF16)
    gstg = [b.sbs(scC2, f"m_gstg{i}", [128, 512], BF16) for i in range(2)]
    gfree = {}
    gstf = {}
    wg_prev = (hbc_done,)
    for gf in range(16):
        def ldwg(g, gf=gf):
            return g.dma_start(
                out=AP(Wgb, 0, [[32 * 512, 128], [512, 32], [1, 512]]),
                in_=AP(wT_in.tensor, gf * 512,
                       [[18560, 128], [128 * 18560, 32], [1, 512]]))
        wgt = p.add("gpsimd", ldwg, waits=wg_prev, inc=dW, k=16)
        for tc in range(16):
            bank = pbA[6 + tc % 2]

            def mmg(t, gf=gf, tc=tc, bank=bank):
                for k in range(32):
                    inst = t.matmul(bank[:],
                                    x0T[:, k * 2051 + 3 + tc * 128: k * 2051 + 3 + tc * 128 + 128],
                                    Wgb[:, k * 512:(k + 1) * 512],
                                    start=(k == 0), stop=(k == 31))
                return inst
            mmt = p.add("tensor", mmg, waits=(wgt, gfree.get(tc % 2)), inc=sT)

            def evg(v, tc=tc, bank=bank, bi=tc % 2):
                return v.tensor_copy(gstg[bi][:], bank.ap())
            et = p.add("vector", evg, waits=(mmt, gstf.get(tc % 2)), inc=sV)
            gfree[tc % 2] = et

            def stg(s, gf=gf, tc=tc, bi=tc % 2):
                return s.dma_start(
                    out=AP(gate_h.tensor, tc * 128 * INTER + gf * 512,
                           [[INTER, 128], [1, 512]]),
                    in_=gstg[bi][:])
            gstf[tc % 2] = p.add("sync", stg, waits=(et,), inc=dO, k=16)
        wg_prev = (sems["T"].tok(),)
    gate_done = sems["dO"].tok()
    scC2.close()
    scA.close()

    scP = b.scope()
    prevf = b.sbs(scP, "m_prevf", [128, 8192], F32)
    prevbf = b.sbs(scP, "m_prevbf", [128, 8192], BF16)
    at_s = b.sbs(scP, "m_at", [128, 128], F32)
    acum = b.sbs(scP, "m_acum", [128, 128], F32)
    dec_s = b.sbs(scP, "m_dec", [128, 128], F32)
    E_s = b.sbs(scP, "m_E", [128, 128], F32)
    dtT = b.sbs(scP, "m_dtT", [128, 128], F32)
    decT = b.sbs(scP, "m_decT", [128, 128], F32)
    ET = b.sbs(scP, "m_ET", [128, 128], F32)
    aT = b.sbs(scP, "m_aT", [128, 128], F32)
    gT = b.sbs(scP, "m_gT", [128, 128], F32)
    grow = b.sbs(scP, "m_grow", [1, 128], F32)
    nc8 = b.sbs(scP, "m_nc8", [128, 16], F32)

    def ssd_pass(full, init_tok, dt_tok, hbc_done, gate_done):
        scZ = b.scope()
        hbz = b.sbs(scZ, f"m_hbz{int(full)}", [128, 80 * 128], BF16)
        hs_tok = b.sbs(scZ, f"m_hstk{int(full)}", [128, 64 * 128], BF16)
        B_tok = b.sbs(scZ, f"m_Btok{int(full)}", [128, 1024], BF16)
        xch = b.sbs(scZ, f"m_xch{int(full)}", [128, 8192], BF16)
        Bdec = b.sbs(scZ, f"m_Bdec{int(full)}", [128, 8192], BF16)
        tmp1 = b.sbs(scZ, f"m_tmp1{int(full)}", [128, 512], F32)
        if full:
            gz = b.sbs(scZ, "m_gz", [128, 8192], BF16)
            rhsD2 = b.sbs(scZ, "m_rhsD2", [128, 8192], BF16)
            Msb = rhsD2
            Sgsb = b.sbs(scZ, "m_Sgsb", [128, 1024], BF16)
            ybuf = b.sbs(scZ, "m_ybuf", [128, 8192], F32)
            ystage = b.sbs(scZ, "m_ystage", [128, 1024], BF16)
        pbZ = [b.pss(scZ, f"m_pZ{int(full)}_{i}", [128, 512], F32) for i in range(6)]
        pbB = b.pss(scZ, f"m_pB{int(full)}", [128, 1024], BF16)

        hbz_free = None
        gz_free = None
        ytoks = []
        old_serial = p.serial
        p.serial = True
        for z in range(NZ):
            def ldhbz(s, z=z):
                return s.dma_start(
                    out=AP(hbz, 0, [[80 * 128, 128], [128, 80], [1, 128]]),
                    in_=AP(hBC_h.tensor, z * 128, [[T, 128], [128 * T, 80], [1, 128]]))
            hbt = p.add("sync", ldhbz, waits=(hbz_free, hbc_done, init_tok), inc=dX, k=16)
            if full:
                def ldgz(s, z=z):
                    return s.dma_start(out=gz[:],
                                       in_=gate_h[z * 128:(z + 1) * 128, :])
                gzt = p.add("sync", ldgz, waits=(gz_free, gate_done), inc=dX, k=16)

            def mka(v, z=z):
                return v.tensor_scalar(at_s[:], dtv[:, z * 128:(z + 1) * 128],
                                       Acol[:], None, ALU.mult)
            t1 = p.add("vector", mka, waits=(dt_tok, atok), inc=sV)

            def mkacum(v):
                return v.tensor_tensor_scan(acum[:], at_s[:], at_s[:], 0.0,
                                            ALU.add, ALU.bypass)
            t2 = p.add("vector", mkacum, waits=(t1,), inc=sV)

            def mkdec(s_):
                return s_.activation(dec_s[:], acum[:], AF.Exp, scale=-1.0,
                                     bias=acum[:, 127:128])
            t3 = p.add("scalar", mkdec, waits=(t2,), inc=sA)

            def mkE(s_):
                return s_.activation(E_s[:], acum[:], AF.Exp)
            t4 = p.add("scalar", mkE, waits=(t2,), inc=sA)

            trs = [(dtv, dtT, ("slice", z)), (dec_s, decT, None), (E_s, ET, None),
                   (at_s, aT, None)]
            ttoks = []
            for i, (srct, dstt, mode) in enumerate(trs):
                def trf(t, srct=srct, mode=mode, i=i):
                    if mode is not None:
                        srcap = srct[:, mode[1] * 128:(mode[1] + 1) * 128]
                    else:
                        srcap = srct[:, :]
                    return t.transpose(
                        AP(pbZ[0], (i % 4) * 128, [[512, 128], [1, 128]]),
                        srcap, ident_sb[:, :])
                tt_ = p.add("tensor", trf, waits=(t3, t4), inc=sT)
                ttoks.append(tt_)

            def evtr(v):
                v.tensor_copy(dtT[:], AP(pbZ[0], 0, [[512, 128], [1, 128]]))
                v.tensor_copy(decT[:], AP(pbZ[0], 128, [[512, 128], [1, 128]]))
                v.tensor_copy(ET[:], AP(pbZ[0], 256, [[512, 128], [1, 128]]))
                return v.tensor_copy(aT[:], AP(pbZ[0], 384, [[512, 128], [1, 128]]))
            trt = p.add("vector", evtr, waits=(ttoks[-1],), inc=sV)

            def trg(t):
                return t.transpose(AP(pbZ[1], 0, [[512, 1], [1, 128]]),
                                   acum[:, 127:128], ident_sb[:, :])
            g1 = p.add("tensor", trg, waits=(t2, trt), inc=sT)

            def evg2(v):
                return v.tensor_copy(grow[0:1, :],
                                     AP(pbZ[1], 0, [[512, 1], [1, 128]]))
            g2 = p.add("vector", evg2, waits=(g1,), inc=sV)

            def mmg2(t):
                return t.matmul(pbZ[1][:, 0:128], onesr[0:1, :], grow[0:1, :],
                                start=True, stop=True)
            g3 = p.add("tensor", mmg2, waits=(g2,), inc=sT)

            def evg3(s_):
                return s_.activation(gT[:], pbZ[1][:, 0:128], AF.Exp)
            gT_tok = p.add("scalar", evg3, waits=(g3,), inc=sA)

            hst_toks = []
            for grp in range(8):
                def trh(t, grp=grp):
                    for j in range(8):
                        c = grp * 8 + j
                        inst = t.transpose(pbB[:, j * 128:(j + 1) * 128],
                                           hbz[:, c * 128:(c + 1) * 128], umaskb[:, :]
                                           if False else Ibz[:, :])
                    return inst
                tt_ = p.add("tensor", trh, waits=(hbt, hst_toks[-1] if hst_toks else None),
                            inc=sT)

                def evh(v, grp=grp):
                    return v.tensor_copy(hs_tok[:, grp * 1024:(grp + 1) * 1024], pbB[:])
                hst_toks.append(p.add("vector", evh, waits=(tt_,), inc=sV))

            def trB(t):
                for j in range(8):
                    inst = t.transpose(pbB[:, j * 128:(j + 1) * 128],
                                       hbz[:, (64 + j) * 128:(64 + j + 1) * 128],
                                       Ibz[:, :])
                return inst
            tb_ = p.add("tensor", trB, waits=(hbt, hst_toks[-1]), inc=sT)

            def evB(v):
                return v.tensor_copy(B_tok[:], pbB[:])
            btok_t = p.add("vector", evB, waits=(tb_,), inc=sV)

            def mkxch(v):
                return v.tensor_tensor(
                    xch[:],
                    AP(hs_tok, 0, [[64 * 128, 128], [1, 8192]]),
                    AP(dtT, 0, [[128, 128], [1, 128], [0, 64]]), ALU.mult)
            xch_tok = p.add("vector", mkxch, waits=(hst_toks[-1], trt), inc=sV)

            if full:
                def cprev(g):
                    return g.tensor_copy(prevbf[:], prevf[:])
                pbf_tok = p.add("gpsimd", cprev, waits=(init_tok,), inc=sP)

                sg_toks = []
                for g_ in range(8):
                    def mmsg(t, g_=g_):
                        return t.matmul(pbZ[2][:, 0:128],
                                        hbz[:, (64 + g_) * 128:(64 + g_ + 1) * 128],
                                        hbz[:, (72 + g_) * 128:(72 + g_ + 1) * 128],
                                        start=True, stop=True)
                    m1 = p.add("tensor", mmsg, waits=(hbt, sg_toks[-1] if sg_toks else None),
                               inc=sT)

                    def evsg(v, g_=g_):
                        return v.tensor_tensor(Sgsb[:, g_ * 128:(g_ + 1) * 128],
                                               pbZ[2][:, 0:128], umaskb[:, :], ALU.mult)
                    sg_toks.append(p.add("vector", evsg, waits=(m1,), inc=sV))
                sg_all = sg_toks[-1]

            if full:
                def gsil(s_):
                    return s_.activation(gz[:], gz[:], AF.Silu)
                gz_sil = p.add("scalar", gsil, waits=(gzt,), inc=sA)

            def pscale(g):
                return g.tensor_tensor(
                    prevf[:], prevf[:],
                    AP(gT, 0, [[128, 128], [1, 128], [0, 64]]), ALU.mult)
            u1_tok = p.add("gpsimd", pscale,
                           waits=(gT_tok, init_tok) + ((pbf_tok,) if full else ()),
                           inc=sP)

            for hh in range(2):
                def mkbdec(g, hh=hh):
                    return g.tensor_tensor(
                        Bdec[:],
                        AP(B_tok, hh * 4 * 128, [[1024, 128], [128, 4], [0, 16], [1, 128]]),
                        AP(decT, hh * 64, [[128, 128], [16, 4], [1, 16], [0, 128]]),
                        ALU.mult)
                bd_tok = p.add("gpsimd", mkbdec, waits=(btok_t, trt), inc=sP)

                if full:
                    def mkrhs(g, hh=hh):
                        return g.tensor_tensor(
                            rhsD2[:],
                            AP(aT, hh * 64, [[128, 128], [1, 64], [0, 128]]),
                            AP(lemaskb, 0, [[128, 128], [0, 64], [1, 128]]),
                            ALU.mult)
                    rhs_tok = p.add("gpsimd", mkrhs, waits=(trt,), inc=sP)
                    m_toks = []
                    for i in range(16):
                        def mmd2(t, i=i):
                            return t.matmul(pbZ[3 + i % 2][:], upstrb[:, :],
                                            rhsD2[:, i * 512:(i + 1) * 512],
                                            start=True, stop=True)
                        md = p.add("tensor", mmd2,
                                   waits=(rhs_tok, c8, m_toks[-2] if len(m_toks) >= 2 else None),
                                   inc=sT)

                        def expd(s_, i=i):
                            return s_.activation(ystage[:, 0:512], pbZ[3 + i % 2][:],
                                                 AF.Exp)
                        ed = p.add("scalar", expd, waits=(md,), inc=sA)

                        def mkm(v, i=i, hh=hh):
                            g_ = (hh * 64 + i * 4) // 16
                            return v.tensor_tensor(
                                Msb[:, i * 512:(i + 1) * 512], ystage[:, 0:512],
                                AP(Sgsb, g_ * 128, [[1024, 128], [0, 4], [1, 128]]),
                                ALU.mult)
                        m_toks.append(p.add("vector", mkm, waits=(ed, sg_all), inc=sV))
                    m_all = m_toks[-1]

                for hg in range(8):
                    def mmy(t, hh=hh, hg=hg, full=full):
                        inst = None
                        for h8 in range(8):
                            hloc = hh * 64 + hg * 8 + h8
                            g_ = hloc // 16
                            hi = hg * 8 + h8
                            if full:
                                t.matmul(pbZ[3][:, h8 * 64:(h8 + 1) * 64],
                                         Msb[:, hi * 128:(hi + 1) * 128],
                                         xch[:, hloc * 64:(hloc + 1) * 64],
                                         start=True, stop=True)
                                t.matmul(pbZ[4][:, h8 * 64:(h8 + 1) * 64],
                                         hbz[:, (72 + g_) * 128:(72 + g_ + 1) * 128],
                                         prevbf[:, hloc * 64:(hloc + 1) * 64],
                                         start=True, stop=True)
                            inst = t.matmul(pbZ[5][:, h8 * 64:(h8 + 1) * 64],
                                            Bdec[:, hi * 128:(hi + 1) * 128],
                                            xch[:, hloc * 64:(hloc + 1) * 64],
                                            start=True, stop=True)
                        return inst
                    ws = [xch_tok, bd_tok, u1_tok]
                    if full:
                        ws += [m_all, pbf_tok]
                    mmt = p.add("tensor", mmy, waits=tuple(ws), inc=sT)

                    def padd(v, hh=hh, hg=hg):
                        off = (hh * 64 + hg * 8) * 64
                        return v.tensor_tensor(prevf[:, off:off + 512],
                                               prevf[:, off:off + 512],
                                               pbZ[5][:], ALU.add)
                    pu = p.add("vector", padd, waits=(mmt, u1_tok), inc=sV)

                    if full:
                        def yev1(v, hh=hh, hg=hg):
                            return v.tensor_tensor(
                                tmp1[:], pbZ[4][:],
                                AP(ET, hh * 64 + hg * 8, [[128, 128], [1, 8], [0, 64]]),
                                ALU.mult)
                        y1t = p.add("vector", yev1, waits=(mmt,), inc=sV)

                        def yev2(v, hh=hh, hg=hg):
                            off = (hh * 64 + hg * 8) * 64
                            return v.tensor_tensor(ybuf[:, off:off + 512], tmp1[:],
                                                   pbZ[3][:], ALU.add)
                        y2t = p.add("vector", yev2, waits=(y1t,), inc=sV)

                        def yev3(g, hh=hh, hg=hg):
                            off = (hh * 64 + hg * 8) * 64
                            return g.tensor_tensor(
                                tmp1[:],
                                AP(hs_tok, off, [[64 * 128, 128], [1, 512]]),
                                AP(Dexp, hh * 64 + hg * 8, [[128, 128], [1, 8], [0, 64]]),
                                ALU.mult)
                        y3t = p.add("gpsimd", yev3, waits=(y2t, dexp_tok), inc=sP)

                        def yev4(v, hh=hh, hg=hg):
                            off = (hh * 64 + hg * 8) * 64
                            return v.tensor_tensor(ybuf[:, off:off + 512],
                                                   ybuf[:, off:off + 512], tmp1[:],
                                                   ALU.add)
                        p.add("vector", yev4, waits=(y3t,), inc=sV)

            if full:
                def gmul(v):
                    return v.tensor_tensor(ybuf[:], ybuf[:], gz[:], ALU.mult)
                gm = p.add("vector", gmul, waits=(gz_sil, sems["V"].tok()), inc=sV)

                sq_toks = []
                for gg_ in range(8):
                    def sqg(s_, gg_=gg_):
                        return s_.activation(rhsD2[:, 0:1024],
                                             ybuf[:, gg_ * 1024:(gg_ + 1) * 1024],
                                             AF.Square, accum_out=nc8[:, gg_:gg_ + 1])
                    sq_toks.append(p.add("scalar", sqg,
                                         waits=(gm, sq_toks[-1] if sq_toks else None),
                                         inc=sA))

                def veps2(v):
                    return v.tensor_scalar(nc8[:, 8:16], nc8[:, 0:8], 1.0 / 1024.0,
                                           EPS, ALU.mult, ALU.add)
                ve = p.add("vector", veps2, waits=(sq_toks[-1],), inc=sV)

                def rstd2(s_):
                    s_.activation(nc8[:, 8:16], nc8[:, 8:16], AF.Ln)
                    return s_.activation(nc8[:, 8:16], nc8[:, 8:16], AF.Exp, scale=-0.5)
                rs = p.add("scalar", rstd2, waits=(ve,), inc=sA)

                nrm_toks = []
                for gg_ in range(8):
                    def nrmg(v, gg_=gg_):
                        return v.tensor_scalar(gz[:, gg_ * 1024:(gg_ + 1) * 1024],
                                               ybuf[:, gg_ * 1024:(gg_ + 1) * 1024],
                                               nc8[:, 8 + gg_:9 + gg_], None, ALU.mult)
                    nrm_toks.append(p.add("vector", nrmg, waits=(rs,), inc=sV))
                nrm_all = nrm_toks[-1]

                ztoks = []
                for grp in range(8):
                    def try_(t, grp=grp):
                        for j in range(8):
                            c = grp * 8 + j
                            inst = t.transpose(pbB[:, j * 128:(j + 1) * 128],
                                               gz[:, c * 128:(c + 1) * 128], Ibz[:, :])
                        return inst
                    tt_ = p.add("tensor", try_, waits=(nrm_all,
                                                       ztoks[-1] if ztoks else None), inc=sT)

                    def evy(v, grp=grp):
                        return v.tensor_copy(ystage[:], pbB[:])
                    ee = p.add("vector", evy, waits=(tt_,), inc=sV)

                    def sty(s, grp=grp, z=z):
                        return s.dma_start(
                            out=AP(yT_h.tensor, grp * 8 * 128 * T + z * 128,
                                   [[T, 128], [128 * T, 8], [1, 128]]),
                            in_=AP(ystage, 0, [[1024, 128], [128, 8], [1, 128]]))
                    ztoks.append(p.add("sync", sty, waits=(ee,), inc=dO, k=16))
                ytoks.append(ztoks[-1])
                gz_free = ztoks[-1]
            hbz_free = sems["T"].tok()
        p.serial = old_serial
        scZ.close()
        return ytoks[-1] if full else sems["V"].tok()

    def mkIbz(g):
        return g.tensor_copy(Ibz[:], ident_sb[:, :])
    ibz_tok = p.add("gpsimd", mkIbz, waits=(start_tok,), inc=sP)

    def pz(g):
        return g.memset(prevf[:], 0.0)
    pz_tok = p.add("gpsimd", pz, waits=(ibz_tok,), inc=sP)

    dt_tok = sems["V"].tok()
    sp_done = ssd_pass(False, pz_tok, dt_tok, hbc_done, None)

    scC = b.scope()
    cstg = b.sbs(scC, "m_cstg", [128, 8192], F32)

    def mkcontrib(v):
        return v.tensor_scalar(cstg[:], prevf[:], zc[:], None, ALU.mult)
    cb = p.add("vector", mkcontrib, waits=(sp_done, sems["V"].tok()), inc=sV)

    def stc(s):
        return s.dma_start(out=stc_h, in_=cstg[:])
    cdma = p.add("sync", stc, waits=(cb,), inc=dO, k=16)

    def coll(g):
        return g.collective_compute("AllReduce", ALU.add, replica_groups,
                                    [stc_h.opt()], [stp_h.opt()])
    ctok_ = p.add("gpsimd", coll, waits=(cdma,), inc=sP, k=1)

    def ldprior(s):
        return s.dma_start(out=cstg[:], in_=stp_h)
    pl = p.add("sync", ldprior, waits=(ctok_,), inc=dX, k=16)

    def mkinit(v):
        return v.tensor_scalar(prevf[:], cstg[:], m1z[:], None, ALU.mult)
    init2 = p.add("vector", mkinit, waits=(pl,), inc=sV)
    scC.close()

    y_done = ssd_pass(True, init2, dt_tok, hbc_done, gate_done)
    scP.close()

    scE = b.scope()
    Wop = b.sbs(scE, "m_Wop", [128, 64 * 1024], BF16)
    nwc = b.sbs(scE, "m_nwc", [128, 64], F32)
    ytt = [b.sbs(scE, f"m_ytt{i}", [128, 64 * 128], BF16) for i in range(2)]
    est = [b.sbs(scE, f"m_est{i}", [128, 1024], F32) for i in range(2)]
    pbE = [b.pss(scE, f"m_pE{i}", [128, 512], F32) for i in range(4)]

    def ldnw(s):
        return s.dma_start(out=nwc[:], in_=normw_in)
    nwt = p.add("sync", ldnw, waits=(y_done,), inc=dX, k=16)

    mo_toks = []
    wq_prev2 = (y_done,)
    ytt_free = {}
    est_free = {}
    ebank_free = {}
    for oq in range(4):
        def ldop(g, oq=oq):
            return g.dma_start(
                out=AP(Wop, 0, [[64 * 1024, 128], [1024, 64], [1, 1024]]),
                in_=AP(opT_in.tensor, oq * 1024,
                       [[4096, 128], [128 * 4096, 64], [1, 1024]]))
        wt_ = p.add("gpsimd", ldop, waits=wq_prev2, inc=dW, k=16)

        nw_toks = []
        for ic in range(64):
            def scw(v, ic=ic):
                return v.tensor_scalar(Wop[:, ic * 1024:(ic + 1) * 1024],
                                       Wop[:, ic * 1024:(ic + 1) * 1024],
                                       nwc[:, ic:ic + 1], None, ALU.mult)
            nw_toks.append(p.add("vector", scw, waits=(wt_, nwt), inc=sV))
        nw_all = nw_toks[-1]

        for tc in range(16):
            bi = tc % 2

            def ldy(s, tc=tc, bi=bi):
                return s.dma_start(
                    out=AP(ytt[bi], 0, [[64 * 128, 128], [128, 64], [1, 128]]),
                    in_=AP(yT_h.tensor, tc * 128, [[T, 128], [128 * T, 64], [1, 128]]))
            yt_ = p.add("sync", ldy, waits=(ytt_free.get(bi), y_done), inc=dX, k=16)

            ev2 = []
            for on in range(2):
                bank = pbE[(tc * 2 + on) % 4]

                def mmo2(t, tc=tc, on=on, bank=bank, bi=bi):
                    for ic in range(64):
                        inst = t.matmul(bank[:],
                                        ytt[bi][:, ic * 128:(ic + 1) * 128],
                                        Wop[:, ic * 1024 + on * 512: ic * 1024 + on * 512 + 512],
                                        start=(ic == 0), stop=(ic == 63))
                    return inst
                mmt = p.add("tensor", mmo2,
                            waits=(yt_, nw_all, ebank_free.get((oq, tc, on))), inc=sT)
                ebank_free[(oq, tc + 2, on)] = mmt

                def evo2(s_, on=on, bank=bank, bi=bi):
                    return s_.activation(est[bi][:, on * 512:(on + 1) * 512], bank.ap(),
                                         AF.Copy)
                ev2.append(p.add("scalar", evo2, waits=(mmt, est_free.get(bi)), inc=sA))
            ytt_free[bi] = ev2[-1]

            def stmo(s, oq=oq, tc=tc, bi=bi):
                return s.dma_start(
                    out=AP(mo_h.tensor, tc * 128 * D + oq * 1024, [[D, 128], [1, 1024]]),
                    in_=est[bi][:])
            est_free[bi] = p.add("sync", stmo, waits=(ev2[-1],), inc=dO, k=16)
            mo_toks.append(est_free[bi])
        wq_prev2 = (sems["T"].tok(),)
    mo_done = mo_toks[-1]
    scE.close()

    if _dbg is not None:
        def dmp(nm, src_ap, wait):
            def dd(s, nm=nm, src_ap=src_ap):
                return s.dma_start(out=_dbg[nm], in_=src_ap)
            return p.add("sync", dd, waits=(wait,), inc=dO, k=16)
        dmp("hBC", hBC_h, mo_done)
        dmp("gate", gate_h, mo_done)
        dmp("yT", yT_h, mo_done)
        dmp("mo", mo_h, mo_done)
        dmp("stc", stc_h, mo_done)
        dmp("stp", stp_h, mo_done)
        def ddt(s):
            return s.dma_start(out=_dbg["dtv"], in_=dtv[:])
        p.add("sync", ddt, waits=(mo_done,), inc=dO, k=16)

    scF = b.scope()
    aot = [b.sbs(scF, f"m_aot{i}", [128, D], F32) for i in range(2)]
    rxf = [b.sbs(scF, f"m_rxf{i}", [128, D], F32) for i in range(2)]
    accf = [b.sbs(scF, f"m_accf{i}", [128, D], F32) for i in range(2)]
    sqf = b.sbs(scF, "m_sqf", [128, D], BF16)
    mcolf = b.sbs(scF, "m_mcolf", [128, 8], F32)
    last_store = None
    aot_free = {}
    accf_free = {}
    for tt in range(NT):
        bi = tt % 2

        def ldao(s, tt=tt, bi=bi):
            return s.dma_start(out=aot[bi][:], in_=mo_h[tt * 128:(tt + 1) * 128, :])
        at = p.add("sync", ldao, waits=(aot_free.get(bi), mo_done), inc=dX, k=16)

        def ldrx(s, tt=tt, bi=bi):
            return s.dma_start(out=rxf[bi][:], in_=x_in[tt * 128:(tt + 1) * 128, :])
        rt = p.add("sync", ldrx, waits=(aot_free.get(('r', bi)), mo_done), inc=dX, k=16)

        def addr(v, bi=bi):
            return v.tensor_tensor(accf[bi][:], aot[bi][:], rxf[bi][:], ALU.add)
        adt = p.add("vector", addr, waits=(at, rt, accf_free.get(bi)), inc=sV)
        aot_free[bi] = adt
        aot_free[('r', bi)] = adt

        lt = emit_ln(p, sems, accf[bi], sqf, mcolf, bi * 4, (adt,))

        def sto(s, tt=tt, bi=bi):
            return s.dma_start(out=out_ap[tt * 128:(tt + 1) * 128, :], in_=accf[bi][:])
        last_store = p.add("sync", sto, waits=(lt,), inc=dO, k=16)
        accf_free[bi] = last_store
    scF.close()
    scDT.close()
    p.serial = False
    return last_store


def build_l0mamba(replica_groups, serial=False, debug=False):
    b = Bld()
    p = Prog(b)
    p.serial = serial
    p.serial_all = serial
    x_in = b.inp("x", [T, D])
    xhalo_in = b.inp("xhalo", [3, D])
    wT_in = b.inp("wT", [D, 18560])
    convw_in = b.inp("convw", [128, 320])
    convb_in = b.inp("convb", [128, 80])
    dtb_in = b.inp("dtb", [128, 1])
    Alog_in = b.inp("Alog", [128, 1])
    Drow_in = b.inp("Drow", [1, 128])
    normw_in = b.inp("normw", [128, 64])
    opT_in = b.inp("opT", [INTER, D])
    zcol_in = b.inp("zcol", [128, 1])
    m1z_in = b.inp("m1z", [128, 1])
    umask_in = b.inp("umask", [128, 128])
    lemask_in = b.inp("lemask", [128, 128])
    upstrict_in = b.inp("upstrict", [128, 128], BF16)
    id_in = b.inp("ident", [128, 128])
    out_ap = b.outp("out", [T, D])
    sems = {k2: b.sem(k2) for k2 in ("dW", "dX", "dO", "T", "V", "A", "P")}
    p.all_sems = sems
    ident_sb = b.sb("ident_sb", [128, 128], F32)

    def ldi(s):
        return s.dma_start(out=ident_sb[:], in_=id_in)
    itok = p.add("sync", ldi, inc=sems["dX"], k=16)
    dbg = None
    if debug:
        dbg = dict(hBC=b.outp("dbg_hBC", [CONVD, T], BF16),
                   gate=b.outp("dbg_gate", [T, INTER], BF16),
                   yT=b.outp("dbg_yT", [INTER, T], BF16),
                   mo=b.outp("dbg_mo", [T, D]),
                   stc=b.outp("dbg_stc", [128, 8192]),
                   stp=b.outp("dbg_stp", [128, 8192]),
                   dtv=b.outp("dbg_dtv", [128, T]))
    last = emit_l0mamba(p, b, sems, x_in, xhalo_in, wT_in, convw_in, convb_in,
                        dtb_in, Alog_in, Drow_in, normw_in, opT_in, zcol_in,
                        m1z_in, umask_in, lemask_in, upstrict_in, ident_sb,
                        out_ap, replica_groups, start_tok=itok, dbg=dbg)

    def fin(s, last=last):
        return s.nop()
    p.add("sync", fin, waits=(last,))
    p.finish()
    b.es.close()
    return b.nc


def l0mamba_host_inputs(hc):
    umask = np.triu(np.ones((128, 128), np.float32))
    lemask = np.triu(np.ones((128, 128), np.float32))
    upstrict = np.triu(np.ones((128, 128), np.float32), 1)
    ups = np.tril(np.ones((128, 128), np.float32), -1).T
    ups = (np.arange(128)[:, None] > np.arange(128)[None, :]).astype(np.float32)
    zcol = np.full((128, 1), 1.0 if hc == 0 else 0.0, np.float32)
    m1z = np.full((128, 1), 0.0 if hc == 0 else 1.0, np.float32)
    return dict(umask=umask, lemask=lemask,
                upstrict=ups.astype(ml_dtypes.bfloat16),
                zcol=zcol, m1z=m1z, ident=np.eye(128, dtype=np.float32))




def build_fused(replica_groups):
    b = Bld()
    p = Prog(b)
    x_in = b.inp("x", [T, D])
    xhalo_in = b.inp("xhalo", [3, D])
    wT_in = b.inp("wT", [D, 18560])
    convw_in = b.inp("convw", [128, 320])
    convb_in = b.inp("convb", [128, 80])
    dtb_in = b.inp("dtb", [128, 1])
    Alog_in = b.inp("Alog", [128, 1])
    Drow_in = b.inp("Drow", [1, 128])
    normw_in = b.inp("normw", [128, 64])
    opT_in = b.inp("opT", [INTER, D])
    zcol_in = b.inp("zcol", [128, 1])
    m1z_in = b.inp("m1z", [128, 1])
    umask_in = b.inp("umask", [128, 128])
    lemask_in = b.inp("lemask", [128, 128])
    upstrict_in = b.inp("upstrict", [128, 128], BF16)
    wgT_in = b.inp("wgT", [8, 512, HID8])
    wuT_in = b.inp("wuT", [8, 512, HID8])
    wdT_in = b.inp("wdT", [8, HID8, 512])
    Wq_in = b.inp("Wq", [D, D])
    Wk_in = b.inp("Wk", [D, 1024])
    Wv_in = b.inp("Wv", [D, 1024])
    Wo_in = b.inp("Wo", [D, D])
    cos_in = b.inp("cos", [128, T])
    sinm_in = b.inp("sinm", [128, T])
    pad_in = b.inp("pad", [1, 4096])
    dmask_in = b.inp("dmask", [128, 2048])
    prot_in = b.inp("prot", [128, 128], BF16)
    V_in = b.inp("V", [D, 1024])
    UT_in = b.inp("UT", [1024, D])
    S_in = b.inp("S_r", [128, 8])
    id_in = b.inp("ident", [128, 128])
    out_ap = b.outp("out", [T, D])

    x1_h = b.dram("x1_h", [T, D], F32)
    x2_h = b.dram("x2_h", [T, D], F32)
    x3_h = b.dram("x3_h", [T, D], F32)

    sems = {k2: b.sem(k2) for k2 in ("dW", "dX", "dO", "T", "V", "A", "P")}
    p.all_sems = sems
    ident_sb = b.sb("ident_sb", [128, 128], F32)

    def ldi(s):
        return s.dma_start(out=ident_sb[:], in_=id_in)
    itok = p.add("sync", ldi, inc=sems["dX"], k=16)

    def block_scope(emit_fn):
        old_es = b.es
        sc = ExitStack()
        b.es = sc
        tok = emit_fn()
        b.es = old_es
        sc.close()
        return tok

    t1 = block_scope(lambda: emit_l0mamba(
        p, b, sems, x_in, xhalo_in, wT_in, convw_in, convb_in, dtb_in, Alog_in,
        Drow_in, normw_in, opT_in, zcol_in, m1z_in, umask_in, lemask_in,
        upstrict_in, ident_sb, x1_h, replica_groups, start_tok=itok))
    t2 = block_scope(lambda: emit_l0ffn(
        p, b, sems, x1_h, wgT_in, wuT_in, wdT_in, ident_sb, x2_h, start_tok=t1))
    t3 = block_scope(lambda: emit_l1attn(
        p, b, sems, x2_h, Wq_in, Wk_in, Wv_in, Wo_in, cos_in, sinm_in, zcol_in,
        pad_in, dmask_in, prot_in, ident_sb, x3_h, replica_groups, start_tok=t2))
    t4 = block_scope(lambda: emit_l1ffn(
        p, b, sems, x3_h, V_in, UT_in, S_in, ident_sb, t3, out_ap, start_tok=t3))

    def fin(s):
        return s.nop()
    p.add("sync", fin, waits=(t4,))
    p.finish()
    b.es.close()
    return b.nc


RG = [[0, 1], [2, 3], [4, 5], [6, 7]]
_cache = {}

FUSED = True


def _progs():
    if FUSED:
        if "fused" not in _cache:
            _cache["fused"] = build_fused(RG)
    else:
        if "m" not in _cache:
            _cache["m"] = build_l0mamba(RG)
            _cache["f0"] = build_l0ffn()
            _cache["a"] = build_l1attn(RG)
            _cache["f1"] = build_l1ffn()
    return _cache


def kernel(L0_attn_x, L0_attn_in_proj_w, L0_attn_conv_w, L0_attn_conv_b,
           L0_attn_dt_bias, L0_attn_A_log, L0_attn_D, L0_attn_norm_w,
           L0_attn_out_proj_w, L0_ffn_w_gate, L0_ffn_w_up, L0_ffn_w_down,
           L1_attn_Wq, L1_attn_Wk, L1_attn_Wv, L1_attn_Wo, L1_ffn_U, L1_ffn_S,
           L1_ffn_V):
    x = np.ascontiguousarray(np.asarray(L0_attn_x, dtype=np.float32))
    progs = _progs()

    wT = np.ascontiguousarray(np.asarray(L0_attn_in_proj_w).T)
    convw_r = np.ascontiguousarray(
        np.asarray(L0_attn_conv_w)[:, 0, :].reshape(80, 128, 4).transpose(1, 0, 2).reshape(128, 320))
    convb_r = np.ascontiguousarray(np.asarray(L0_attn_conv_b).reshape(80, 128).T)
    dtb = np.ascontiguousarray(np.asarray(L0_attn_dt_bias).reshape(128, 1))
    alog = np.ascontiguousarray(np.asarray(L0_attn_A_log).reshape(128, 1))
    drow = np.ascontiguousarray(np.asarray(L0_attn_D).reshape(1, 128))
    normw_r = np.ascontiguousarray(np.asarray(L0_attn_norm_w).reshape(64, 128).T)
    opT = np.ascontiguousarray(np.asarray(L0_attn_out_proj_w).T)
    wgT = np.ascontiguousarray(np.transpose(np.asarray(L0_ffn_w_gate), (0, 2, 1)))
    wuT = np.ascontiguousarray(np.transpose(np.asarray(L0_ffn_w_up), (0, 2, 1)))
    wdT = np.ascontiguousarray(np.transpose(np.asarray(L0_ffn_w_down), (0, 2, 1)))
    Wq = np.ascontiguousarray(np.asarray(L1_attn_Wq))
    Wk = np.ascontiguousarray(np.asarray(L1_attn_Wk))
    Wv = np.ascontiguousarray(np.asarray(L1_attn_Wv))
    Wo = np.ascontiguousarray(np.asarray(L1_attn_Wo))
    UT = np.ascontiguousarray(np.asarray(L1_ffn_U).T)
    S_r = np.ascontiguousarray(np.asarray(L1_ffn_S).reshape(8, 128).T)
    Vw = np.ascontiguousarray(np.asarray(L1_ffn_V))
    ident = np.eye(128, dtype=np.float32)

    def shard(c):
        bb, hc = c // 2, c % 2
        return np.ascontiguousarray(x[bb, hc * T:(hc + 1) * T])

    def halo(c):
        bb, hc = c // 2, c % 2
        if hc == 0:
            return np.zeros((3, D), np.float32)
        return np.ascontiguousarray(x[bb, T - 3:T])

    if FUSED:
        in_maps = []
        for c in range(NCORES):
            hc = c % 2
            dd = dict(x=shard(c), xhalo=halo(c), wT=wT, convw=convw_r,
                      convb=convb_r, dtb=dtb, Alog=alog, Drow=drow,
                      normw=normw_r, opT=opT, wgT=wgT, wuT=wuT, wdT=wdT,
                      Wq=Wq, Wk=Wk, Wv=Wv, Wo=Wo, V=Vw, UT=UT, S_r=S_r)
            dd.update(l0mamba_host_inputs(hc))
            aa = l1attn_host_inputs(hc)
            del aa["zcol"], aa["ident"]
            dd.update(aa)
            in_maps.append(dd)
        res = run_bass_kernel_spmd(progs["fused"], in_maps,
                                   core_ids=list(range(NCORES)))
        x4 = [res.results[c]["out"] for c in range(NCORES)]
    else:
        in_maps = []
        for c in range(NCORES):
            hc = c % 2
            dd = dict(x=shard(c), xhalo=halo(c), wT=wT, convw=convw_r, convb=convb_r,
                      dtb=dtb, Alog=alog, Drow=drow, normw=normw_r, opT=opT)
            dd.update(l0mamba_host_inputs(hc))
            in_maps.append(dd)
        res = run_bass_kernel_spmd(progs["m"], in_maps, core_ids=list(range(NCORES)))
        x1 = [res.results[c]["out"] for c in range(NCORES)]

        in_maps = [dict(x=x1[c], wgT=wgT, wuT=wuT, wdT=wdT, ident=ident)
                   for c in range(NCORES)]
        res = run_bass_kernel_spmd(progs["f0"], in_maps, core_ids=list(range(NCORES)))
        x2 = [res.results[c]["out"] for c in range(NCORES)]

        in_maps = []
        for c in range(NCORES):
            hc = c % 2
            dd = dict(x=x2[c], Wq=Wq, Wk=Wk, Wv=Wv, Wo=Wo)
            dd.update(l1attn_host_inputs(hc))
            in_maps.append(dd)
        res = run_bass_kernel_spmd(progs["a"], in_maps, core_ids=list(range(NCORES)))
        x3 = [res.results[c]["out"] for c in range(NCORES)]

        in_maps = [dict(x=x3[c], V=Vw, UT=UT, S_r=S_r, ident=ident)
                   for c in range(NCORES)]
        res = run_bass_kernel_spmd(progs["f1"], in_maps, core_ids=list(range(NCORES)))
        x4 = [res.results[c]["out"] for c in range(NCORES)]

    out = np.zeros((4, 4096, D), np.float32)
    for c in range(NCORES):
        bb, hc = c // 2, c % 2
        out[bb, hc * T:(hc + 1) * T] = x4[c]
    return out
